# version 07
# baseline (speedup 1.0000x reference)
import math
from contextlib import ExitStack

import numpy as np
import ml_dtypes

import concourse.bass as bass
import concourse.mybir as mybir
from concourse.bass_utils import run_bass_kernel_spmd

F32 = mybir.dt.float32
BF16 = mybir.dt.bfloat16
AF = mybir.ActivationFunctionType
ALU = mybir.AluOpType
AX = mybir.AxisListType

NCORES = 8
STOP_AFTER = None


class Cfg:
    def __init__(self, D=4096, T=1024, H=32, E=8, R=16, GD=32, capfrac=3.0):
        self.D = D
        self.T = T
        self.H = H
        self.HD = 128
        self.E = E
        self.R = R
        self.RC = E * R
        self.GD = GD
        self.cap = float(int(capfrac * (NCORES * T) / E))
        self.NC = D // 128
        self.NT = T // 128
        self.NK = NCORES * T // 128
        self.NG = 4
        self.HPG = H // self.NG
        self.OG = max(8, D // 256)
        self.OGW = D // self.OG
        assert D % 128 == 0 and T % 512 == 0 and H % self.NG == 0 and H == D // 128
        assert self.OGW % 128 == 0 and (self.OGW // 128) * self.OG * 128 == D
        self.scale = 1.0 / math.sqrt(self.HD)


class Sem:
    def __init__(self, ctx, nc, name):
        self.h = ctx.enter_context(nc.semaphore(name))
        self.n = 0

    def inc(self, instr, k=1):
        instr.then_inc(self.h, k)
        self.n += k
        return self.n


_TRK_SKIP = {"nop", "engine_nop", "wait_ge", "wait_op", "drain", "sem_inc"}
_TRK_DRAIN = {"tensor_scalar", "tensor_single_scalar", "tensor_scalar_add",
              "tensor_scalar_sub", "tensor_scalar_mul", "tensor_scalar_max",
              "tensor_scalar_min", "scalar_tensor_tensor", "reciprocal"}


def _is_ap(x):
    return hasattr(x, "tensor") or hasattr(x, "ap")


class Trk:

    def __init__(self, e):
        object.__setattr__(self, "_e", e)
        object.__setattr__(self, "last", None)

    def __getattr__(self, n):
        f = getattr(self._e, n)
        if callable(f) and not n.startswith("_") and n not in _TRK_SKIP:
            is_dve = type(self._e).__name__ == "BassVectorEngine"
            def g(*a, **k):
                if is_dve and n != "dma_start" and self.last is not None:
                    self._e.drain()
                r = f(*a, **k)
                if hasattr(r, "then_inc"):
                    object.__setattr__(self, "last", r)
                return r
            return g
        return f


def build(cfg: Cfg, taps=(), detect_races=False, phases=3):
    c = cfg
    nc = bass.Bass("TRN2", target_bir_lowering=False, detect_race_conditions=detect_races)
    T, D, NC, NT, H, E, GD, RC = c.T, c.D, c.NC, c.NT, c.H, c.E, c.GD, c.RC
    HV = [(i * 512, (i + 1) * 512) for i in range(T // 512)]
    NKC = c.NK

    def xin(name, shape, dtype=F32):
        return nc.dram_tensor(name, list(shape), dtype, kind="ExternalInput")

    x_d = xin("x_fm", [D, T])
    wq_d = xin("wq_b", [NC, 128, D])
    wk_d = xin("wk_b", [NC, 128, D])
    wo_d = xin("wo_b", [NC, 128, D])
    wff_d = xin("wff_b", [NC, 128, D])
    wv_d = xin("wv_b", [c.OG, 128, NC * c.OGW])
    a_d = xin("a_b", [128, NC * RC])
    bcat_d = xin("b_cat", [128, D])
    gw1_d = xin("gw1_b", [128, NC * GD])
    gw2_d = xin("gw2_t", [GD, E])
    bq_d = xin("bq_c", [128, NC])
    bk_d = xin("bk_c", [128, NC])
    bo_d = xin("bo_c", [128, NC])
    bv_d = xin("bv_r", [1, D], BF16)
    gb1_d = xin("gb1_c", [GD, 1])
    gb2_d = xin("gb2_r", [1, E])
    lng_d = xin("lng_c", [GD, 1])
    lnb_d = xin("lnb_c", [GD, 1])
    gq_d = xin("gq_c", [128, 1])
    gk_d = xin("gk_c", [128, 1])
    map_d = xin("map_alpha", [E, 128])
    idf_d = xin("ident_f32", [128, 128])

    out_d = nc.dram_tensor("out_fm", [D, T], F32, kind="ExternalOutput")
    tap_d = {}
    for tname, tshape, tdt in taps:
        tap_d[tname] = nc.dram_tensor(tname, list(tshape), tdt, kind="ExternalOutput")

    res1_d = nc.dram_tensor("res1", [D, T], F32, kind="Internal")
    res2_d = nc.dram_tensor("res2", [D, T], F32, kind="Internal")
    ao_d = nc.dram_tensor("ao", [H * 128, T], BF16, kind="Internal")
    q_d = nc.dram_tensor("q_sp", [H * 128, T], BF16, kind="Internal")
    kg_own = [nc.dram_tensor(f"kg_own{g}", [c.HPG * 128, T], BF16, kind="Internal") for g in range(c.NG)]
    vg_own = [nc.dram_tensor(f"vg_own{g}", [c.HPG * NT * 128, 128], BF16, kind="Internal") for g in range(c.NG)]
    kg_gth = [nc.dram_tensor(f"kg_gth{g}", [NCORES * c.HPG * 128, T], BF16, kind="Internal") for g in range(c.NG)]
    vg_gth = [nc.dram_tensor(f"vg_gth{g}", [NCORES * c.HPG * NT * 128, 128], BF16, kind="Internal") for g in range(c.NG)]
    cnt_in = [nc.dram_tensor(f"cnt_in{i}", [1, 2 * E], F32, kind="Internal") for i in range(2)]
    cnt_out = [nc.dram_tensor(f"cnt_out{i}", [1, 2 * E], F32, kind="Internal") for i in range(2)]

    ctx = ExitStack()
    sb = lambda name, shape, dt: ctx.enter_context(nc.sbuf_tensor(name, list(shape), dt))

    acc_s = sb("acc_s", [128, T], F32)
    acc_q = sb("acc_q", [128, T], F32)
    tbuf = [sb(f"tchunk{i}", [128, T], F32) for i in range(2)]
    xinb = [sb(f"xin{i}", [128, T], F32) for i in range(2)]
    rout = [sb(f"rout{i}", [128, T], F32) for i in range(2)]
    sqs = sb("sqs", [128, T], F32)
    a_sb = sb("a_sb", [128, NC * RC], BF16)
    bcat_sb = sb("bcat_sb", [128, D], BF16)
    gw1_sb = sb("gw1_sb", [128, NC * GD], F32)
    gw2_sb = sb("gw2_sb", [GD, E], F32)
    bq_sb = sb("bq_sb", [128, NC], F32)
    bk_sb = sb("bk_sb", [128, NC], F32)
    bo_sb = sb("bo_sb", [128, NC], F32)
    bv_sb = sb("bv_sb", [1, D], BF16)
    gb1_sb = sb("gb1_sb", [GD, 1], F32)
    gb2row = sb("gb2row", [1, E], F32)
    gb2b_sb = sb("gb2b_sb", [128, E], F32)
    lng_sb = sb("lng_sb", [GD, 1], F32)
    lnb_sb = sb("lnb_sb", [GD, 1], F32)
    gq_sb = sb("gq_sb", [128, 1], F32)
    gk_sb = sb("gk_sb", [128, 1], F32)
    map_sb = sb("map_sb", [E, 128], F32)
    idf_sb = sb("idf_sb", [128, 128], F32)
    eps5_sb = sb("eps5_sb", [1, 1], F32)
    eps6_sb = sb("eps6_sb", [1, 1], F32)
    ones_cb = sb("ones_cb", [128, 1], BF16)
    ones_rb = sb("ones_rb", [1, 128], BF16)
    ones_cf = sb("ones_cf", [128, 1], F32)
    ones_rf = sb("ones_rf", [1, 128], F32)
    row_s = sb("row_s", [1, T], F32)
    row_q = sb("row_q", [1, T], F32)
    row_m = sb("row_m", [1, T], F32)
    row_r = sb("row_r", [1, T], F32)
    g_sb = sb("g_sb", [128, E], F32)
    gm_sb = sb("gm_sb", [128, E], F32)
    eq_sb = sb("eq_sb", [128, 2 * E * NT], F32)
    m1_sb = sb("m1_sb", [128, 1], F32)
    m2_sb = sb("m2_sb", [128, 1], F32)
    e21_sb = sb("e21_sb", [128, 1], F32)
    zden_sb = sb("zden_sb", [128, 1], F32)
    g12_sb = sb("g12_sb", [128, 2 * NT], F32)
    wz_sb = sb("wz_sb", [128, 2 * E], F32)
    w_sb = sb("w_sb", [128, E], F32)
    cnt_sb = sb("cnt_sb", [1, 2 * E], F32)
    alw_row = sb("alw_row", [1, 2 * E], F32)
    alw_sb = sb("alw_sb", [128, 2 * E], F32)
    pm_sb = sb("pm_sb", [128, 1], F32)
    mrow_sb = sb("mrow_sb", [1, 1], F32)
    brow_sb = sb("brow_sb", [1, 1], F32)
    b2row_sb = sb("b2row_sb", [1, 1], F32)
    bias_sb = sb("bias_sb", [128, 1], F32)
    qraw = [sb(f"qraw{i}", [128, T], F32) for i in range(2)]
    q2_sb = sb("q2_sb", [128, T], BF16)
    kq_out = [sb(f"kq_out{i}", [128, T], BF16) for i in range(2)]
    vout_sb = [sb(f"vout{i}", [128, c.OGW], BF16) for i in range(2)]

    mks = lambda name: Sem(ctx, nc, name)
    dx = [mks("dx0"), mks("dx1")]
    dy = mks("dy")
    gw = [mks("gw0"), mks("gw1")]
    gz = mks("gz")
    cc = mks("cc")
    tm = mks("tm")
    vm = mks("vm")
    am = mks("am")
    gm = mks("gm")

    block = ctx.enter_context(nc.Block())

    @block.sync
    def _(e):
        e = Trk(e)
        for t, d in [(bq_sb, bq_d), (bk_sb, bk_d), (bo_sb, bo_d),
                     (gb1_sb, gb1_d), (lng_sb, lng_d), (lnb_sb, lnb_d),
                     (gq_sb, gq_d), (gk_sb, gk_d), (map_sb, map_d),
                     (idf_sb, idf_d), (gw2_sb, gw2_d), (gw1_sb, gw1_d),
                     (gb2row, gb2_d)]:
            dy.n = dy.inc(e.dma_start(t[:], d.ap()), 16)

    @block.gpsimd
    def _(e):
        e = Trk(e)
        gz.inc(e.dma_start(a_sb[:], a_d.ap()), 16)
        gz.inc(e.dma_start(bcat_sb[:], bcat_d.ap()), 16)
        gz.inc(e.dma_start(bv_sb[:], bv_d.ap()), 16)
        gm.inc(e.memset(eps5_sb[:], 1e-5))
        gm.inc(e.memset(eps6_sb[:], 1e-6))
        gm.inc(e.memset(ones_cb[:], 1.0))
        gm.inc(e.memset(ones_rb[:], 1.0))
        gm.inc(e.memset(ones_cf[:], 1.0))
        gm.inc(e.memset(ones_rf[:], 1.0))
    dy_setup, gz_setup, gm_setup = dy.n, gz.n, gm.n

    @block.scalar
    def _(e):
        e = Trk(e)
        e.wait_ge(gm.h, gm_setup)

    with nc.psum_tensor("ps_gb2", [128, E], F32) as ps_gb2:
        @block.tensor
        def _(e):
            e = Trk(e)
            e.wait_ge(dy.h, dy_setup)
            e.wait_ge(gm.h, gm_setup)
            tm.inc(e.matmul(ps_gb2[:, :], ones_rf[0:1, :], gb2row[0:1, :],
                            start=True, stop=True))
        tm_gb2 = tm.n

        @block.vector
        def _(e):
            e = Trk(e)
            e.wait_ge(tm.h, tm_gb2)
            vm.inc(e.tensor_copy(gb2b_sb[:], ps_gb2[:]))

    if STOP_AFTER == "setup":
        @block.sync
        def _(e):
            e = Trk(e)
            e.wait_ge(dy.h, dy.n)
        ctx.close()
        return nc
    vm_st = {}
    dx_st = {}
    for ic in range(NC):
        par = ic % 2
        xi, dxs = xinb[par], dx[par]

        @block.sync
        def _(e, ic=ic, xi=xi, dxs=dxs):
            e = Trk(e)
            if ic >= 2:
                e.wait_ge(vm.h, vm_st[ic - 2])
            dx_st[ic] = dxs.inc(e.dma_start(xi[:], x_d.ap()[ic * 128:(ic + 1) * 128, :]), 16)

        @block.vector
        def _(e, ic=ic, xi=xi, dxs=dxs):
            e = Trk(e)
            e.wait_ge(dxs.h, dx_st[ic])
            if ic == 0:
                e.tensor_copy(acc_s[:], xi[:])
                e.tensor_mul(acc_q[:], xi[:], xi[:])
            else:
                e.tensor_add(acc_s[:], acc_s[:], xi[:])
                e.tensor_mul(sqs[:], xi[:], xi[:])
                e.tensor_add(acc_q[:], acc_q[:], sqs[:])
            vm_st[ic] = vm.inc(e.last)

    def ln_rows(ps_sum, eps_ap, nfeat):
        @block.tensor
        def _(e):
            e = Trk(e)
            e.wait_ge(vm.h, vm.n)
            for (lo, hi) in HV:
                e.matmul(ps_sum[0:1, lo:hi], ones_cf[:], acc_s[:, lo:hi], start=True, stop=True)
                e.matmul(ps_sum[0:1, T + lo:T + hi], ones_cf[:], acc_q[:, lo:hi], start=True, stop=True)
            tm.inc(e.last)
        t0 = tm.n
        if STOP_AFTER == "rowsA":
            return

        @block.vector
        def _(e):
            e = Trk(e)
            e.wait_ge(tm.h, t0)
            e.tensor_scalar_mul(row_m[:], ps_sum[0:1, 0:T], 1.0 / nfeat)
            e.tensor_scalar_mul(row_q[:], ps_sum[0:1, T:2 * T], 1.0 / nfeat)
            e.tensor_mul(row_s[:], row_m[:], row_m[:])
            e.tensor_sub(row_q[:], row_q[:], row_s[:])
            vm.inc(e.last)
        v0 = vm.n
        if STOP_AFTER == "rowsB":
            return

        @block.scalar
        def _(e):
            e = Trk(e)
            e.wait_ge(vm.h, v0)
            e.activation(row_s[:], row_q[:], AF.Ln, bias=eps_ap[0:1, 0:1])
            e.activation(row_r[:], row_s[:], AF.Exp, scale=-0.5)
            am.inc(e.last)

    def bcast_rows(ps_meanB, ps_rstdB):
        @block.tensor
        def _(e):
            e = Trk(e)
            e.wait_ge(am.h, am.n)
            for (lo, hi) in HV:
                e.matmul(ps_meanB[:, lo:hi], ones_rf[0:1, :], row_m[0:1, lo:hi], start=True, stop=True)
                e.matmul(ps_rstdB[:, lo:hi], ones_rf[0:1, :], row_r[0:1, lo:hi], start=True, stop=True)
            tm.inc(e.last)

    def ffn_phase(src_d, dst_d, icnt):
        vm_phase0 = vm.n
        tm_phase0 = tm.n
        fstk = ExitStack()
        xf = fstk.enter_context(nc.sbuf_tensor(f"xf_f{icnt}", [128, NC * T], BF16))
        h_sb = fstk.enter_context(nc.sbuf_tensor(f"h_sb_{icnt}", [GD, T], F32))
        h2_sb = fstk.enter_context(nc.sbuf_tensor(f"h2_sb_{icnt}", [GD, T], F32))
        hsq_sb = fstk.enter_context(nc.sbuf_tensor(f"hsq_sb_{icnt}", [GD, T], F32))
        wt_sb = fstk.enter_context(nc.sbuf_tensor(f"wt_sb_{icnt}", [E, T], F32))
        wtexp_sb = fstk.enter_context(nc.sbuf_tensor(f"wtexp_sb_{icnt}", [128, T], F32))
        tmps_sb = fstk.enter_context(nc.sbuf_tensor(f"tmps_sb_{icnt}", [128, T], BF16))
        if STOP_AFTER == "stats":
            fstk.close()
            return
        with nc.psum_tensor(f"ffn_sum_{icnt}", [1, 2 * T], F32) as ps_sum:
            ln_rows(ps_sum, eps5_sb, D)
        if STOP_AFTER == "rowsdbg":
            @block.sync
            def _(e):
                e = Trk(e)
                e.wait_ge(am.h, am.n)
                dy.inc(e.dma_start(out_d.ap()[0:1, :], row_m[0:1, :]), 16)
                dy.inc(e.dma_start(out_d.ap()[1:2, :], row_r[0:1, :]), 16)
                dy.inc(e.dma_start(out_d.ap()[2:3, :], row_q[0:1, :]), 16)
                dy.inc(e.dma_start(out_d.ap()[4:132, :], acc_s[:, :]), 16)
                dy.inc(e.dma_start(out_d.ap()[132:260, :], acc_q[:, :]), 16)
                e.wait_ge(dy.h, dy.n)
        if STOP_AFTER in ("rows", "rowsA", "rowsB", "rowsdbg"):
            fstk.close()
            return

        with nc.psum_tensor(f"ffn_tmp_{icnt}", [128, T], F32) as ps_tmp:
            with (nc.psum_tensor(f"ffn_meanB_{icnt}", [128, T], F32) as ps_meanB,
                  nc.psum_tensor(f"ffn_rstdB_{icnt}", [128, T], F32) as ps_rstdB,
                  nc.psum_tensor(f"ffn_h_{icnt}", [GD, T], F32) as ps_h):
                bcast_rows(ps_meanB, ps_rstdB)
                t_bc = tm.n

                vm_ch = {}
                tm_ch = {}
                dx_ch = {}
                for ic in range(NC):
                    par = ic % 2
                    xi, dxs, tb = xinb[par], dx[par], tbuf[par]

                    @block.sync
                    def _(e, ic=ic, xi=xi, dxs=dxs):
                        e = Trk(e)
                        if ic >= 2:
                            e.wait_ge(vm.h, vm_ch[ic - 2])
                        else:
                            e.wait_ge(vm.h, vm_phase0)
                        dx_ch[ic] = dxs.inc(
                            e.dma_start(xi[:], src_d.ap()[ic * 128:(ic + 1) * 128, :]), 16)

                    @block.vector
                    def _(e, ic=ic, xi=xi, dxs=dxs, tb=tb):
                        e = Trk(e)
                        e.wait_ge(dxs.h, dx_ch[ic])
                        if ic == 0:
                            e.wait_ge(tm.h, t_bc)
                        if ic >= 2:
                            e.wait_ge(tm.h, tm_ch[ic - 2])
                        e.tensor_sub(tb[:], xi[:], ps_meanB[:])
                        e.tensor_mul(xf[:, ic * T:(ic + 1) * T], tb[:], ps_rstdB[:])
                        vm_ch[ic] = vm.inc(e.last)

                    @block.tensor
                    def _(e, ic=ic, tb=tb):
                        e = Trk(e)
                        e.wait_ge(vm.h, vm_ch[ic])
                        if ic == 0:
                            e.wait_ge(gz.h, gz_setup)
                        for (lo, hi) in HV:
                            e.matmul(ps_h[:, lo:hi], gw1_sb[:, ic * GD:(ic + 1) * GD],
                                     tb[:, lo:hi], start=(ic == 0), stop=(ic == NC - 1))
                        for (lo, hi) in HV:
                            e.matmul(ps_tmp[:, lo:hi], a_sb[:, ic * RC:(ic + 1) * RC],
                                     xf[:, ic * T + lo: ic * T + hi],
                                     start=(ic == 0), stop=(ic == NC - 1))
                        tm_ch[ic] = tm.inc(e.last)
                t_hmm = tm.n

                @block.scalar
                def _(e):
                    e = Trk(e)
                    e.wait_ge(tm.h, t_hmm)
                    am.inc(e.activation(h_sb[:], ps_h[:], AF.Copy))
                a_hcp = am.n

                @block.vector
                def _(e):
                    e = Trk(e)
                    e.wait_ge(am.h, a_hcp)
                    e.tensor_mul(h2_sb[:], h_sb[:], ps_rstdB[0:GD, :])
                    e.tensor_scalar_add(h2_sb[:], h2_sb[:], gb1_sb[:, 0:1])
                    vm.inc(e.last)
            if STOP_AFTER == "norm":
                @block.gpsimd
                def _(e):
                    e = Trk(e)
                    e.wait_ge(vm.h, vm.n)
                    gz.inc(e.dma_start(out_d.ap()[0:128, :], xf[:, 0:T]), 16)
                    gz.inc(e.dma_start(out_d.ap()[128:256, :], xf[:, (NC - 1) * T:NC * T]), 16)
                @block.sync
                def _(e):
                    e = Trk(e)
                    e.wait_ge(vm.h, vm.n)
                    dy.inc(e.dma_start(out_d.ap()[256:256 + GD, :], h2_sb[:, :]), 16)
                    e.wait_ge(dy.h, dy.n)
                    e.wait_ge(gz.h, gz.n)
                fstk.close()
                return

            with nc.psum_tensor(f"g_sum_{icnt}", [1, 2 * T], F32) as ps_gsum:
                @block.vector
                def _(e):
                    e = Trk(e)
                    e.tensor_mul(hsq_sb[:], h2_sb[:], h2_sb[:])
                    vm.inc(e.last)
                v_hsq = vm.n

                @block.tensor
                def _(e):
                    e = Trk(e)
                    e.wait_ge(vm.h, v_hsq)
                    for (lo, hi) in HV:
                        e.matmul(ps_gsum[0:1, lo:hi], ones_cf[0:GD, :], h2_sb[:, lo:hi], start=True, stop=True)
                        e.matmul(ps_gsum[0:1, T + lo:T + hi], ones_cf[0:GD, :], hsq_sb[:, lo:hi], start=True, stop=True)
                    tm.inc(e.last)
                t_gs = tm.n

                @block.vector
                def _(e):
                    e = Trk(e)
                    e.wait_ge(tm.h, t_gs)
                    e.tensor_scalar_mul(row_m[:], ps_gsum[0:1, 0:T], 1.0 / GD)
                    e.tensor_scalar_mul(row_q[:], ps_gsum[0:1, T:2 * T], 1.0 / GD)
                    e.tensor_mul(row_s[:], row_m[:], row_m[:])
                    e.tensor_sub(row_q[:], row_q[:], row_s[:])
                    vm.inc(e.last)
                v_gv = vm.n

                @block.scalar
                def _(e):
                    e = Trk(e)
                    e.wait_ge(vm.h, v_gv)
                    e.activation(row_s[:], row_q[:], AF.Ln, bias=eps5_sb[0:1, 0:1])
                    e.activation(row_r[:], row_s[:], AF.Exp, scale=-0.5)
                    am.inc(e.last)

            with (nc.psum_tensor(f"g_meanB_{icnt}", [GD, T], F32) as ps_gmB,
                  nc.psum_tensor(f"g_rstdB_{icnt}", [GD, T], F32) as ps_grB):
                @block.tensor
                def _(e):
                    e = Trk(e)
                    e.wait_ge(am.h, am.n)
                    for (lo, hi) in HV:
                        e.matmul(ps_gmB[:, lo:hi], ones_rf[0:1, 0:GD], row_m[0:1, lo:hi], start=True, stop=True)
                        e.matmul(ps_grB[:, lo:hi], ones_rf[0:1, 0:GD], row_r[0:1, lo:hi], start=True, stop=True)
                    tm.inc(e.last)
                t_gb = tm.n

                @block.vector
                def _(e):
                    e = Trk(e)
                    e.wait_ge(tm.h, t_gb)
                    e.tensor_sub(h_sb[:], h2_sb[:], ps_gmB[:])
                    e.tensor_mul(h_sb[:], h_sb[:], ps_grB[:])
                    e.tensor_scalar(h_sb[:], h_sb[:], lng_sb[:, 0:1], lnb_sb[:, 0:1], ALU.mult, ALU.add)
                    e.tensor_relu(h_sb[:], h_sb[:])
                    vm.inc(e.last)
                v_hn = vm.n

            if STOP_AFTER == "gateln":
                @block.sync
                def _(e):
                    e = Trk(e)
                    e.wait_ge(vm.h, vm.n)
                    dy.inc(e.dma_start(out_d.ap()[0:GD, :], h_sb[:, :]), 16)
                    e.wait_ge(dy.h, dy.n)
                fstk.close()
                return
            with (nc.psum_tensor(f"g_ps_{icnt}", [128, E * NT], F32) as ps_g,
                  nc.psum_tensor(f"wt_ps_{icnt}", [E, 128], F32) as ps_wt,
                  nc.psum_tensor(f"cnt_ps_{icnt}", [1, 2 * E], F32) as ps_cnt,
                  nc.psum_tensor(f"alw_ps_{icnt}", [128, 2 * E], F32) as ps_alw):
                @block.tensor
                def _(e):
                    e = Trk(e)
                    e.wait_ge(vm.h, v_hn)
                    for t in range(NT):
                        e.matmul(ps_g[:, t * E:(t + 1) * E], h_sb[:, t * 128:(t + 1) * 128],
                                 gw2_sb[:, :], start=True, stop=True)
                    tm.inc(e.last)
                t_g = tm.n

                vm_g = {}
                for t in range(NT):
                    @block.vector
                    def _(e, t=t):
                        e = Trk(e)
                        e.wait_ge(tm.h, t_g)
                        eqv = eq_sb[:, t * 2 * E:(t + 1) * 2 * E]
                        e.tensor_add(g_sb[:], ps_g[:, t * E:(t + 1) * E], gb2b_sb[:])
                        e.reduce_max(m1_sb[:], g_sb[:], axis=AX.X)
                        e.tensor_single_scalar(eqv[:, 0:E], g_sb[:], m1_sb[:, 0:1], ALU.is_equal)
                        e.scalar_tensor_tensor(gm_sb[:], eqv[:, 0:E], -1e30, g_sb[:], ALU.mult, ALU.add)
                        e.reduce_max(m2_sb[:], gm_sb[:], axis=AX.X)
                        e.tensor_single_scalar(eqv[:, E:2 * E], gm_sb[:], m2_sb[:, 0:1], ALU.is_equal)
                        e.tensor_sub(m2_sb[:], m2_sb[:], m1_sb[:])
                        vm_g[t] = vm.inc(e.last)

                    @block.scalar
                    def _(e, t=t):
                        e = Trk(e)
                        e.wait_ge(vm.h, vm_g[t])
                        am.inc(e.activation(e21_sb[:], m2_sb[:], AF.Exp))
                    a_e = am.n

                    @block.vector
                    def _(e, t=t):
                        e = Trk(e)
                        e.wait_ge(am.h, a_e)
                        e.tensor_scalar_add(zden_sb[:], e21_sb[:], 1.0)
                        e.reciprocal(g12_sb[:, 2 * t:2 * t + 1], zden_sb[:])
                        e.tensor_mul(g12_sb[:, 2 * t + 1:2 * t + 2], e21_sb[:],
                                     g12_sb[:, 2 * t:2 * t + 1])
                        vm.inc(e.last)
                    v_g12 = vm.n

                    @block.tensor
                    def _(e, t=t):
                        e = Trk(e)
                        e.wait_ge(vm.h, v_g12)
                        e.matmul(ps_cnt[0:1, :], ones_cf[:, :],
                                 eq_sb[:, t * 2 * E:(t + 1) * 2 * E],
                                 start=(t == 0), stop=(t == NT - 1))
                        tm.inc(e.last)
                t_cnt = tm.n

                if STOP_AFTER == "gates":
                    @block.sync
                    def _(e):
                        e = Trk(e)
                        e.wait_ge(tm.h, tm.n)
                        e.wait_ge(vm.h, vm.n)
                        dy.inc(e.dma_start(out_d.ap()[0:128, 0:2 * E * NT], eq_sb[:, :]), 16)
                        dy.inc(e.dma_start(out_d.ap()[128:256, 0:2 * NT], g12_sb[:, :]), 16)
                        dy.inc(e.dma_start(out_d.ap()[256:384, 0:E], g_sb[:, :]), 16)
                        dy.inc(e.dma_start(out_d.ap()[384:512, 0:1], m1_sb[:, :], allow_slow_non_contiguous=True), 16)
                        dy.inc(e.dma_start(out_d.ap()[512:640, 0:1], m2_sb[:, :], allow_slow_non_contiguous=True), 16)
                        e.wait_ge(dy.h, dy.n)
                    fstk.close()
                    return
                @block.vector
                def _(e):
                    e = Trk(e)
                    e.wait_ge(tm.h, t_cnt)
                    vm.inc(e.tensor_copy(cnt_sb[:], ps_cnt[:]))
                v_cnt = vm.n

                @block.sync
                def _(e):
                    e = Trk(e)
                    e.wait_ge(vm.h, v_cnt)
                    dy.inc(e.dma_start(cnt_in[icnt].ap(), cnt_sb[:]), 16)
                d_ci = dy.n

                @block.gpsimd
                def _(e):
                    e = Trk(e)
                    e.wait_ge(dy.h, d_ci)
                    cc.inc(e.collective_compute(
                        "AllReduce", ALU.add, replica_groups=[list(range(NCORES))],
                        ins=[cnt_in[icnt].ap().opt()], outs=[cnt_out[icnt].ap().opt()]), 1)
                c_ar = cc.n

                @block.sync
                def _(e):
                    e = Trk(e)
                    e.wait_ge(cc.h, c_ar)
                    dy.inc(e.dma_start(cnt_sb[:], cnt_out[icnt].ap()), 16)
                d_co = dy.n

                @block.vector
                def _(e):
                    e = Trk(e)
                    e.wait_ge(dy.h, d_co)
                    e.tensor_single_scalar(alw_row[:], cnt_sb[:], c.cap, ALU.is_le)
                    vm.inc(e.last)
                v_alw = vm.n

                @block.tensor
                def _(e):
                    e = Trk(e)
                    e.wait_ge(vm.h, v_alw)
                    tm.inc(e.matmul(ps_alw[:, :], ones_rf[0:1, :], alw_row[0:1, :], start=True, stop=True))
                t_alw = tm.n

                @block.vector
                def _(e):
                    e = Trk(e)
                    e.wait_ge(tm.h, t_alw)
                    vm.inc(e.tensor_copy(alw_sb[:], ps_alw[:]))
                v_alwc = vm.n

                tm_wt = {}
                for t in range(NT):
                    @block.vector
                    def _(e, t=t):
                        e = Trk(e)
                        e.wait_ge(vm.h, v_alwc)
                        if t >= 1:
                            e.wait_ge(tm.h, tm_wt[t - 1])
                        eqv = eq_sb[:, t * 2 * E:(t + 1) * 2 * E]
                        e.tensor_scalar_mul(wz_sb[:, 0:E], eqv[:, 0:E], g12_sb[:, 2 * t:2 * t + 1])
                        e.tensor_scalar_mul(wz_sb[:, E:2 * E], eqv[:, E:2 * E], g12_sb[:, 2 * t + 1:2 * t + 2])
                        e.tensor_mul(wz_sb[:], wz_sb[:], alw_sb[:])
                        e.tensor_add(w_sb[:], wz_sb[:, 0:E], wz_sb[:, E:2 * E])
                        vm.inc(e.last)
                    v_w = vm.n

                    @block.tensor
                    def _(e, t=t):
                        e = Trk(e)
                        e.wait_ge(vm.h, v_w)
                        tm_wt[t] = tm.inc(e.transpose(ps_wt[:, :], w_sb[:, :], idf_sb[:, :]))

                    @block.vector
                    def _(e, t=t):
                        e = Trk(e)
                        e.wait_ge(tm.h, tm_wt[t])
                        vm.inc(e.tensor_copy(wt_sb[:, t * 128:(t + 1) * 128], ps_wt[:, :]))
                v_wt = vm.n

            if STOP_AFTER == "w":
                @block.sync
                def _(e):
                    e = Trk(e)
                    e.wait_ge(vm.h, vm.n)
                    dy.inc(e.dma_start(out_d.ap()[0:E, :], wt_sb[:, :]), 16)
                    dy.inc(e.dma_start(out_d.ap()[8:9, 0:2 * E], cnt_sb[:, :]), 16)
                    dy.inc(e.dma_start(out_d.ap()[9:10, 0:2 * E], alw_row[:, :]), 16)
                    dy.inc(e.dma_start(out_d.ap()[10:138, 0:E], w_sb[:, :]), 16)
                    dy.inc(e.dma_start(out_d.ap()[138:266, 0:2 * NT], g12_sb[:, :]), 16)
                    dy.inc(e.dma_start(out_d.ap()[266:394, 0:2 * E * NT], eq_sb[:, :]), 16)
                    dy.inc(e.dma_start(out_d.ap()[394:522, 0:2 * E], alw_sb[:, :]), 16)
                    e.wait_ge(dy.h, dy.n)
                fstk.close()
                return
            with nc.psum_tensor(f"wtexp_ps_{icnt}", [128, T], F32) as ps_we:
                @block.tensor
                def _(e):
                    e = Trk(e)
                    e.wait_ge(vm.h, v_wt)
                    for (lo, hi) in HV:
                        e.matmul(ps_we[:, lo:hi], map_sb[:, :], wt_sb[:, lo:hi], start=True, stop=True)
                    tm.inc(e.last)
                t_we = tm.n

                @block.vector
                def _(e):
                    e = Trk(e)
                    e.wait_ge(tm.h, t_we)
                    e.tensor_copy(wtexp_sb[:], ps_we[:])
                    e.tensor_mul(tmps_sb[:], ps_tmp[:], wtexp_sb[:])
                    vm.inc(e.last)
        v_tmps = vm.n

        with (nc.psum_tensor(f"main0_{icnt}", [128, T], F32) as pm0,
              nc.psum_tensor(f"main1_{icnt}", [128, T], F32) as pm1,
              nc.sbuf_tensor(f"wslf0_{icnt}", [128, NC * 128], BF16) as wsl0,
              nc.sbuf_tensor(f"wslf1_{icnt}", [128, NC * 128], BF16) as wsl1):
            wslab = [wsl0, wsl1]
            ps_main = [pm0, pm1]
            gw_oc = {}
            tm_oc = {}
            vm_oc = {}
            dxm_oc = {}
            dyr_oc = {}
            for oc in range(NC):
                par = oc % 2
                gws = gw[par]

                @block.gpsimd
                def _(e, oc=oc, par=par, gws=gws):
                    e = Trk(e)
                    if oc >= 2:
                        e.wait_ge(tm.h, tm_oc[oc - 2])
                    else:
                        e.wait_ge(tm.h, tm_phase0)
                    gw_oc[oc] = gws.inc(
                        e.dma_start(wslab[par][:], wff_d.ap()[oc, :, :]), 16)

                @block.sync
                def _(e, oc=oc, par=par):
                    e = Trk(e)
                    if oc >= 2:
                        e.wait_ge(vm.h, vm_oc[oc - 2])
                    else:
                        e.wait_ge(vm.h, vm_ch[NC - 1])
                    dxm_oc[oc] = dx[par].inc(
                        e.dma_start(xinb[par][:], src_d.ap()[oc * 128:(oc + 1) * 128, :]), 16)

                @block.tensor
                def _(e, oc=oc, par=par, gws=gws):
                    e = Trk(e)
                    e.wait_ge(gws.h, gw_oc[oc])
                    if oc == 0:
                        e.wait_ge(vm.h, v_tmps)
                    if oc >= 2:
                        e.wait_ge(vm.h, vm_oc[oc - 2])
                    for (lo, hi) in HV:
                        for ic in range(NC):
                            e.matmul(ps_main[par][:, lo:hi],
                                     wslab[par][:, ic * 128:(ic + 1) * 128],
                                     xf[:, ic * T + lo: ic * T + hi],
                                     start=(ic == 0), stop=False)
                        e.matmul(ps_main[par][:, lo:hi],
                                 bcat_sb[:, oc * 128:(oc + 1) * 128],
                                 tmps_sb[:, lo:hi], start=False, stop=True)
                    tm_oc[oc] = tm.inc(e.last)

                @block.vector
                def _(e, oc=oc, par=par):
                    e = Trk(e)
                    e.wait_ge(tm.h, tm_oc[oc])
                    e.wait_ge(dx[par].h, dxm_oc[oc])
                    if oc >= 2:
                        e.wait_ge(dy.h, dyr_oc[oc - 2])
                    e.scalar_tensor_tensor(rout[par][:], ps_main[par][:], 0.5,
                                           xinb[par][:], ALU.mult, ALU.add)
                    if dst_d is not None:
                        if oc == 0:
                            e.tensor_copy(acc_s[:], rout[par][:])
                            e.tensor_mul(acc_q[:], rout[par][:], rout[par][:])
                        else:
                            e.tensor_add(acc_s[:], acc_s[:], rout[par][:])
                            e.tensor_mul(sqs[:], rout[par][:], rout[par][:])
                            e.tensor_add(acc_q[:], acc_q[:], sqs[:])
                    vm_oc[oc] = vm.inc(e.last)

                @block.sync
                def _(e, oc=oc, par=par):
                    e = Trk(e)
                    e.wait_ge(vm.h, vm_oc[oc])
                    tgt = dst_d if dst_d is not None else out_d
                    dyr_oc[oc] = dy.inc(e.dma_start(tgt.ap()[oc * 128:(oc + 1) * 128, :], rout[par][:]), 16)
        fstk.close()

    def qkv_attn_phase():
        vm_phase0 = vm.n
        qstk = ExitStack()
        xf = qstk.enter_context(nc.sbuf_tensor("xf_qkv", [128, NC * T], BF16))
        with nc.psum_tensor("ln2_sum", [1, 2 * T], F32) as ps_sum:
            ln_rows(ps_sum, eps5_sb, D)
        with (nc.psum_tensor("ln2_meanB", [128, T], F32) as ps_meanB,
              nc.psum_tensor("ln2_rstdB", [128, T], F32) as ps_rstdB):
            bcast_rows(ps_meanB, ps_rstdB)
            t_bc = tm.n
            vm_ch = {}
            dx_ch = {}
            for ic in range(NC):
                par = ic % 2
                xi, dxs, tb = xinb[par], dx[par], tbuf[par]

                @block.sync
                def _(e, ic=ic, xi=xi, dxs=dxs):
                    e = Trk(e)
                    if ic >= 2:
                        e.wait_ge(vm.h, vm_ch[ic - 2])
                    else:
                        e.wait_ge(vm.h, vm_phase0)
                    dx_ch[ic] = dxs.inc(
                        e.dma_start(xi[:], res1_d.ap()[ic * 128:(ic + 1) * 128, :]), 16)

                @block.vector
                def _(e, ic=ic, xi=xi, dxs=dxs, tb=tb):
                    e = Trk(e)
                    e.wait_ge(dxs.h, dx_ch[ic])
                    if ic == 0:
                        e.wait_ge(tm.h, t_bc)
                    e.tensor_sub(tb[:], xi[:], ps_meanB[:])
                    e.tensor_mul(xf[:, ic * T:(ic + 1) * T], tb[:], ps_rstdB[:])
                    vm_ch[ic] = vm.inc(e.last)
        v_xf2 = vm.n

        def kq_loop(w_dram, bias_sbuf, gain_sbuf, dst, eps_ap, tag):
            tm_kq0 = tm.n
            with (nc.psum_tensor(f"kq0_{tag}", [128, T], F32) as p0,
                  nc.psum_tensor(f"kq1_{tag}", [128, T], F32) as p1,
                  nc.psum_tensor(f"kq_ss_{tag}", [1, T], F32) as ps_ss,
                  nc.psum_tensor(f"kq_rb_{tag}", [128, T], F32) as ps_rb,
                  nc.sbuf_tensor(f"wslk0_{tag}", [128, NC * 128], BF16) as wsl0,
                  nc.sbuf_tensor(f"wslk1_{tag}", [128, NC * 128], BF16) as wsl1):
                wslab = [wsl0, wsl1]
                ps_qk = [p0, p1]
                gw_oc, tm_oc, am_oc, vm_oc = {}, {}, {}, {}
                tm_rb, vm_fin, dy_fin = {}, {}, {}
                for oc in range(H):
                    par = oc % 2
                    gws = gw[par]

                    @block.gpsimd
                    def _(e, oc=oc, par=par, gws=gws):
                        e = Trk(e)
                        if oc >= 2:
                            e.wait_ge(tm.h, tm_oc[oc - 2])
                        else:
                            e.wait_ge(tm.h, tm_kq0)
                        gw_oc[oc] = gws.inc(e.dma_start(wslab[par][:], w_dram.ap()[oc, :, :]), 16)

                    @block.tensor
                    def _(e, oc=oc, par=par, gws=gws):
                        e = Trk(e)
                        e.wait_ge(gws.h, gw_oc[oc])
                        if oc == 0:
                            e.wait_ge(vm.h, v_xf2)
                        if oc >= 2:
                            e.wait_ge(am.h, am_oc[oc - 2])
                        for (lo, hi) in HV:
                            for ic in range(NC):
                                e.matmul(ps_qk[par][:, lo:hi],
                                         wslab[par][:, ic * 128:(ic + 1) * 128],
                                         xf[:, ic * T + lo: ic * T + hi],
                                         start=(ic == 0), stop=(ic == NC - 1))
                        tm_oc[oc] = tm.inc(e.last)

                    @block.scalar
                    def _(e, oc=oc, par=par):
                        e = Trk(e)
                        e.wait_ge(tm.h, tm_oc[oc])
                        if oc >= 2:
                            e.wait_ge(vm.h, vm_fin[oc - 2])
                        am_oc[oc] = am.inc(e.activation(
                            qraw[par][:], ps_qk[par][:], AF.Identity, bias=bias_sbuf[:, oc:oc + 1]))

                    @block.vector
                    def _(e, oc=oc, par=par):
                        e = Trk(e)
                        e.wait_ge(am.h, am_oc[oc])
                        if oc >= 1:
                            e.wait_ge(vm.h, vm_fin.get(oc - 1, 0))
                        e.tensor_mul(q2_sb[:], qraw[par][:], qraw[par][:])
                        vm_oc[oc] = vm.inc(e.last)

                    @block.tensor
                    def _(e, oc=oc):
                        e = Trk(e)
                        e.wait_ge(vm.h, vm_oc[oc])
                        for (lo, hi) in HV:
                            e.matmul(ps_ss[0:1, lo:hi], ones_cb[:, :], q2_sb[:, lo:hi],
                                     start=True, stop=True)
                        tm.inc(e.last)
                    t_ss = tm.n

                    @block.vector
                    def _(e, oc=oc):
                        e = Trk(e)
                        e.wait_ge(tm.h, t_ss)
                        e.tensor_scalar_mul(row_m[:], ps_ss[0:1, 0:T], 1.0 / 128.0)
                        vm.inc(e.last)
                    v_ms = vm.n

                    @block.scalar
                    def _(e, oc=oc):
                        e = Trk(e)
                        e.wait_ge(vm.h, v_ms)
                        e.activation(row_s[:], row_m[:], AF.Ln, bias=eps_ap[0:1, 0:1])
                        e.activation(row_r[:], row_s[:], AF.Exp, scale=-0.5)
                        am.inc(e.last)
                    a_rs = am.n

                    @block.tensor
                    def _(e, oc=oc):
                        e = Trk(e)
                        e.wait_ge(am.h, a_rs)
                        for (lo, hi) in HV:
                            e.matmul(ps_rb[:, lo:hi], ones_rf[0:1, :], row_r[0:1, lo:hi],
                                     start=True, stop=True)
                        tm_rb[oc] = tm.inc(e.last)

                    @block.vector
                    def _(e, oc=oc, par=par):
                        e = Trk(e)
                        e.wait_ge(tm.h, tm_rb[oc])
                        if oc >= 2:
                            e.wait_ge(dy.h, dy_fin.get(oc - 2, 0))
                        e.tensor_mul(qraw[par][:], qraw[par][:], ps_rb[:])
                        e.tensor_scalar_mul(kq_out[par][:], qraw[par][:], gain_sbuf[:, 0:1])
                        vm_fin[oc] = vm.inc(e.last)

                    @block.sync
                    def _(e, oc=oc, par=par):
                        e = Trk(e)
                        e.wait_ge(vm.h, vm_fin[oc])
                        if dst[0] == "dram":
                            g = oc // c.HPG
                            r0 = (oc % c.HPG) * 128
                            dy_fin[oc] = dy.inc(e.dma_start(dst[1][g].ap()[r0:r0 + 128, :], kq_out[par][:]), 16)
                        else:
                            dy_fin[oc] = dy.inc(e.dma_start(dst[1].ap()[oc * 128:(oc + 1) * 128, :], kq_out[par][:]), 16)
            return dict(vm_fin=vm_fin)

        kq_loop(wk_d, bk_sb, gk_sb, ("dram", kg_own), eps6_sb, "k")
        dy_after_k = dy.n

        with ExitStack() as vctx:
            wvslab = [vctx.enter_context(nc.sbuf_tensor(f"wvslab{i}", [128, NC * c.OGW], BF16))
                      for i in range(2)]
            ps_v = vctx.enter_context(nc.psum_tensor("ps_v", [128, NT * 512], F32))
            tm_v0 = tm.n
            gw_og, tm_og, am_ot, dy_v = {}, {}, {}, {}
            for og in range(c.OG):
                par = og % 2

                @block.gpsimd
                def _(e, og=og, par=par):
                    e = Trk(e)
                    if og >= 2:
                        e.wait_ge(tm.h, tm_og[og - 2])
                    else:
                        e.wait_ge(tm.h, tm_v0)
                    gw_og[og] = gw[par].inc(e.dma_start(wvslab[par][:], wv_d.ap()[og, :, :]), 16)

                @block.tensor
                def _(e, og=og, par=par):
                    e = Trk(e)
                    e.wait_ge(gw[par].h, gw_og[og])
                    if og == 0:
                        e.wait_ge(vm.h, v_xf2)
                        e.wait_ge(gm.h, gm_setup)
                    if og >= 1:
                        e.wait_ge(am.h, am_ot[(og - 1, NT - 1)])
                    for tt in range(NT):
                        for ic in range(NC):
                            e.matmul(ps_v[:, tt * 512:tt * 512 + c.OGW],
                                     xf[:, ic * T + tt * 128: ic * T + (tt + 1) * 128],
                                     wvslab[par][:, ic * c.OGW:(ic + 1) * c.OGW],
                                     start=(ic == 0), stop=False)
                        e.matmul(ps_v[:, tt * 512:tt * 512 + c.OGW],
                                 ones_rb[0:1, :], bv_sb[0:1, og * c.OGW:(og + 1) * c.OGW],
                                 start=False, stop=True)
                        tm_og[og] = tm.inc(e.last)

                for tt in range(NT):
                    @block.scalar
                    def _(e, og=og, tt=tt):
                        e = Trk(e)
                        e.wait_ge(tm.h, tm_og[og] - (NT - 1 - tt))
                        if (og, tt) != (0, 0):
                            prev = (og, tt - 1) if tt > 0 else (og - 1, NT - 1)
                            e.wait_ge(dy.h, dy_v.get(prev, 0))
                        am_ot[(og, tt)] = am.inc(e.activation(
                            vout_sb[tt % 2][:, 0:c.OGW], ps_v[:, tt * 512:tt * 512 + c.OGW], AF.Copy))

                    @block.sync
                    def _(e, og=og, tt=tt):
                        e = Trk(e)
                        e.wait_ge(am.h, am_ot[(og, tt)])
                        g = og // (c.OG // c.NG)
                        hg0 = (og % (c.OG // c.NG)) * (c.OGW // 128)
                        nj = c.OGW // 128
                        dst4 = vg_own[g].ap().rearrange("(a b p) d -> p a b d", b=NT, p=128)
                        src = vout_sb[tt % 2][:, 0:c.OGW].rearrange("p (j d) -> p j d", d=128)
                        dy_v[(og, tt)] = dy.inc(
                            e.dma_start(dst4[:, hg0:hg0 + nj, tt, :], src), 16)

                ogpg = c.OG // c.NG
                if og % ogpg == ogpg - 1:
                    g = og // ogpg

                    @block.gpsimd
                    def _(e, og=og, g=g):
                        e = Trk(e)
                        e.wait_ge(dy.h, dy_v[(og, NT - 1)])
                        e.wait_ge(dy.h, dy_after_k)
                        cc.inc(e.collective_compute(
                            "AllGather", ALU.bypass, replica_groups=[list(range(NCORES))],
                            ins=[kg_own[g].ap().opt()], outs=[kg_gth[g].ap().opt()]), 1)
                        cc.inc(e.collective_compute(
                            "AllGather", ALU.bypass, replica_groups=[list(range(NCORES))],
                            ins=[vg_own[g].ap().opt()], outs=[vg_gth[g].ap().opt()]), 1)
            cc_after_g = {g: None for g in range(c.NG)}
            ccn = cc.n
            for g in range(c.NG):
                cc_after_g[g] = ccn - 2 * (c.NG - 1 - g)

        kq_loop(wq_d, bq_sb, gq_sb, ("dram1", q_d), eps6_sb, "q")
        dy_qdone = dy.n
        qstk.close()

        if True:
            with ExitStack() as actx:
                kh = [actx.enter_context(nc.sbuf_tensor(f"kh{i}", [128, NCORES * T], BF16)) for i in range(2)]
                vh = [actx.enter_context(nc.sbuf_tensor(f"vh{i}", [128, NKC * 128], BF16)) for i in range(2)]
                qh = [actx.enter_context(nc.sbuf_tensor(f"qh{i}", [128, T], BF16)) for i in range(2)]
                pt_sb = actx.enter_context(nc.sbuf_tensor("pt_sb", [128, 3 * T], BF16))
                invl_sb = actx.enter_context(nc.sbuf_tensor("invl_sb", [1, T], F32))
                invlb_sb = actx.enter_context(nc.sbuf_tensor("invlb_sb", [128, T], F32))
                ao_sb = [actx.enter_context(nc.sbuf_tensor(f"ao_sb{i}", [128, T], BF16)) for i in range(2)]
                ps_s = [actx.enter_context(nc.psum_tensor(f"att_s{i}", [128, T], F32)) for i in range(2)]
                ps_o = actx.enter_context(nc.psum_tensor("att_o", [128, T], F32))
                ps_l = actx.enter_context(nc.psum_tensor("att_l", [1, T], F32))

                tm_hd = {}
                vm_hd = {}
                dxk = {}
                dxv = {}
                dy_ao = {}
                for h in range(H):
                    par = h % 2
                    g = h // c.HPG
                    hh = h % c.HPG
                    tm0 = tm.n
                    vm0 = vm.n
                    am0 = am.n

                    @block.sync
                    def _(e, h=h, par=par, g=g, hh=hh):
                        e = Trk(e)
                        e.wait_ge(cc.h, cc_after_g[g])
                        if h >= 2:
                            e.wait_ge(tm.h, tm_hd[h - 2])
                        src = kg_gth[g].ap().rearrange("(r a p) t -> p r a t", a=c.HPG, p=128)
                        dst = kh[par][:].rearrange("p (r t) -> p r t", t=T)
                        dxk[h] = dx[par].inc(e.dma_start(dst, src[:, :, hh, :]), 16)
                        e.wait_ge(dy.h, dy_qdone)
                        dxk[h] = dx[par].inc(
                            e.dma_start(qh[par][:], q_d.ap()[h * 128:(h + 1) * 128, :]), 16)
                        vsrc = vg_gth[g].ap().rearrange("(r a b p) d -> p r a b d",
                                                        a=c.HPG, b=NT, p=128)
                        vdst = vh[par][:].rearrange("p (r b d) -> p r b d", b=NT, d=128)
                        for r in range(NCORES):
                            dxv[h] = dx[par].inc(
                                e.dma_start(vdst[:, r, :, :], vsrc[:, r, hh, :, :]), 16)

                    @block.tensor
                    def _(e, h=h, par=par, tm0=tm0, vm0=vm0, am0=am0):
                        e = Trk(e)
                        e.wait_ge(dx[par].h, dxk[h])
                        e.wait_ge(am.h, am0)
                        if h >= 1:
                            e.wait_ge(vm.h, vm_hd[h - 1])
                        qv = qh[par]
                        for cix in range(2):
                            for (lo, hi) in HV:
                                e.matmul(ps_s[cix][:, lo:hi],
                                         kh[par][:, cix * 128:(cix + 1) * 128],
                                         qv[:, lo:hi], start=True, stop=True)
                            tm.inc(e.last)
                        e.wait_ge(vm.h, vm0 + 1)
                        tm.inc(e.transpose(ps_o[0:1, 0:128], pm_sb[:, :], idf_sb[:, :]))
                        e.wait_ge(vm.h, vm0 + 2)
                        tm.inc(e.matmul(ps_o[:, 0:1], ones_rf[0:1, :], brow_sb[0:1, 0:1],
                                        start=True, stop=True))
                        for cix in range(NKC):
                            e.wait_ge(am.h, am0 + cix + 1)
                            if cix == 0:
                                e.wait_ge(dx[par].h, dxv[h])
                            ptv = pt_sb[:, (cix % 3) * T:(cix % 3 + 1) * T]
                            for (lo, hi) in HV:
                                e.matmul(ps_o[:, lo:hi], vh[par][:, cix * 128:(cix + 1) * 128],
                                         ptv[:, lo:hi], start=(cix == 0), stop=(cix == NKC - 1))
                            for (lo, hi) in HV:
                                e.matmul(ps_l[0:1, lo:hi], ones_cb[:, :], ptv[:, lo:hi],
                                         start=(cix == 0), stop=(cix == NKC - 1))
                            if cix + 2 < NKC:
                                for (lo, hi) in HV:
                                    e.matmul(ps_s[cix % 2][:, lo:hi],
                                             kh[par][:, (cix + 2) * 128:(cix + 3) * 128],
                                             qv[:, lo:hi], start=True, stop=True)
                            tm.inc(e.last)
                        e.wait_ge(vm.h, vm0 + 4)
                        for (lo, hi) in HV:
                            e.matmul(ps_s[0][:, lo:hi], ones_rf[0:1, :], invl_sb[0:1, lo:hi],
                                     start=True, stop=True)
                        tm_hd[h] = tm.inc(e.last)
                        assert tm_hd[h] == tm0 + 5 + NKC

                    @block.scalar
                    def _(e, h=h, tm0=tm0, vm0=vm0, am0=am0):
                        e = Trk(e)
                        for cix in range(NKC):
                            if cix < 2:
                                e.wait_ge(tm.h, tm0 + cix + 1)
                            else:
                                e.wait_ge(tm.h, tm0 + cix + 3)
                            if cix == 0:
                                e.wait_ge(vm.h, vm0 + 3)
                            ptv = pt_sb[:, (cix % 3) * T:(cix % 3 + 1) * T]
                            am.inc(e.activation(ptv[:], ps_s[cix % 2][:], AF.Exp,
                                                bias=bias_sb[:, 0:1], scale=c.scale))

                    @block.vector
                    def _(e, h=h, par=par, tm0=tm0, vm0=vm0):
                        e = Trk(e)
                        e.wait_ge(tm.h, tm0 + 1)
                        e.reduce_max(pm_sb[:], ps_s[0][:], axis=AX.X)
                        vm.inc(e.last)
                        e.wait_ge(tm.h, tm0 + 3)
                        e.reduce_max(mrow_sb[:], ps_o[0:1, 0:128], axis=AX.X)
                        e.tensor_scalar_mul(brow_sb[:], mrow_sb[:], 0.25)
                        e.tensor_scalar_mul(b2row_sb[:], mrow_sb[:], -0.25)
                        e.tensor_max(brow_sb[:], brow_sb[:], b2row_sb[:])
                        e.tensor_add(brow_sb[:], brow_sb[:], mrow_sb[:])
                        e.tensor_scalar_add(brow_sb[:], brow_sb[:], 10.0)
                        e.tensor_scalar_mul(brow_sb[:], brow_sb[:], -c.scale)
                        vm.inc(e.last)
                        e.wait_ge(tm.h, tm0 + 4)
                        e.tensor_copy(bias_sb[:], ps_o[:, 0:1])
                        vm.inc(e.last)
                        e.wait_ge(tm.h, tm0 + 4 + NKC)
                        e.reciprocal(invl_sb[:], ps_l[0:1, :])
                        vm.inc(e.last)
                        e.wait_ge(tm.h, tm0 + 5 + NKC)
                        if h >= 2:
                            e.wait_ge(dy.h, dy_ao[h - 2])
                        e.tensor_copy(invlb_sb[:], ps_s[0][:])
                        e.tensor_mul(ao_sb[par][:], ps_o[:], invlb_sb[:])
                        vm_hd[h] = vm.inc(e.last)
                        assert vm_hd[h] == vm0 + 5

                    @block.sync
                    def _(e, h=h, par=par):
                        e = Trk(e)
                        e.wait_ge(vm.h, vm_hd[h])
                        dy_ao[h] = dy.inc(e.dma_start(ao_d.ap()[h * 128:(h + 1) * 128, :], ao_sb[par][:]), 16)
                dy_after_ao = dy.n

        with ExitStack() as octx:
            aob = octx.enter_context(nc.sbuf_tensor("aob", [128, H * T], BF16))
            p0 = octx.enter_context(nc.psum_tensor("op0", [128, T], F32))
            p1 = octx.enter_context(nc.psum_tensor("op1", [128, T], F32))
            wslab = [octx.enter_context(nc.sbuf_tensor(f"wslo{i}", [128, NC * 128], BF16)) for i in range(2)]
            ps_op = [p0, p1]

            @block.sync
            def _(e):
                e = Trk(e)
                e.wait_ge(dy.h, dy_after_ao)
                for af in range(H):
                    dy.inc(e.dma_start(aob[:, af * T:(af + 1) * T],
                                       ao_d.ap()[af * 128:(af + 1) * 128, :]), 16)
            dy_aob = dy.n

            gw_oc, tm_oc, am_oc, vm_oc, dxo_oc, dyr_oc = {}, {}, {}, {}, {}, {}
            tm_op0 = tm.n
            vm_op0 = vm.n
            for oc in range(NC):
                par = oc % 2

                @block.gpsimd
                def _(e, oc=oc, par=par):
                    e = Trk(e)
                    if oc >= 2:
                        e.wait_ge(tm.h, tm_oc[oc - 2])
                    else:
                        e.wait_ge(tm.h, tm_op0)
                    gw_oc[oc] = gw[par].inc(e.dma_start(wslab[par][:], wo_d.ap()[oc, :, :]), 16)

                @block.sync
                def _(e, oc=oc, par=par):
                    e = Trk(e)
                    if oc >= 2:
                        e.wait_ge(vm.h, vm_oc[oc - 2])
                    else:
                        e.wait_ge(vm.h, vm_op0)
                    dxo_oc[oc] = dx[par].inc(
                        e.dma_start(xinb[par][:], res1_d.ap()[oc * 128:(oc + 1) * 128, :]), 16)

                @block.tensor
                def _(e, oc=oc, par=par):
                    e = Trk(e)
                    e.wait_ge(gw[par].h, gw_oc[oc])
                    if oc == 0:
                        e.wait_ge(dy.h, dy_aob)
                    if oc >= 2:
                        e.wait_ge(am.h, am_oc[oc - 2])
                    for (lo, hi) in HV:
                        for af in range(H):
                            e.matmul(ps_op[par][:, lo:hi],
                                     wslab[par][:, af * 128:(af + 1) * 128],
                                     aob[:, af * T + lo: af * T + hi],
                                     start=(af == 0), stop=(af == H - 1))
                    tm_oc[oc] = tm.inc(e.last)

                @block.scalar
                def _(e, oc=oc, par=par):
                    e = Trk(e)
                    e.wait_ge(tm.h, tm_oc[oc])
                    if oc >= 2:
                        e.wait_ge(vm.h, vm_oc[oc - 2])
                    am_oc[oc] = am.inc(e.activation(
                        qraw[par][:], ps_op[par][:], AF.Identity, bias=bo_sb[:, oc:oc + 1]))

                @block.vector
                def _(e, oc=oc, par=par):
                    e = Trk(e)
                    e.wait_ge(am.h, am_oc[oc])
                    e.wait_ge(dx[par].h, dxo_oc[oc])
                    if oc >= 2:
                        e.wait_ge(dy.h, dyr_oc[oc - 2])
                    e.tensor_add(rout[par][:], qraw[par][:], xinb[par][:])
                    if oc == 0:
                        e.tensor_copy(acc_s[:], rout[par][:])
                        e.tensor_mul(acc_q[:], rout[par][:], rout[par][:])
                    else:
                        e.tensor_add(acc_s[:], acc_s[:], rout[par][:])
                        e.tensor_mul(sqs[:], rout[par][:], rout[par][:])
                        e.tensor_add(acc_q[:], acc_q[:], sqs[:])
                    vm_oc[oc] = vm.inc(e.last)

                @block.sync
                def _(e, oc=oc, par=par):
                    e = Trk(e)
                    e.wait_ge(vm.h, vm_oc[oc])
                    dyr_oc[oc] = dy.inc(e.dma_start(res2_d.ap()[oc * 128:(oc + 1) * 128, :], rout[par][:]), 16)

    if phases == 1:
        ffn_phase(x_d, None, 0)
    else:
        ffn_phase(x_d, res1_d, 0)
    if phases >= 2 and STOP_AFTER is None:
        qkv_attn_phase()
    if phases == 2 and STOP_AFTER is None:
        @block.sync
        def _(e):
            e = Trk(e)
            e.wait_ge(dy.h, dy.n)
            for ic in range(NC):
                dy.inc(e.dma_start(xinb[0][:], res2_d.ap()[ic * 128:(ic + 1) * 128, :]), 16)
                e.wait_ge(dy.h, dy.n)
                dy.inc(e.dma_start(out_d.ap()[ic * 128:(ic + 1) * 128, :], xinb[0][:]), 16)
                e.wait_ge(dy.h, dy.n)
    if phases >= 3 and STOP_AFTER is None:
        ffn_phase(res2_d, None, 1)

    @block.sync
    def _(e):
        e = Trk(e)
        e.wait_ge(dy.h, dy.n)
        for s_ in (dx[0], dx[1]):
            e.wait_ge(s_.h, s_.n)

    ctx.close()
    return nc


def _block_oc(w):
    Din, Dout = w.shape
    return np.ascontiguousarray(
        w.reshape(Din // 128, 128, Dout // 128, 128).transpose(2, 1, 0, 3)
         .reshape(Dout // 128, 128, Din))


def _block_og(w, og):
    Din, Dout = w.shape
    ow = Dout // og
    return np.ascontiguousarray(
        w.reshape(Din // 128, 128, og, ow).transpose(2, 1, 0, 3)
         .reshape(og, 128, (Din // 128) * ow))


def prep_inputs(cfg, inp):
    c = cfg
    f = np.float32
    wq = np.asarray(inp["L0_attn_Wq"], f)
    wk = np.asarray(inp["L0_attn_Wk"], f)
    wv = np.asarray(inp["L0_attn_Wv"], f)
    wo = np.asarray(inp["L0_attn_Wo"], f)
    wff = np.asarray(inp["L0_ffn_weight"], f).T
    la = np.asarray(inp["L0_ffn_lora_A"], f)
    lb = np.asarray(inp["L0_ffn_lora_B"], f)
    gw1 = np.asarray(inp["L0_ffn_gw1"], f)
    gw2 = np.asarray(inp["L0_ffn_gw2"], f)
    a_cat = la.transpose(1, 0, 2).reshape(c.D, c.RC)
    b_cat = lb.reshape(c.RC, c.D)
    mp = np.zeros((c.E, 128), f)
    for e in range(c.E):
        mp[e, e * c.R:(e + 1) * c.R] = 1.0 / c.R
    d = {
        "wq_b": _block_oc(wq), "wk_b": _block_oc(wk), "wo_b": _block_oc(wo),
        "wff_b": _block_oc(wff), "wv_b": _block_og(wv, c.OG),
        "a_b": np.ascontiguousarray(
            a_cat.reshape(c.NC, 128, c.RC).transpose(1, 0, 2).reshape(128, c.NC * c.RC)),
        "b_cat": np.ascontiguousarray(b_cat.reshape(128, c.D)) if c.RC == 128 else None,
        "gw1_b": np.ascontiguousarray(
            gw1.T.reshape(c.NC, 128, c.GD).transpose(1, 0, 2).reshape(128, c.NC * c.GD)),
        "gw2_t": np.ascontiguousarray(gw2.T),
        "bq_c": np.ascontiguousarray(np.asarray(inp["L0_attn_bq"], f).reshape(c.NC, 128).T),
        "bk_c": np.ascontiguousarray(np.asarray(inp["L0_attn_bk"], f).reshape(c.NC, 128).T),
        "bo_c": np.ascontiguousarray(np.asarray(inp["L0_attn_bo"], f).reshape(c.NC, 128).T),
        "bv_r": np.asarray(inp["L0_attn_bv"], f).reshape(1, c.D).astype(ml_dtypes.bfloat16),
        "gb1_c": np.asarray(inp["L0_ffn_gb1"], f).reshape(c.GD, 1),
        "gb2_r": np.asarray(inp["L0_ffn_gb2"], f).reshape(1, c.E),
        "lng_c": np.asarray(inp["L0_ffn_ln_gamma"], f).reshape(c.GD, 1),
        "lnb_c": np.asarray(inp["L0_ffn_ln_beta"], f).reshape(c.GD, 1),
        "gq_c": np.asarray(inp["L0_attn_gq"], f).reshape(128, 1),
        "gk_c": np.asarray(inp["L0_attn_gk"], f).reshape(128, 1),
        "map_alpha": mp,
        "ident_f32": np.eye(128, dtype=f),
    }
    assert d["b_cat"] is not None, "RC must be 128"
    return d


CFG_FULL = Cfg()
_CACHE = {}


def kernel(**inputs):
    cfg = CFG_FULL
    x = np.asarray(inputs["L0_attn_hidden_states"], np.float32)[0]
    shared = prep_inputs(cfg, inputs)
    in_maps = []
    for cix in range(NCORES):
        m = dict(shared)
        m["x_fm"] = np.ascontiguousarray(x[cix * cfg.T:(cix + 1) * cfg.T, :].T)
        in_maps.append(m)
    key = "full"
    if key not in _CACHE:
        _CACHE[key] = build(cfg)
    nc = _CACHE[key]
    res = run_bass_kernel_spmd(nc, in_maps, core_ids=list(range(NCORES)))
    out = np.empty((1, NCORES * cfg.T, cfg.D), np.float32)
    for cix in range(NCORES):
        out[0, cix * cfg.T:(cix + 1) * cfg.T, :] = res.results[cix]["out_fm"].T
    return out


# version 08
# speedup vs baseline: 1.0292x; 1.0292x over previous
import math
from contextlib import ExitStack

import numpy as np
import ml_dtypes

import concourse.bass as bass
import concourse.mybir as mybir
from concourse.bass_utils import run_bass_kernel_spmd

F32 = mybir.dt.float32
BF16 = mybir.dt.bfloat16
AF = mybir.ActivationFunctionType
ALU = mybir.AluOpType
AX = mybir.AxisListType

NCORES = 8
STOP_AFTER = None


class Cfg:
    def __init__(self, D=4096, T=1024, H=32, E=8, R=16, GD=32, capfrac=3.0):
        self.D = D
        self.T = T
        self.H = H
        self.HD = 128
        self.E = E
        self.R = R
        self.RC = E * R
        self.GD = GD
        self.cap = float(int(capfrac * (NCORES * T) / E))
        self.NC = D // 128
        self.NT = T // 128
        self.NK = NCORES * T // 128
        self.NG = 4
        self.HPG = H // self.NG
        self.OG = max(8, D // 256)
        self.OGW = D // self.OG
        assert D % 128 == 0 and T % 512 == 0 and H % self.NG == 0 and H == D // 128
        assert self.OGW % 128 == 0 and (self.OGW // 128) * self.OG * 128 == D
        self.scale = 1.0 / math.sqrt(self.HD)


class Sem:
    def __init__(self, ctx, nc, name):
        self.h = ctx.enter_context(nc.semaphore(name))
        self.n = 0

    def inc(self, instr, k=1):
        instr.then_inc(self.h, k)
        self.n += k
        return self.n


_TRK_SKIP = {"nop", "engine_nop", "wait_ge", "wait_op", "drain", "sem_inc"}
_TRK_DRAIN = {"tensor_scalar", "tensor_single_scalar", "tensor_scalar_add",
              "tensor_scalar_sub", "tensor_scalar_mul", "tensor_scalar_max",
              "tensor_scalar_min", "scalar_tensor_tensor", "reciprocal"}


def _is_ap(x):
    return hasattr(x, "tensor") or hasattr(x, "ap")


class Trk:

    def __init__(self, e):
        object.__setattr__(self, "_e", e)
        object.__setattr__(self, "last", None)

    def __getattr__(self, n):
        f = getattr(self._e, n)
        if callable(f) and not n.startswith("_") and n not in _TRK_SKIP:
            is_dve = type(self._e).__name__ == "BassVectorEngine"
            def g(*a, **k):
                if is_dve and n != "dma_start" and self.last is not None:
                    self._e.drain()
                r = f(*a, **k)
                if hasattr(r, "then_inc"):
                    object.__setattr__(self, "last", r)
                return r
            return g
        return f


def build(cfg: Cfg, taps=(), detect_races=False, phases=3):
    c = cfg
    nc = bass.Bass("TRN2", target_bir_lowering=False, detect_race_conditions=detect_races)
    T, D, NC, NT, H, E, GD, RC = c.T, c.D, c.NC, c.NT, c.H, c.E, c.GD, c.RC
    HV = [(i * 512, (i + 1) * 512) for i in range(T // 512)]
    NKC = c.NK

    def xin(name, shape, dtype=F32):
        return nc.dram_tensor(name, list(shape), dtype, kind="ExternalInput")

    x_d = xin("x_fm", [D, T])
    wq_d = xin("wq_b", [NC, 128, D])
    wk_d = xin("wk_b", [NC, 128, D])
    wo_d = xin("wo_b", [NC, 128, D])
    wff_d = xin("wff_b", [NC, 128, D])
    wv_d = xin("wv_b", [c.OG, 128, NC * c.OGW])
    a_d = xin("a_b", [128, NC * RC])
    bcat_d = xin("b_cat", [128, D])
    gw1_d = xin("gw1_b", [128, NC * GD])
    gw2_d = xin("gw2_t", [GD, E])
    bq_d = xin("bq_c", [128, NC])
    bk_d = xin("bk_c", [128, NC])
    bo_d = xin("bo_c", [128, NC])
    bv_d = xin("bv_r", [1, D], BF16)
    gb1_d = xin("gb1_c", [GD, 1])
    gb2_d = xin("gb2_r", [1, E])
    lng_d = xin("lng_c", [GD, 1])
    lnb_d = xin("lnb_c", [GD, 1])
    gq_d = xin("gq_c", [128, 1])
    gk_d = xin("gk_c", [128, 1])
    map_d = xin("map_alpha", [E, 128])
    idf_d = xin("ident_f32", [128, 128])

    out_d = nc.dram_tensor("out_fm", [D, T], F32, kind="ExternalOutput")
    tap_d = {}
    for tname, tshape, tdt in taps:
        tap_d[tname] = nc.dram_tensor(tname, list(tshape), tdt, kind="ExternalOutput")

    res1_d = nc.dram_tensor("res1", [D, T], F32, kind="Internal")
    res2_d = nc.dram_tensor("res2", [D, T], F32, kind="Internal")
    ao_d = nc.dram_tensor("ao", [H * 128, T], BF16, kind="Internal")
    q_d = nc.dram_tensor("q_sp", [H * 128, T], BF16, kind="Internal")
    kg_own = [nc.dram_tensor(f"kg_own{g}", [c.HPG * 128, T], BF16, kind="Internal") for g in range(c.NG)]
    vg_own = [nc.dram_tensor(f"vg_own{g}", [c.HPG * NT * 128, 128], BF16, kind="Internal") for g in range(c.NG)]
    kg_gth = [nc.dram_tensor(f"kg_gth{g}", [NCORES * c.HPG * 128, T], BF16, kind="Internal") for g in range(c.NG)]
    vg_gth = [nc.dram_tensor(f"vg_gth{g}", [NCORES * c.HPG * NT * 128, 128], BF16, kind="Internal") for g in range(c.NG)]
    cnt_in = [nc.dram_tensor(f"cnt_in{i}", [1, 2 * E], F32, kind="Internal") for i in range(2)]
    cnt_out = [nc.dram_tensor(f"cnt_out{i}", [1, 2 * E], F32, kind="Internal") for i in range(2)]

    ctx = ExitStack()
    sb = lambda name, shape, dt: ctx.enter_context(nc.sbuf_tensor(name, list(shape), dt))

    acc_s = sb("acc_s", [128, T], F32)
    acc_q = sb("acc_q", [128, T], F32)
    tbuf = [sb(f"tchunk{i}", [128, T], F32) for i in range(2)]
    xinb = [sb(f"xin{i}", [128, T], F32) for i in range(2)]
    rout = [sb(f"rout{i}", [128, T], F32) for i in range(2)]
    sqs = sb("sqs", [128, T], F32)
    a_sb = sb("a_sb", [128, NC * RC], BF16)
    bcat_sb = sb("bcat_sb", [128, D], BF16)
    gw1_sb = sb("gw1_sb", [128, NC * GD], F32)
    gw2_sb = sb("gw2_sb", [GD, E], F32)
    bq_sb = sb("bq_sb", [128, NC], F32)
    bk_sb = sb("bk_sb", [128, NC], F32)
    bo_sb = sb("bo_sb", [128, NC], F32)
    bv_sb = sb("bv_sb", [1, D], BF16)
    gb1_sb = sb("gb1_sb", [GD, 1], F32)
    gb2row = sb("gb2row", [1, E], F32)
    gb2b_sb = sb("gb2b_sb", [128, E], F32)
    lng_sb = sb("lng_sb", [GD, 1], F32)
    lnb_sb = sb("lnb_sb", [GD, 1], F32)
    gq_sb = sb("gq_sb", [128, 1], F32)
    gk_sb = sb("gk_sb", [128, 1], F32)
    map_sb = sb("map_sb", [E, 128], F32)
    idf_sb = sb("idf_sb", [128, 128], F32)
    eps5_sb = sb("eps5_sb", [1, 1], F32)
    eps6_sb = sb("eps6_sb", [1, 1], F32)
    ones_cb = sb("ones_cb", [128, 1], BF16)
    ones_rb = sb("ones_rb", [1, 128], BF16)
    ones_cf = sb("ones_cf", [128, 1], F32)
    ones_rf = sb("ones_rf", [1, 128], F32)
    row_s = sb("row_s", [1, T], F32)
    row_q = sb("row_q", [1, T], F32)
    row_m = sb("row_m", [1, T], F32)
    row_r = sb("row_r", [1, T], F32)
    g_sb = sb("g_sb", [128, E], F32)
    gm_sb = sb("gm_sb", [128, E], F32)
    eq_sb = sb("eq_sb", [128, 2 * E * NT], F32)
    m1_sb = sb("m1_sb", [128, 1], F32)
    m2_sb = sb("m2_sb", [128, 1], F32)
    e21_sb = sb("e21_sb", [128, 1], F32)
    zden_sb = sb("zden_sb", [128, 1], F32)
    g12_sb = sb("g12_sb", [128, 2 * NT], F32)
    wz_sb = sb("wz_sb", [128, 2 * E], F32)
    w_sb = sb("w_sb", [128, E], F32)
    cnt_sb = sb("cnt_sb", [1, 2 * E], F32)
    alw_row = sb("alw_row", [1, 2 * E], F32)
    alw_sb = sb("alw_sb", [128, 2 * E], F32)
    pm_sb = sb("pm_sb", [128, 1], F32)
    mrow_sb = sb("mrow_sb", [1, 1], F32)
    brow_sb = sb("brow_sb", [1, 1], F32)
    b2row_sb = sb("b2row_sb", [1, 1], F32)
    bias_sb = sb("bias_sb", [128, 1], F32)
    qraw = [sb(f"qraw{i}", [128, T], F32) for i in range(2)]
    q2_sb = sb("q2_sb", [128, T], BF16)
    kq_out = [sb(f"kq_out{i}", [128, T], BF16) for i in range(2)]
    vout_sb = [sb(f"vout{i}", [128, c.OGW], BF16) for i in range(2)]

    mks = lambda name: Sem(ctx, nc, name)
    dx = [mks("dx0"), mks("dx1")]
    dy = mks("dy")
    gw = [mks("gw0"), mks("gw1")]
    gz = mks("gz")
    cc = mks("cc")
    tm = mks("tm")
    vm = mks("vm")
    am = mks("am")
    gm = mks("gm")

    block = ctx.enter_context(nc.Block())

    @block.sync
    def _(e):
        e = Trk(e)
        for t, d in [(bq_sb, bq_d), (bk_sb, bk_d), (bo_sb, bo_d),
                     (gb1_sb, gb1_d), (lng_sb, lng_d), (lnb_sb, lnb_d),
                     (gq_sb, gq_d), (gk_sb, gk_d), (map_sb, map_d),
                     (idf_sb, idf_d), (gw2_sb, gw2_d), (gw1_sb, gw1_d),
                     (gb2row, gb2_d)]:
            dy.n = dy.inc(e.dma_start(t[:], d.ap()), 16)

    @block.gpsimd
    def _(e):
        e = Trk(e)
        gz.inc(e.dma_start(a_sb[:], a_d.ap()), 16)
        gz.inc(e.dma_start(bcat_sb[:], bcat_d.ap()), 16)
        gz.inc(e.dma_start(bv_sb[:], bv_d.ap()), 16)
        gm.inc(e.memset(eps5_sb[:], 1e-5))
        gm.inc(e.memset(eps6_sb[:], 1e-6))
        gm.inc(e.memset(ones_cb[:], 1.0))
        gm.inc(e.memset(ones_rb[:], 1.0))
        gm.inc(e.memset(ones_cf[:], 1.0))
        gm.inc(e.memset(ones_rf[:], 1.0))
    dy_setup, gz_setup, gm_setup = dy.n, gz.n, gm.n

    @block.scalar
    def _(e):
        e = Trk(e)
        e.wait_ge(gm.h, gm_setup)

    with nc.psum_tensor("ps_gb2", [128, E], F32) as ps_gb2:
        @block.tensor
        def _(e):
            e = Trk(e)
            e.wait_ge(dy.h, dy_setup)
            e.wait_ge(gm.h, gm_setup)
            tm.inc(e.matmul(ps_gb2[:, :], ones_rf[0:1, :], gb2row[0:1, :],
                            start=True, stop=True))
        tm_gb2 = tm.n

        @block.vector
        def _(e):
            e = Trk(e)
            e.wait_ge(tm.h, tm_gb2)
            vm.inc(e.tensor_copy(gb2b_sb[:], ps_gb2[:]))

    if STOP_AFTER == "setup":
        @block.sync
        def _(e):
            e = Trk(e)
            e.wait_ge(dy.h, dy.n)
        ctx.close()
        return nc
    vm_st = {}
    dx_st = {}
    for ic in range(NC):
        par = ic % 2
        xi, dxs = xinb[par], dx[par]

        @block.sync
        def _(e, ic=ic, xi=xi, dxs=dxs):
            e = Trk(e)
            if ic >= 2:
                e.wait_ge(vm.h, vm_st[ic - 2])
            dx_st[ic] = dxs.inc(e.dma_start(xi[:], x_d.ap()[ic * 128:(ic + 1) * 128, :]), 16)

        @block.vector
        def _(e, ic=ic, xi=xi, dxs=dxs):
            e = Trk(e)
            e.wait_ge(dxs.h, dx_st[ic])
            if ic == 0:
                e.tensor_copy(acc_s[:], xi[:])
                e.tensor_mul(acc_q[:], xi[:], xi[:])
            else:
                e.tensor_add(acc_s[:], acc_s[:], xi[:])
                e.tensor_mul(sqs[:], xi[:], xi[:])
                e.tensor_add(acc_q[:], acc_q[:], sqs[:])
            vm_st[ic] = vm.inc(e.last)

    def ln_rows(ps_sum, eps_ap, nfeat):
        @block.tensor
        def _(e):
            e = Trk(e)
            e.wait_ge(vm.h, vm.n)
            for (lo, hi) in HV:
                e.matmul(ps_sum[0:1, lo:hi], ones_cf[:], acc_s[:, lo:hi], start=True, stop=True)
                e.matmul(ps_sum[0:1, T + lo:T + hi], ones_cf[:], acc_q[:, lo:hi], start=True, stop=True)
            tm.inc(e.last)
        t0 = tm.n
        if STOP_AFTER == "rowsA":
            return

        @block.vector
        def _(e):
            e = Trk(e)
            e.wait_ge(tm.h, t0)
            e.tensor_scalar_mul(row_m[:], ps_sum[0:1, 0:T], 1.0 / nfeat)
            e.tensor_scalar_mul(row_q[:], ps_sum[0:1, T:2 * T], 1.0 / nfeat)
            e.tensor_mul(row_s[:], row_m[:], row_m[:])
            e.tensor_sub(row_q[:], row_q[:], row_s[:])
            vm.inc(e.last)
        v0 = vm.n
        if STOP_AFTER == "rowsB":
            return

        @block.scalar
        def _(e):
            e = Trk(e)
            e.wait_ge(vm.h, v0)
            e.activation(row_s[:], row_q[:], AF.Ln, bias=eps_ap[0:1, 0:1])
            e.activation(row_r[:], row_s[:], AF.Exp, scale=-0.5)
            am.inc(e.last)

    def bcast_rows(ps_meanB, ps_rstdB):
        @block.tensor
        def _(e):
            e = Trk(e)
            e.wait_ge(am.h, am.n)
            for (lo, hi) in HV:
                e.matmul(ps_meanB[:, lo:hi], ones_rf[0:1, :], row_m[0:1, lo:hi], start=True, stop=True)
                e.matmul(ps_rstdB[:, lo:hi], ones_rf[0:1, :], row_r[0:1, lo:hi], start=True, stop=True)
            tm.inc(e.last)

    def ffn_phase(src_d, dst_d, icnt):
        vm_phase0 = vm.n
        tm_phase0 = tm.n
        fstk = ExitStack()
        xf = fstk.enter_context(nc.sbuf_tensor(f"xf_f{icnt}", [128, NC * T], BF16))
        h_sb = fstk.enter_context(nc.sbuf_tensor(f"h_sb_{icnt}", [GD, T], F32))
        h2_sb = fstk.enter_context(nc.sbuf_tensor(f"h2_sb_{icnt}", [GD, T], F32))
        hsq_sb = fstk.enter_context(nc.sbuf_tensor(f"hsq_sb_{icnt}", [GD, T], F32))
        wt_sb = fstk.enter_context(nc.sbuf_tensor(f"wt_sb_{icnt}", [E, T], F32))
        wtexp_sb = fstk.enter_context(nc.sbuf_tensor(f"wtexp_sb_{icnt}", [128, T], F32))
        tmps_sb = fstk.enter_context(nc.sbuf_tensor(f"tmps_sb_{icnt}", [128, T], BF16))
        if STOP_AFTER == "stats":
            fstk.close()
            return
        with nc.psum_tensor(f"ffn_sum_{icnt}", [1, 2 * T], F32) as ps_sum:
            ln_rows(ps_sum, eps5_sb, D)
        if STOP_AFTER == "rowsdbg":
            @block.sync
            def _(e):
                e = Trk(e)
                e.wait_ge(am.h, am.n)
                dy.inc(e.dma_start(out_d.ap()[0:1, :], row_m[0:1, :]), 16)
                dy.inc(e.dma_start(out_d.ap()[1:2, :], row_r[0:1, :]), 16)
                dy.inc(e.dma_start(out_d.ap()[2:3, :], row_q[0:1, :]), 16)
                dy.inc(e.dma_start(out_d.ap()[4:132, :], acc_s[:, :]), 16)
                dy.inc(e.dma_start(out_d.ap()[132:260, :], acc_q[:, :]), 16)
                e.wait_ge(dy.h, dy.n)
        if STOP_AFTER in ("rows", "rowsA", "rowsB", "rowsdbg"):
            fstk.close()
            return

        with nc.psum_tensor(f"ffn_tmp_{icnt}", [128, T], F32) as ps_tmp:
            with (nc.psum_tensor(f"ffn_meanB_{icnt}", [128, T], F32) as ps_meanB,
                  nc.psum_tensor(f"ffn_rstdB_{icnt}", [128, T], F32) as ps_rstdB,
                  nc.psum_tensor(f"ffn_h_{icnt}", [GD, T], F32) as ps_h):
                bcast_rows(ps_meanB, ps_rstdB)
                t_bc = tm.n

                vm_ch = {}
                tm_ch = {}
                dx_ch = {}
                for ic in range(NC):
                    par = ic % 2
                    xi, dxs, tb = xinb[par], dx[par], tbuf[par]

                    @block.sync
                    def _(e, ic=ic, xi=xi, dxs=dxs):
                        e = Trk(e)
                        if ic >= 2:
                            e.wait_ge(vm.h, vm_ch[ic - 2])
                        else:
                            e.wait_ge(vm.h, vm_phase0)
                        dx_ch[ic] = dxs.inc(
                            e.dma_start(xi[:], src_d.ap()[ic * 128:(ic + 1) * 128, :]), 16)

                    @block.vector
                    def _(e, ic=ic, xi=xi, dxs=dxs, tb=tb):
                        e = Trk(e)
                        e.wait_ge(dxs.h, dx_ch[ic])
                        if ic == 0:
                            e.wait_ge(tm.h, t_bc)
                        if ic >= 2:
                            e.wait_ge(tm.h, tm_ch[ic - 2])
                        e.tensor_sub(tb[:], xi[:], ps_meanB[:])
                        e.tensor_mul(xf[:, ic * T:(ic + 1) * T], tb[:], ps_rstdB[:])
                        vm_ch[ic] = vm.inc(e.last)

                    @block.tensor
                    def _(e, ic=ic, tb=tb):
                        e = Trk(e)
                        e.wait_ge(vm.h, vm_ch[ic])
                        if ic == 0:
                            e.wait_ge(gz.h, gz_setup)
                        for (lo, hi) in HV:
                            e.matmul(ps_h[:, lo:hi], gw1_sb[:, ic * GD:(ic + 1) * GD],
                                     tb[:, lo:hi], start=(ic == 0), stop=(ic == NC - 1))
                        for (lo, hi) in HV:
                            e.matmul(ps_tmp[:, lo:hi], a_sb[:, ic * RC:(ic + 1) * RC],
                                     xf[:, ic * T + lo: ic * T + hi],
                                     start=(ic == 0), stop=(ic == NC - 1))
                        tm_ch[ic] = tm.inc(e.last)
                t_hmm = tm.n

                @block.scalar
                def _(e):
                    e = Trk(e)
                    e.wait_ge(tm.h, t_hmm)
                    am.inc(e.activation(h_sb[:], ps_h[:], AF.Copy))
                a_hcp = am.n

                @block.vector
                def _(e):
                    e = Trk(e)
                    e.wait_ge(am.h, a_hcp)
                    e.tensor_mul(h2_sb[:], h_sb[:], ps_rstdB[0:GD, :])
                    e.tensor_scalar_add(h2_sb[:], h2_sb[:], gb1_sb[:, 0:1])
                    vm.inc(e.last)
            if STOP_AFTER == "norm":
                @block.gpsimd
                def _(e):
                    e = Trk(e)
                    e.wait_ge(vm.h, vm.n)
                    gz.inc(e.dma_start(out_d.ap()[0:128, :], xf[:, 0:T]), 16)
                    gz.inc(e.dma_start(out_d.ap()[128:256, :], xf[:, (NC - 1) * T:NC * T]), 16)
                @block.sync
                def _(e):
                    e = Trk(e)
                    e.wait_ge(vm.h, vm.n)
                    dy.inc(e.dma_start(out_d.ap()[256:256 + GD, :], h2_sb[:, :]), 16)
                    e.wait_ge(dy.h, dy.n)
                    e.wait_ge(gz.h, gz.n)
                fstk.close()
                return

            with nc.psum_tensor(f"g_sum_{icnt}", [1, 2 * T], F32) as ps_gsum:
                @block.vector
                def _(e):
                    e = Trk(e)
                    e.tensor_mul(hsq_sb[:], h2_sb[:], h2_sb[:])
                    vm.inc(e.last)
                v_hsq = vm.n

                @block.tensor
                def _(e):
                    e = Trk(e)
                    e.wait_ge(vm.h, v_hsq)
                    for (lo, hi) in HV:
                        e.matmul(ps_gsum[0:1, lo:hi], ones_cf[0:GD, :], h2_sb[:, lo:hi], start=True, stop=True)
                        e.matmul(ps_gsum[0:1, T + lo:T + hi], ones_cf[0:GD, :], hsq_sb[:, lo:hi], start=True, stop=True)
                    tm.inc(e.last)
                t_gs = tm.n

                @block.vector
                def _(e):
                    e = Trk(e)
                    e.wait_ge(tm.h, t_gs)
                    e.tensor_scalar_mul(row_m[:], ps_gsum[0:1, 0:T], 1.0 / GD)
                    e.tensor_scalar_mul(row_q[:], ps_gsum[0:1, T:2 * T], 1.0 / GD)
                    e.tensor_mul(row_s[:], row_m[:], row_m[:])
                    e.tensor_sub(row_q[:], row_q[:], row_s[:])
                    vm.inc(e.last)
                v_gv = vm.n

                @block.scalar
                def _(e):
                    e = Trk(e)
                    e.wait_ge(vm.h, v_gv)
                    e.activation(row_s[:], row_q[:], AF.Ln, bias=eps5_sb[0:1, 0:1])
                    e.activation(row_r[:], row_s[:], AF.Exp, scale=-0.5)
                    am.inc(e.last)

            with (nc.psum_tensor(f"g_meanB_{icnt}", [GD, T], F32) as ps_gmB,
                  nc.psum_tensor(f"g_rstdB_{icnt}", [GD, T], F32) as ps_grB):
                @block.tensor
                def _(e):
                    e = Trk(e)
                    e.wait_ge(am.h, am.n)
                    for (lo, hi) in HV:
                        e.matmul(ps_gmB[:, lo:hi], ones_rf[0:1, 0:GD], row_m[0:1, lo:hi], start=True, stop=True)
                        e.matmul(ps_grB[:, lo:hi], ones_rf[0:1, 0:GD], row_r[0:1, lo:hi], start=True, stop=True)
                    tm.inc(e.last)
                t_gb = tm.n

                @block.vector
                def _(e):
                    e = Trk(e)
                    e.wait_ge(tm.h, t_gb)
                    e.tensor_sub(h_sb[:], h2_sb[:], ps_gmB[:])
                    e.tensor_mul(h_sb[:], h_sb[:], ps_grB[:])
                    e.tensor_scalar(h_sb[:], h_sb[:], lng_sb[:, 0:1], lnb_sb[:, 0:1], ALU.mult, ALU.add)
                    e.tensor_relu(h_sb[:], h_sb[:])
                    vm.inc(e.last)
                v_hn = vm.n

            if STOP_AFTER == "gateln":
                @block.sync
                def _(e):
                    e = Trk(e)
                    e.wait_ge(vm.h, vm.n)
                    dy.inc(e.dma_start(out_d.ap()[0:GD, :], h_sb[:, :]), 16)
                    e.wait_ge(dy.h, dy.n)
                fstk.close()
                return
            with (nc.psum_tensor(f"g_ps_{icnt}", [128, E * NT], F32) as ps_g,
                  nc.psum_tensor(f"wt_ps_{icnt}", [E, 128], F32) as ps_wt,
                  nc.psum_tensor(f"cnt_ps_{icnt}", [1, 2 * E], F32) as ps_cnt,
                  nc.psum_tensor(f"alw_ps_{icnt}", [128, 2 * E], F32) as ps_alw):
                @block.tensor
                def _(e):
                    e = Trk(e)
                    e.wait_ge(vm.h, v_hn)
                    for t in range(NT):
                        e.matmul(ps_g[:, t * E:(t + 1) * E], h_sb[:, t * 128:(t + 1) * 128],
                                 gw2_sb[:, :], start=True, stop=True)
                    tm.inc(e.last)
                t_g = tm.n

                vm_g = {}
                for t in range(NT):
                    @block.vector
                    def _(e, t=t):
                        e = Trk(e)
                        e.wait_ge(tm.h, t_g)
                        eqv = eq_sb[:, t * 2 * E:(t + 1) * 2 * E]
                        e.tensor_add(g_sb[:], ps_g[:, t * E:(t + 1) * E], gb2b_sb[:])
                        e.reduce_max(m1_sb[:], g_sb[:], axis=AX.X)
                        e.tensor_single_scalar(eqv[:, 0:E], g_sb[:], m1_sb[:, 0:1], ALU.is_equal)
                        e.scalar_tensor_tensor(gm_sb[:], eqv[:, 0:E], -1e30, g_sb[:], ALU.mult, ALU.add)
                        e.reduce_max(m2_sb[:], gm_sb[:], axis=AX.X)
                        e.tensor_single_scalar(eqv[:, E:2 * E], gm_sb[:], m2_sb[:, 0:1], ALU.is_equal)
                        e.tensor_sub(m2_sb[:], m2_sb[:], m1_sb[:])
                        vm_g[t] = vm.inc(e.last)

                    @block.scalar
                    def _(e, t=t):
                        e = Trk(e)
                        e.wait_ge(vm.h, vm_g[t])
                        am.inc(e.activation(e21_sb[:], m2_sb[:], AF.Exp))
                    a_e = am.n

                    @block.vector
                    def _(e, t=t):
                        e = Trk(e)
                        e.wait_ge(am.h, a_e)
                        e.tensor_scalar_add(zden_sb[:], e21_sb[:], 1.0)
                        e.reciprocal(g12_sb[:, 2 * t:2 * t + 1], zden_sb[:])
                        e.tensor_mul(g12_sb[:, 2 * t + 1:2 * t + 2], e21_sb[:],
                                     g12_sb[:, 2 * t:2 * t + 1])
                        vm.inc(e.last)
                    v_g12 = vm.n

                    @block.tensor
                    def _(e, t=t):
                        e = Trk(e)
                        e.wait_ge(vm.h, v_g12)
                        e.matmul(ps_cnt[0:1, :], ones_cf[:, :],
                                 eq_sb[:, t * 2 * E:(t + 1) * 2 * E],
                                 start=(t == 0), stop=(t == NT - 1))
                        tm.inc(e.last)
                t_cnt = tm.n

                if STOP_AFTER == "gates":
                    @block.sync
                    def _(e):
                        e = Trk(e)
                        e.wait_ge(tm.h, tm.n)
                        e.wait_ge(vm.h, vm.n)
                        dy.inc(e.dma_start(out_d.ap()[0:128, 0:2 * E * NT], eq_sb[:, :]), 16)
                        dy.inc(e.dma_start(out_d.ap()[128:256, 0:2 * NT], g12_sb[:, :]), 16)
                        dy.inc(e.dma_start(out_d.ap()[256:384, 0:E], g_sb[:, :]), 16)
                        dy.inc(e.dma_start(out_d.ap()[384:512, 0:1], m1_sb[:, :], allow_slow_non_contiguous=True), 16)
                        dy.inc(e.dma_start(out_d.ap()[512:640, 0:1], m2_sb[:, :], allow_slow_non_contiguous=True), 16)
                        e.wait_ge(dy.h, dy.n)
                    fstk.close()
                    return
                @block.vector
                def _(e):
                    e = Trk(e)
                    e.wait_ge(tm.h, t_cnt)
                    vm.inc(e.tensor_copy(cnt_sb[:], ps_cnt[:]))
                v_cnt = vm.n

                @block.sync
                def _(e):
                    e = Trk(e)
                    e.wait_ge(vm.h, v_cnt)
                    dy.inc(e.dma_start(cnt_in[icnt].ap(), cnt_sb[:]), 16)
                d_ci = dy.n

                @block.gpsimd
                def _(e):
                    e = Trk(e)
                    e.wait_ge(dy.h, d_ci)
                    cc.inc(e.collective_compute(
                        "AllReduce", ALU.add, replica_groups=[list(range(NCORES))],
                        ins=[cnt_in[icnt].ap().opt()], outs=[cnt_out[icnt].ap().opt()]), 1)
                c_ar = cc.n

                @block.sync
                def _(e):
                    e = Trk(e)
                    e.wait_ge(cc.h, c_ar)
                    dy.inc(e.dma_start(cnt_sb[:], cnt_out[icnt].ap()), 16)
                d_co = dy.n

                @block.vector
                def _(e):
                    e = Trk(e)
                    e.wait_ge(dy.h, d_co)
                    e.tensor_single_scalar(alw_row[:], cnt_sb[:], c.cap, ALU.is_le)
                    vm.inc(e.last)
                v_alw = vm.n

                @block.tensor
                def _(e):
                    e = Trk(e)
                    e.wait_ge(vm.h, v_alw)
                    tm.inc(e.matmul(ps_alw[:, :], ones_rf[0:1, :], alw_row[0:1, :], start=True, stop=True))
                t_alw = tm.n

                @block.vector
                def _(e):
                    e = Trk(e)
                    e.wait_ge(tm.h, t_alw)
                    vm.inc(e.tensor_copy(alw_sb[:], ps_alw[:]))
                v_alwc = vm.n

                tm_wt = {}
                for t in range(NT):
                    @block.vector
                    def _(e, t=t):
                        e = Trk(e)
                        e.wait_ge(vm.h, v_alwc)
                        if t >= 1:
                            e.wait_ge(tm.h, tm_wt[t - 1])
                        eqv = eq_sb[:, t * 2 * E:(t + 1) * 2 * E]
                        e.tensor_scalar_mul(wz_sb[:, 0:E], eqv[:, 0:E], g12_sb[:, 2 * t:2 * t + 1])
                        e.tensor_scalar_mul(wz_sb[:, E:2 * E], eqv[:, E:2 * E], g12_sb[:, 2 * t + 1:2 * t + 2])
                        e.tensor_mul(wz_sb[:], wz_sb[:], alw_sb[:])
                        e.tensor_add(w_sb[:], wz_sb[:, 0:E], wz_sb[:, E:2 * E])
                        vm.inc(e.last)
                    v_w = vm.n

                    @block.tensor
                    def _(e, t=t):
                        e = Trk(e)
                        e.wait_ge(vm.h, v_w)
                        tm_wt[t] = tm.inc(e.transpose(ps_wt[:, :], w_sb[:, :], idf_sb[:, :]))

                    @block.vector
                    def _(e, t=t):
                        e = Trk(e)
                        e.wait_ge(tm.h, tm_wt[t])
                        vm.inc(e.tensor_copy(wt_sb[:, t * 128:(t + 1) * 128], ps_wt[:, :]))
                v_wt = vm.n

            if STOP_AFTER == "w":
                @block.sync
                def _(e):
                    e = Trk(e)
                    e.wait_ge(vm.h, vm.n)
                    dy.inc(e.dma_start(out_d.ap()[0:E, :], wt_sb[:, :]), 16)
                    dy.inc(e.dma_start(out_d.ap()[8:9, 0:2 * E], cnt_sb[:, :]), 16)
                    dy.inc(e.dma_start(out_d.ap()[9:10, 0:2 * E], alw_row[:, :]), 16)
                    dy.inc(e.dma_start(out_d.ap()[10:138, 0:E], w_sb[:, :]), 16)
                    dy.inc(e.dma_start(out_d.ap()[138:266, 0:2 * NT], g12_sb[:, :]), 16)
                    dy.inc(e.dma_start(out_d.ap()[266:394, 0:2 * E * NT], eq_sb[:, :]), 16)
                    dy.inc(e.dma_start(out_d.ap()[394:522, 0:2 * E], alw_sb[:, :]), 16)
                    e.wait_ge(dy.h, dy.n)
                fstk.close()
                return
            with nc.psum_tensor(f"wtexp_ps_{icnt}", [128, T], F32) as ps_we:
                @block.tensor
                def _(e):
                    e = Trk(e)
                    e.wait_ge(vm.h, v_wt)
                    for (lo, hi) in HV:
                        e.matmul(ps_we[:, lo:hi], map_sb[:, :], wt_sb[:, lo:hi], start=True, stop=True)
                    tm.inc(e.last)
                t_we = tm.n

                @block.vector
                def _(e):
                    e = Trk(e)
                    e.wait_ge(tm.h, t_we)
                    e.tensor_copy(wtexp_sb[:], ps_we[:])
                    e.tensor_mul(tmps_sb[:], ps_tmp[:], wtexp_sb[:])
                    vm.inc(e.last)
        v_tmps = vm.n

        with (nc.psum_tensor(f"main0_{icnt}", [128, T], F32) as pm0,
              nc.psum_tensor(f"main1_{icnt}", [128, T], F32) as pm1,
              nc.sbuf_tensor(f"wslf0_{icnt}", [128, NC * 128], BF16) as wsl0,
              nc.sbuf_tensor(f"wslf1_{icnt}", [128, NC * 128], BF16) as wsl1):
            wslab = [wsl0, wsl1]
            ps_main = [pm0, pm1]
            gw_oc = {}
            tm_oc = {}
            vm_oc = {}
            dxm_oc = {}
            dyr_oc = {}
            for oc in range(NC):
                par = oc % 2
                gws = gw[par]

                @block.gpsimd
                def _(e, oc=oc, par=par, gws=gws):
                    e = Trk(e)
                    if oc >= 2:
                        e.wait_ge(tm.h, tm_oc[oc - 2])
                    else:
                        e.wait_ge(tm.h, tm_phase0)
                    gw_oc[oc] = gws.inc(
                        e.dma_start(wslab[par][:], wff_d.ap()[oc, :, :]), 16)

                @block.sync
                def _(e, oc=oc, par=par):
                    e = Trk(e)
                    if oc >= 2:
                        e.wait_ge(vm.h, vm_oc[oc - 2])
                    else:
                        e.wait_ge(vm.h, vm_ch[NC - 1])
                    dxm_oc[oc] = dx[par].inc(
                        e.dma_start(xinb[par][:], src_d.ap()[oc * 128:(oc + 1) * 128, :]), 16)

                @block.tensor
                def _(e, oc=oc, par=par, gws=gws):
                    e = Trk(e)
                    e.wait_ge(gws.h, gw_oc[oc])
                    if oc == 0:
                        e.wait_ge(vm.h, v_tmps)
                    if oc >= 2:
                        e.wait_ge(vm.h, vm_oc[oc - 2])
                    for (lo, hi) in HV:
                        for ic in range(NC):
                            e.matmul(ps_main[par][:, lo:hi],
                                     wslab[par][:, ic * 128:(ic + 1) * 128],
                                     xf[:, ic * T + lo: ic * T + hi],
                                     start=(ic == 0), stop=False)
                        e.matmul(ps_main[par][:, lo:hi],
                                 bcat_sb[:, oc * 128:(oc + 1) * 128],
                                 tmps_sb[:, lo:hi], start=False, stop=True)
                    tm_oc[oc] = tm.inc(e.last)

                @block.vector
                def _(e, oc=oc, par=par):
                    e = Trk(e)
                    e.wait_ge(tm.h, tm_oc[oc])
                    e.wait_ge(dx[par].h, dxm_oc[oc])
                    if oc >= 2:
                        e.wait_ge(dy.h, dyr_oc[oc - 2])
                    e.scalar_tensor_tensor(rout[par][:], ps_main[par][:], 0.5,
                                           xinb[par][:], ALU.mult, ALU.add)
                    if dst_d is not None:
                        if oc == 0:
                            e.tensor_copy(acc_s[:], rout[par][:])
                            e.tensor_mul(acc_q[:], rout[par][:], rout[par][:])
                        else:
                            e.tensor_add(acc_s[:], acc_s[:], rout[par][:])
                            e.tensor_mul(sqs[:], rout[par][:], rout[par][:])
                            e.tensor_add(acc_q[:], acc_q[:], sqs[:])
                    vm_oc[oc] = vm.inc(e.last)

                @block.sync
                def _(e, oc=oc, par=par):
                    e = Trk(e)
                    e.wait_ge(vm.h, vm_oc[oc])
                    tgt = dst_d if dst_d is not None else out_d
                    dyr_oc[oc] = dy.inc(e.dma_start(tgt.ap()[oc * 128:(oc + 1) * 128, :], rout[par][:]), 16)
        fstk.close()

    def qkv_attn_phase():
        vm_phase0 = vm.n
        qstk = ExitStack()
        xf = qstk.enter_context(nc.sbuf_tensor("xf_qkv", [128, NC * T], BF16))
        with nc.psum_tensor("ln2_sum", [1, 2 * T], F32) as ps_sum:
            ln_rows(ps_sum, eps5_sb, D)
        with (nc.psum_tensor("ln2_meanB", [128, T], F32) as ps_meanB,
              nc.psum_tensor("ln2_rstdB", [128, T], F32) as ps_rstdB):
            bcast_rows(ps_meanB, ps_rstdB)
            t_bc = tm.n
            vm_ch = {}
            dx_ch = {}
            for ic in range(NC):
                par = ic % 2
                xi, dxs, tb = xinb[par], dx[par], tbuf[par]

                @block.sync
                def _(e, ic=ic, xi=xi, dxs=dxs):
                    e = Trk(e)
                    if ic >= 2:
                        e.wait_ge(vm.h, vm_ch[ic - 2])
                    else:
                        e.wait_ge(vm.h, vm_phase0)
                    dx_ch[ic] = dxs.inc(
                        e.dma_start(xi[:], res1_d.ap()[ic * 128:(ic + 1) * 128, :]), 16)

                @block.vector
                def _(e, ic=ic, xi=xi, dxs=dxs, tb=tb):
                    e = Trk(e)
                    e.wait_ge(dxs.h, dx_ch[ic])
                    if ic == 0:
                        e.wait_ge(tm.h, t_bc)
                    e.tensor_sub(tb[:], xi[:], ps_meanB[:])
                    e.tensor_mul(xf[:, ic * T:(ic + 1) * T], tb[:], ps_rstdB[:])
                    vm_ch[ic] = vm.inc(e.last)
        v_xf2 = vm.n

        def kq_loop(w_dram, bias_sbuf, gain_sbuf, dst, eps_ap, tag):
            tm_kq0 = tm.n
            with (nc.psum_tensor(f"kq0_{tag}", [128, T], F32) as p0,
                  nc.psum_tensor(f"kq1_{tag}", [128, T], F32) as p1,
                  nc.psum_tensor(f"kq_ss_{tag}", [1, T], F32) as ps_ss,
                  nc.psum_tensor(f"kq_rb_{tag}", [128, T], F32) as ps_rb,
                  nc.sbuf_tensor(f"wslk0_{tag}", [128, NC * 128], BF16) as wsl0,
                  nc.sbuf_tensor(f"wslk1_{tag}", [128, NC * 128], BF16) as wsl1):
                wslab = [wsl0, wsl1]
                ps_qk = [p0, p1]
                gw_oc, tm_oc, am_oc, vm_oc = {}, {}, {}, {}
                tm_rb, vm_fin, dy_fin = {}, {}, {}
                for oc in range(H):
                    par = oc % 2
                    gws = gw[par]

                    @block.gpsimd
                    def _(e, oc=oc, par=par, gws=gws):
                        e = Trk(e)
                        if oc >= 2:
                            e.wait_ge(tm.h, tm_oc[oc - 2])
                        else:
                            e.wait_ge(tm.h, tm_kq0)
                        gw_oc[oc] = gws.inc(e.dma_start(wslab[par][:], w_dram.ap()[oc, :, :]), 16)

                    @block.tensor
                    def _(e, oc=oc, par=par, gws=gws):
                        e = Trk(e)
                        e.wait_ge(gws.h, gw_oc[oc])
                        if oc == 0:
                            e.wait_ge(vm.h, v_xf2)
                        if oc >= 2:
                            e.wait_ge(am.h, am_oc[oc - 2])
                        for (lo, hi) in HV:
                            for ic in range(NC):
                                e.matmul(ps_qk[par][:, lo:hi],
                                         wslab[par][:, ic * 128:(ic + 1) * 128],
                                         xf[:, ic * T + lo: ic * T + hi],
                                         start=(ic == 0), stop=(ic == NC - 1))
                        tm_oc[oc] = tm.inc(e.last)

                    @block.scalar
                    def _(e, oc=oc, par=par):
                        e = Trk(e)
                        e.wait_ge(tm.h, tm_oc[oc])
                        if oc >= 2:
                            e.wait_ge(vm.h, vm_fin[oc - 2])
                        am_oc[oc] = am.inc(e.activation(
                            qraw[par][:], ps_qk[par][:], AF.Identity, bias=bias_sbuf[:, oc:oc + 1]))

                    @block.vector
                    def _(e, oc=oc, par=par):
                        e = Trk(e)
                        e.wait_ge(am.h, am_oc[oc])
                        if oc >= 1:
                            e.wait_ge(vm.h, vm_fin.get(oc - 1, 0))
                        e.tensor_mul(q2_sb[:], qraw[par][:], qraw[par][:])
                        vm_oc[oc] = vm.inc(e.last)

                    @block.tensor
                    def _(e, oc=oc):
                        e = Trk(e)
                        e.wait_ge(vm.h, vm_oc[oc])
                        for (lo, hi) in HV:
                            e.matmul(ps_ss[0:1, lo:hi], ones_cb[:, :], q2_sb[:, lo:hi],
                                     start=True, stop=True)
                        tm.inc(e.last)
                    t_ss = tm.n

                    @block.vector
                    def _(e, oc=oc):
                        e = Trk(e)
                        e.wait_ge(tm.h, t_ss)
                        e.tensor_scalar_mul(row_m[:], ps_ss[0:1, 0:T], 1.0 / 128.0)
                        vm.inc(e.last)
                    v_ms = vm.n

                    @block.scalar
                    def _(e, oc=oc):
                        e = Trk(e)
                        e.wait_ge(vm.h, v_ms)
                        e.activation(row_s[:], row_m[:], AF.Ln, bias=eps_ap[0:1, 0:1])
                        e.activation(row_r[:], row_s[:], AF.Exp, scale=-0.5)
                        am.inc(e.last)
                    a_rs = am.n

                    @block.tensor
                    def _(e, oc=oc):
                        e = Trk(e)
                        e.wait_ge(am.h, a_rs)
                        for (lo, hi) in HV:
                            e.matmul(ps_rb[:, lo:hi], ones_rf[0:1, :], row_r[0:1, lo:hi],
                                     start=True, stop=True)
                        tm_rb[oc] = tm.inc(e.last)

                    @block.vector
                    def _(e, oc=oc, par=par):
                        e = Trk(e)
                        e.wait_ge(tm.h, tm_rb[oc])
                        if oc >= 2:
                            e.wait_ge(dy.h, dy_fin.get(oc - 2, 0))
                        e.tensor_mul(qraw[par][:], qraw[par][:], ps_rb[:])
                        e.tensor_scalar_mul(kq_out[par][:], qraw[par][:], gain_sbuf[:, 0:1])
                        vm_fin[oc] = vm.inc(e.last)

                    @block.sync
                    def _(e, oc=oc, par=par):
                        e = Trk(e)
                        e.wait_ge(vm.h, vm_fin[oc])
                        if dst[0] == "dram":
                            g = oc // c.HPG
                            r0 = (oc % c.HPG) * 128
                            dy_fin[oc] = dy.inc(e.dma_start(dst[1][g].ap()[r0:r0 + 128, :], kq_out[par][:]), 16)
                        else:
                            dy_fin[oc] = dy.inc(e.dma_start(dst[1].ap()[oc * 128:(oc + 1) * 128, :], kq_out[par][:]), 16)
            return dict(vm_fin=vm_fin)

        kq_loop(wk_d, bk_sb, gk_sb, ("dram", kg_own), eps6_sb, "k")
        dy_after_k = dy.n

        with ExitStack() as vctx:
            wvslab = [vctx.enter_context(nc.sbuf_tensor(f"wvslab{i}", [128, NC * c.OGW], BF16))
                      for i in range(2)]
            ps_v = vctx.enter_context(nc.psum_tensor("ps_v", [128, NT * 512], F32))
            tm_v0 = tm.n
            gw_og, tm_og, am_ot, dy_v = {}, {}, {}, {}
            for og in range(c.OG):
                par = og % 2

                @block.gpsimd
                def _(e, og=og, par=par):
                    e = Trk(e)
                    if og >= 2:
                        e.wait_ge(tm.h, tm_og[og - 2])
                    else:
                        e.wait_ge(tm.h, tm_v0)
                    gw_og[og] = gw[par].inc(e.dma_start(wvslab[par][:], wv_d.ap()[og, :, :]), 16)

                @block.tensor
                def _(e, og=og, par=par):
                    e = Trk(e)
                    e.wait_ge(gw[par].h, gw_og[og])
                    if og == 0:
                        e.wait_ge(vm.h, v_xf2)
                        e.wait_ge(gm.h, gm_setup)
                    if og >= 1:
                        e.wait_ge(am.h, am_ot[(og - 1, NT - 1)])
                    for tt in range(NT):
                        for ic in range(NC):
                            e.matmul(ps_v[:, tt * 512:tt * 512 + c.OGW],
                                     xf[:, ic * T + tt * 128: ic * T + (tt + 1) * 128],
                                     wvslab[par][:, ic * c.OGW:(ic + 1) * c.OGW],
                                     start=(ic == 0), stop=False)
                        e.matmul(ps_v[:, tt * 512:tt * 512 + c.OGW],
                                 ones_rb[0:1, :], bv_sb[0:1, og * c.OGW:(og + 1) * c.OGW],
                                 start=False, stop=True)
                        tm_og[og] = tm.inc(e.last)

                for tt in range(NT):
                    @block.scalar
                    def _(e, og=og, tt=tt):
                        e = Trk(e)
                        e.wait_ge(tm.h, tm_og[og] - (NT - 1 - tt))
                        if (og, tt) != (0, 0):
                            prev = (og, tt - 1) if tt > 0 else (og - 1, NT - 1)
                            e.wait_ge(dy.h, dy_v.get(prev, 0))
                        am_ot[(og, tt)] = am.inc(e.activation(
                            vout_sb[tt % 2][:, 0:c.OGW], ps_v[:, tt * 512:tt * 512 + c.OGW], AF.Copy))

                    @block.sync
                    def _(e, og=og, tt=tt):
                        e = Trk(e)
                        e.wait_ge(am.h, am_ot[(og, tt)])
                        g = og // (c.OG // c.NG)
                        hg0 = (og % (c.OG // c.NG)) * (c.OGW // 128)
                        nj = c.OGW // 128
                        dst4 = vg_own[g].ap().rearrange("(a b p) d -> p a b d", b=NT, p=128)
                        src = vout_sb[tt % 2][:, 0:c.OGW].rearrange("p (j d) -> p j d", d=128)
                        dy_v[(og, tt)] = dy.inc(
                            e.dma_start(dst4[:, hg0:hg0 + nj, tt, :], src), 16)

            ogpg = c.OG // c.NG
            cc_after_g = {}

            @block.gpsimd
            def _(e):
                e = Trk(e)
                e.wait_ge(dy.h, dy_after_k)
                for g in range(c.NG):
                    e.wait_ge(dy.h, dy_v[(g * ogpg + ogpg - 1, NT - 1)])
                    cc.inc(e.collective_compute(
                        "AllGather", ALU.bypass, replica_groups=[list(range(NCORES))],
                        ins=[kg_own[g].ap().opt()], outs=[kg_gth[g].ap().opt()]), 1)
                    cc.inc(e.collective_compute(
                        "AllGather", ALU.bypass, replica_groups=[list(range(NCORES))],
                        ins=[vg_own[g].ap().opt()], outs=[vg_gth[g].ap().opt()]), 1)
                    cc_after_g[g] = cc.n

        kq_loop(wq_d, bq_sb, gq_sb, ("dram1", q_d), eps6_sb, "q")
        dy_qdone = dy.n
        qstk.close()

        if True:
            with ExitStack() as actx:
                kh = [actx.enter_context(nc.sbuf_tensor(f"kh{i}", [128, NCORES * T], BF16)) for i in range(2)]
                vh = [actx.enter_context(nc.sbuf_tensor(f"vh{i}", [128, NKC * 128], BF16)) for i in range(2)]
                qh = [actx.enter_context(nc.sbuf_tensor(f"qh{i}", [128, T], BF16)) for i in range(2)]
                pt_sb = actx.enter_context(nc.sbuf_tensor("pt_sb", [128, 3 * T], BF16))
                invl_sb = actx.enter_context(nc.sbuf_tensor("invl_sb", [1, T], F32))
                invlb_sb = actx.enter_context(nc.sbuf_tensor("invlb_sb", [128, T], F32))
                ao_sb = [actx.enter_context(nc.sbuf_tensor(f"ao_sb{i}", [128, T], BF16)) for i in range(2)]
                ps_s = [actx.enter_context(nc.psum_tensor(f"att_s{i}", [128, T], F32)) for i in range(2)]
                ps_o = actx.enter_context(nc.psum_tensor("att_o", [128, T], F32))
                ps_l = actx.enter_context(nc.psum_tensor("att_l", [128, T], F32))

                tm_hd = {}
                vm_hd = {}
                dxk = {}
                dxv = {}
                dy_ao = {}
                for h in range(H):
                    par = h % 2
                    g = h // c.HPG
                    hh = h % c.HPG
                    tm0 = tm.n
                    vm0 = vm.n
                    am0 = am.n

                    @block.sync
                    def _(e, h=h, par=par, g=g, hh=hh):
                        e = Trk(e)
                        e.wait_ge(cc.h, cc_after_g[g])
                        if h >= 2:
                            e.wait_ge(tm.h, tm_hd[h - 2])
                        src = kg_gth[g].ap().rearrange("(r a p) t -> p r a t", a=c.HPG, p=128)
                        dst = kh[par][:].rearrange("p (r t) -> p r t", t=T)
                        dxk[h] = dx[par].inc(e.dma_start(dst, src[:, :, hh, :]), 16)
                        e.wait_ge(dy.h, dy_qdone)
                        dxk[h] = dx[par].inc(
                            e.dma_start(qh[par][:], q_d.ap()[h * 128:(h + 1) * 128, :]), 16)
                        vsrc = vg_gth[g].ap().rearrange("(r a b p) d -> p r a b d",
                                                        a=c.HPG, b=NT, p=128)
                        vdst = vh[par][:].rearrange("p (r b d) -> p r b d", b=NT, d=128)
                        for r in range(NCORES):
                            dxv[h] = dx[par].inc(
                                e.dma_start(vdst[:, r, :, :], vsrc[:, r, hh, :, :]), 16)

                    @block.tensor
                    def _(e, h=h, par=par, tm0=tm0, vm0=vm0, am0=am0):
                        e = Trk(e)
                        e.wait_ge(dx[par].h, dxk[h])
                        e.wait_ge(am.h, am0)
                        if h >= 1:
                            e.wait_ge(vm.h, vm_hd[h - 1])
                        qv = qh[par]
                        for cix in range(2):
                            for (lo, hi) in HV:
                                e.matmul(ps_s[cix][:, lo:hi],
                                         kh[par][:, cix * 128:(cix + 1) * 128],
                                         qv[:, lo:hi], start=True, stop=True)
                            tm.inc(e.last)
                        e.wait_ge(vm.h, vm0 + 1)
                        tm.inc(e.transpose(ps_o[0:1, 0:128], pm_sb[:, :], idf_sb[:, :]))
                        e.wait_ge(vm.h, vm0 + 2)
                        tm.inc(e.matmul(ps_o[:, 0:1], ones_rf[0:1, :], brow_sb[0:1, 0:1],
                                        start=True, stop=True))
                        for cix in range(NKC):
                            e.wait_ge(am.h, am0 + cix + 1)
                            if cix == 0:
                                e.wait_ge(dx[par].h, dxv[h])
                            ptv = pt_sb[:, (cix % 3) * T:(cix % 3 + 1) * T]
                            for (lo, hi) in HV:
                                e.matmul(ps_o[:, lo:hi], vh[par][:, cix * 128:(cix + 1) * 128],
                                         ptv[:, lo:hi], start=(cix == 0), stop=(cix == NKC - 1))
                            for (lo, hi) in HV:
                                e.matmul(ps_l[0:1, lo:hi], ones_cb[:, :], ptv[:, lo:hi],
                                         start=(cix == 0), stop=(cix == NKC - 1))
                            if cix + 2 < NKC:
                                for (lo, hi) in HV:
                                    e.matmul(ps_s[cix % 2][:, lo:hi],
                                             kh[par][:, (cix + 2) * 128:(cix + 3) * 128],
                                             qv[:, lo:hi], start=True, stop=True)
                            tm.inc(e.last)
                        e.wait_ge(vm.h, vm0 + 4)
                        for (lo, hi) in HV:
                            e.matmul(ps_s[0][:, lo:hi], ones_rf[0:1, :], invl_sb[0:1, lo:hi],
                                     start=True, stop=True)
                        tm_hd[h] = tm.inc(e.last)
                        assert tm_hd[h] == tm0 + 5 + NKC

                    @block.scalar
                    def _(e, h=h, tm0=tm0, vm0=vm0, am0=am0):
                        e = Trk(e)
                        for cix in range(NKC):
                            if cix < 2:
                                e.wait_ge(tm.h, tm0 + cix + 1)
                            else:
                                e.wait_ge(tm.h, tm0 + cix + 3)
                            if cix == 0:
                                e.wait_ge(vm.h, vm0 + 3)
                            ptv = pt_sb[:, (cix % 3) * T:(cix % 3 + 1) * T]
                            am.inc(e.activation(ptv[:], ps_s[cix % 2][:], AF.Exp,
                                                bias=bias_sb[:, 0:1], scale=c.scale))

                    @block.vector
                    def _(e, h=h, par=par, tm0=tm0, vm0=vm0):
                        e = Trk(e)
                        e.wait_ge(tm.h, tm0 + 1)
                        e.reduce_max(pm_sb[:], ps_s[0][:], axis=AX.X)
                        vm.inc(e.last)
                        e.wait_ge(tm.h, tm0 + 3)
                        e.reduce_max(mrow_sb[:], ps_o[0:1, 0:128], axis=AX.X)
                        e.tensor_scalar_mul(brow_sb[:], mrow_sb[:], 0.25)
                        e.tensor_scalar_mul(b2row_sb[:], mrow_sb[:], -0.25)
                        e.tensor_max(brow_sb[:], brow_sb[:], b2row_sb[:])
                        e.tensor_add(brow_sb[:], brow_sb[:], mrow_sb[:])
                        e.tensor_scalar_add(brow_sb[:], brow_sb[:], 10.0)
                        e.tensor_scalar_mul(brow_sb[:], brow_sb[:], -c.scale)
                        vm.inc(e.last)
                        e.wait_ge(tm.h, tm0 + 4)
                        e.tensor_copy(bias_sb[:], ps_o[:, 0:1])
                        vm.inc(e.last)
                        e.wait_ge(tm.h, tm0 + 4 + NKC)
                        e.reciprocal(invl_sb[:], ps_l[0:1, :])
                        vm.inc(e.last)
                        e.wait_ge(tm.h, tm0 + 5 + NKC)
                        if h >= 2:
                            e.wait_ge(dy.h, dy_ao[h - 2])
                        e.tensor_copy(invlb_sb[:], ps_s[0][:])
                        e.tensor_mul(ao_sb[par][:], ps_o[:], invlb_sb[:])
                        vm_hd[h] = vm.inc(e.last)
                        assert vm_hd[h] == vm0 + 5

                    @block.sync
                    def _(e, h=h, par=par):
                        e = Trk(e)
                        e.wait_ge(vm.h, vm_hd[h])
                        dy_ao[h] = dy.inc(e.dma_start(ao_d.ap()[h * 128:(h + 1) * 128, :], ao_sb[par][:]), 16)
                dy_after_ao = dy.n

        with ExitStack() as octx:
            aob = octx.enter_context(nc.sbuf_tensor("aob", [128, H * T], BF16))
            p0 = octx.enter_context(nc.psum_tensor("op0", [128, T], F32))
            p1 = octx.enter_context(nc.psum_tensor("op1", [128, T], F32))
            wslab = [octx.enter_context(nc.sbuf_tensor(f"wslo{i}", [128, NC * 128], BF16)) for i in range(2)]
            ps_op = [p0, p1]

            @block.sync
            def _(e):
                e = Trk(e)
                e.wait_ge(dy.h, dy_after_ao)
                for af in range(H):
                    dy.inc(e.dma_start(aob[:, af * T:(af + 1) * T],
                                       ao_d.ap()[af * 128:(af + 1) * 128, :]), 16)
            dy_aob = dy.n

            gw_oc, tm_oc, am_oc, vm_oc, dxo_oc, dyr_oc = {}, {}, {}, {}, {}, {}
            tm_op0 = tm.n
            vm_op0 = vm.n
            for oc in range(NC):
                par = oc % 2

                @block.gpsimd
                def _(e, oc=oc, par=par):
                    e = Trk(e)
                    if oc >= 2:
                        e.wait_ge(tm.h, tm_oc[oc - 2])
                    else:
                        e.wait_ge(tm.h, tm_op0)
                    gw_oc[oc] = gw[par].inc(e.dma_start(wslab[par][:], wo_d.ap()[oc, :, :]), 16)

                @block.sync
                def _(e, oc=oc, par=par):
                    e = Trk(e)
                    if oc >= 2:
                        e.wait_ge(vm.h, vm_oc[oc - 2])
                    else:
                        e.wait_ge(vm.h, vm_op0)
                    dxo_oc[oc] = dx[par].inc(
                        e.dma_start(xinb[par][:], res1_d.ap()[oc * 128:(oc + 1) * 128, :]), 16)

                @block.tensor
                def _(e, oc=oc, par=par):
                    e = Trk(e)
                    e.wait_ge(gw[par].h, gw_oc[oc])
                    if oc == 0:
                        e.wait_ge(dy.h, dy_aob)
                    if oc >= 2:
                        e.wait_ge(am.h, am_oc[oc - 2])
                    for (lo, hi) in HV:
                        for af in range(H):
                            e.matmul(ps_op[par][:, lo:hi],
                                     wslab[par][:, af * 128:(af + 1) * 128],
                                     aob[:, af * T + lo: af * T + hi],
                                     start=(af == 0), stop=(af == H - 1))
                    tm_oc[oc] = tm.inc(e.last)

                @block.scalar
                def _(e, oc=oc, par=par):
                    e = Trk(e)
                    e.wait_ge(tm.h, tm_oc[oc])
                    if oc >= 2:
                        e.wait_ge(vm.h, vm_oc[oc - 2])
                    am_oc[oc] = am.inc(e.activation(
                        qraw[par][:], ps_op[par][:], AF.Identity, bias=bo_sb[:, oc:oc + 1]))

                @block.vector
                def _(e, oc=oc, par=par):
                    e = Trk(e)
                    e.wait_ge(am.h, am_oc[oc])
                    e.wait_ge(dx[par].h, dxo_oc[oc])
                    if oc >= 2:
                        e.wait_ge(dy.h, dyr_oc[oc - 2])
                    e.tensor_add(rout[par][:], qraw[par][:], xinb[par][:])
                    if oc == 0:
                        e.tensor_copy(acc_s[:], rout[par][:])
                        e.tensor_mul(acc_q[:], rout[par][:], rout[par][:])
                    else:
                        e.tensor_add(acc_s[:], acc_s[:], rout[par][:])
                        e.tensor_mul(sqs[:], rout[par][:], rout[par][:])
                        e.tensor_add(acc_q[:], acc_q[:], sqs[:])
                    vm_oc[oc] = vm.inc(e.last)

                @block.sync
                def _(e, oc=oc, par=par):
                    e = Trk(e)
                    e.wait_ge(vm.h, vm_oc[oc])
                    dyr_oc[oc] = dy.inc(e.dma_start(res2_d.ap()[oc * 128:(oc + 1) * 128, :], rout[par][:]), 16)

    if phases == 1:
        ffn_phase(x_d, None, 0)
    else:
        ffn_phase(x_d, res1_d, 0)
    if phases >= 2 and STOP_AFTER is None:
        qkv_attn_phase()
    if phases == 2 and STOP_AFTER is None:
        @block.sync
        def _(e):
            e = Trk(e)
            e.wait_ge(dy.h, dy.n)
            for ic in range(NC):
                dy.inc(e.dma_start(xinb[0][:], res2_d.ap()[ic * 128:(ic + 1) * 128, :]), 16)
                e.wait_ge(dy.h, dy.n)
                dy.inc(e.dma_start(out_d.ap()[ic * 128:(ic + 1) * 128, :], xinb[0][:]), 16)
                e.wait_ge(dy.h, dy.n)
    if phases >= 3 and STOP_AFTER is None:
        ffn_phase(res2_d, None, 1)

    @block.sync
    def _(e):
        e = Trk(e)
        e.wait_ge(dy.h, dy.n)
        for s_ in (dx[0], dx[1]):
            e.wait_ge(s_.h, s_.n)

    ctx.close()
    return nc


def _block_oc(w):
    Din, Dout = w.shape
    return np.ascontiguousarray(
        w.reshape(Din // 128, 128, Dout // 128, 128).transpose(2, 1, 0, 3)
         .reshape(Dout // 128, 128, Din))


def _block_og(w, og):
    Din, Dout = w.shape
    ow = Dout // og
    return np.ascontiguousarray(
        w.reshape(Din // 128, 128, og, ow).transpose(2, 1, 0, 3)
         .reshape(og, 128, (Din // 128) * ow))


def prep_inputs(cfg, inp):
    c = cfg
    f = np.float32
    wq = np.asarray(inp["L0_attn_Wq"], f)
    wk = np.asarray(inp["L0_attn_Wk"], f)
    wv = np.asarray(inp["L0_attn_Wv"], f)
    wo = np.asarray(inp["L0_attn_Wo"], f)
    wff = np.asarray(inp["L0_ffn_weight"], f).T
    la = np.asarray(inp["L0_ffn_lora_A"], f)
    lb = np.asarray(inp["L0_ffn_lora_B"], f)
    gw1 = np.asarray(inp["L0_ffn_gw1"], f)
    gw2 = np.asarray(inp["L0_ffn_gw2"], f)
    a_cat = la.transpose(1, 0, 2).reshape(c.D, c.RC)
    b_cat = lb.reshape(c.RC, c.D)
    mp = np.zeros((c.E, 128), f)
    for e in range(c.E):
        mp[e, e * c.R:(e + 1) * c.R] = 1.0 / c.R
    d = {
        "wq_b": _block_oc(wq), "wk_b": _block_oc(wk), "wo_b": _block_oc(wo),
        "wff_b": _block_oc(wff), "wv_b": _block_og(wv, c.OG),
        "a_b": np.ascontiguousarray(
            a_cat.reshape(c.NC, 128, c.RC).transpose(1, 0, 2).reshape(128, c.NC * c.RC)),
        "b_cat": np.ascontiguousarray(b_cat.reshape(128, c.D)) if c.RC == 128 else None,
        "gw1_b": np.ascontiguousarray(
            gw1.T.reshape(c.NC, 128, c.GD).transpose(1, 0, 2).reshape(128, c.NC * c.GD)),
        "gw2_t": np.ascontiguousarray(gw2.T),
        "bq_c": np.ascontiguousarray(np.asarray(inp["L0_attn_bq"], f).reshape(c.NC, 128).T),
        "bk_c": np.ascontiguousarray(np.asarray(inp["L0_attn_bk"], f).reshape(c.NC, 128).T),
        "bo_c": np.ascontiguousarray(np.asarray(inp["L0_attn_bo"], f).reshape(c.NC, 128).T),
        "bv_r": np.asarray(inp["L0_attn_bv"], f).reshape(1, c.D).astype(ml_dtypes.bfloat16),
        "gb1_c": np.asarray(inp["L0_ffn_gb1"], f).reshape(c.GD, 1),
        "gb2_r": np.asarray(inp["L0_ffn_gb2"], f).reshape(1, c.E),
        "lng_c": np.asarray(inp["L0_ffn_ln_gamma"], f).reshape(c.GD, 1),
        "lnb_c": np.asarray(inp["L0_ffn_ln_beta"], f).reshape(c.GD, 1),
        "gq_c": np.asarray(inp["L0_attn_gq"], f).reshape(128, 1),
        "gk_c": np.asarray(inp["L0_attn_gk"], f).reshape(128, 1),
        "map_alpha": mp,
        "ident_f32": np.eye(128, dtype=f),
    }
    assert d["b_cat"] is not None, "RC must be 128"
    return d


CFG_FULL = Cfg()
_CACHE = {}


def kernel(**inputs):
    cfg = CFG_FULL
    x = np.asarray(inputs["L0_attn_hidden_states"], np.float32)[0]
    shared = prep_inputs(cfg, inputs)
    in_maps = []
    for cix in range(NCORES):
        m = dict(shared)
        m["x_fm"] = np.ascontiguousarray(x[cix * cfg.T:(cix + 1) * cfg.T, :].T)
        in_maps.append(m)
    key = "full"
    if key not in _CACHE:
        _CACHE[key] = build(cfg)
    nc = _CACHE[key]
    res = run_bass_kernel_spmd(nc, in_maps, core_ids=list(range(NCORES)))
    out = np.empty((1, NCORES * cfg.T, cfg.D), np.float32)
    for cix in range(NCORES):
        out[0, cix * cfg.T:(cix + 1) * cfg.T, :] = res.results[cix]["out_fm"].T
    return out


# version 09
# speedup vs baseline: 1.0445x; 1.0148x over previous
import math
from contextlib import ExitStack

import numpy as np
import ml_dtypes

import concourse.bass as bass
import concourse.mybir as mybir
from concourse.bass_utils import run_bass_kernel_spmd

F32 = mybir.dt.float32
BF16 = mybir.dt.bfloat16
AF = mybir.ActivationFunctionType
ALU = mybir.AluOpType
AX = mybir.AxisListType

NCORES = 8
STOP_AFTER = None


class Cfg:
    def __init__(self, D=4096, T=1024, H=32, E=8, R=16, GD=32, capfrac=3.0):
        self.D = D
        self.T = T
        self.H = H
        self.HD = 128
        self.E = E
        self.R = R
        self.RC = E * R
        self.GD = GD
        self.cap = float(int(capfrac * (NCORES * T) / E))
        self.NC = D // 128
        self.NT = T // 128
        self.NK = NCORES * T // 128
        self.NG = 4
        self.HPG = H // self.NG
        self.OG = max(8, D // 256)
        self.OGW = D // self.OG
        assert D % 128 == 0 and T % 512 == 0 and H % self.NG == 0 and H == D // 128
        assert self.OGW % 128 == 0 and (self.OGW // 128) * self.OG * 128 == D
        self.scale = 1.0 / math.sqrt(self.HD)


class Sem:
    def __init__(self, ctx, nc, name):
        self.h = ctx.enter_context(nc.semaphore(name))
        self.n = 0

    def inc(self, instr, k=1):
        instr.then_inc(self.h, k)
        self.n += k
        return self.n


_TRK_SKIP = {"nop", "engine_nop", "wait_ge", "wait_op", "drain", "sem_inc"}
_TRK_DRAIN = {"tensor_scalar", "tensor_single_scalar", "tensor_scalar_add",
              "tensor_scalar_sub", "tensor_scalar_mul", "tensor_scalar_max",
              "tensor_scalar_min", "scalar_tensor_tensor", "reciprocal"}


def _is_ap(x):
    return hasattr(x, "tensor") or hasattr(x, "ap")


class Trk:

    def __init__(self, e):
        object.__setattr__(self, "_e", e)
        object.__setattr__(self, "last", None)

    def __getattr__(self, n):
        f = getattr(self._e, n)
        if callable(f) and not n.startswith("_") and n not in _TRK_SKIP:
            is_dve = type(self._e).__name__ == "BassVectorEngine"
            def g(*a, **k):
                if is_dve and n != "dma_start" and self.last is not None:
                    self._e.drain()
                r = f(*a, **k)
                if hasattr(r, "then_inc"):
                    object.__setattr__(self, "last", r)
                return r
            return g
        return f


def build(cfg: Cfg, taps=(), detect_races=False, phases=3):
    c = cfg
    nc = bass.Bass("TRN2", target_bir_lowering=False, detect_race_conditions=detect_races)
    T, D, NC, NT, H, E, GD, RC = c.T, c.D, c.NC, c.NT, c.H, c.E, c.GD, c.RC
    HV = [(i * 512, (i + 1) * 512) for i in range(T // 512)]
    NKC = c.NK

    def xin(name, shape, dtype=F32):
        return nc.dram_tensor(name, list(shape), dtype, kind="ExternalInput")

    x_d = xin("x_fm", [D, T])
    wq_d = xin("wq_b", [NC, 128, D])
    wk_d = xin("wk_b", [NC, 128, D])
    wo_d = xin("wo_b", [NC, 128, D])
    wff_d = xin("wff_b", [NC, 128, D])
    wv_d = xin("wv_b", [c.OG, 128, NC * c.OGW])
    a_d = xin("a_b", [128, NC * RC])
    bcat_d = xin("b_cat", [128, D])
    gw1_d = xin("gw1_b", [128, NC * GD])
    gw2_d = xin("gw2_t", [GD, E])
    bq_d = xin("bq_c", [128, NC])
    bk_d = xin("bk_c", [128, NC])
    bo_d = xin("bo_c", [128, NC])
    bv_d = xin("bv_r", [1, D], BF16)
    gb1_d = xin("gb1_c", [GD, 1])
    gb2_d = xin("gb2_r", [1, E])
    lng_d = xin("lng_c", [GD, 1])
    lnb_d = xin("lnb_c", [GD, 1])
    gq_d = xin("gq_c", [128, 1])
    gk_d = xin("gk_c", [128, 1])
    map_d = xin("map_alpha", [E, 128])
    idf_d = xin("ident_f32", [128, 128])

    out_d = nc.dram_tensor("out_fm", [D, T], F32, kind="ExternalOutput")
    tap_d = {}
    for tname, tshape, tdt in taps:
        tap_d[tname] = nc.dram_tensor(tname, list(tshape), tdt, kind="ExternalOutput")

    res1_d = nc.dram_tensor("res1", [D, T], F32, kind="Internal")
    res2_d = nc.dram_tensor("res2", [D, T], F32, kind="Internal")
    ao_d = nc.dram_tensor("ao", [H * 128, T], BF16, kind="Internal")
    q_d = nc.dram_tensor("q_sp", [H * 128, T], BF16, kind="Internal")
    kg_own = [nc.dram_tensor(f"kg_own{g}", [c.HPG * 128, T], BF16, kind="Internal") for g in range(c.NG)]
    vg_own = [nc.dram_tensor(f"vg_own{g}", [c.HPG * NT * 128, 128], BF16, kind="Internal") for g in range(c.NG)]
    kg_gth = [nc.dram_tensor(f"kg_gth{g}", [NCORES * c.HPG * 128, T], BF16, kind="Internal") for g in range(c.NG)]
    vg_gth = [nc.dram_tensor(f"vg_gth{g}", [NCORES * c.HPG * NT * 128, 128], BF16, kind="Internal") for g in range(c.NG)]
    cnt_in = [nc.dram_tensor(f"cnt_in{i}", [1, 2 * E], F32, kind="Internal") for i in range(2)]
    cnt_out = [nc.dram_tensor(f"cnt_out{i}", [1, 2 * E], F32, kind="Internal") for i in range(2)]

    ctx = ExitStack()
    sb = lambda name, shape, dt: ctx.enter_context(nc.sbuf_tensor(name, list(shape), dt))

    acc_s = sb("acc_s", [128, T], F32)
    acc_q = sb("acc_q", [128, T], F32)
    tbuf = [sb(f"tchunk{i}", [128, T], F32) for i in range(2)]
    xinb = [sb(f"xin{i}", [128, T], F32) for i in range(2)]
    rout = [sb(f"rout{i}", [128, T], F32) for i in range(2)]
    sqs = sb("sqs", [128, T], F32)
    a_sb = sb("a_sb", [128, NC * RC], BF16)
    bcat_sb = sb("bcat_sb", [128, D], BF16)
    gw1_sb = sb("gw1_sb", [128, NC * GD], F32)
    gw2_sb = sb("gw2_sb", [GD, E], F32)
    bq_sb = sb("bq_sb", [128, NC], F32)
    bk_sb = sb("bk_sb", [128, NC], F32)
    bo_sb = sb("bo_sb", [128, NC], F32)
    bv_sb = sb("bv_sb", [1, D], BF16)
    gb1_sb = sb("gb1_sb", [GD, 1], F32)
    gb2row = sb("gb2row", [1, E], F32)
    gb2b_sb = sb("gb2b_sb", [128, E], F32)
    lng_sb = sb("lng_sb", [GD, 1], F32)
    lnb_sb = sb("lnb_sb", [GD, 1], F32)
    gq_sb = sb("gq_sb", [128, 1], F32)
    gk_sb = sb("gk_sb", [128, 1], F32)
    map_sb = sb("map_sb", [E, 128], F32)
    idf_sb = sb("idf_sb", [128, 128], F32)
    eps5_sb = sb("eps5_sb", [1, 1], F32)
    eps6_sb = sb("eps6_sb", [1, 1], F32)
    ones_cb = sb("ones_cb", [128, 1], BF16)
    ones_rb = sb("ones_rb", [1, 128], BF16)
    ones_cf = sb("ones_cf", [128, 1], F32)
    ones_rf = sb("ones_rf", [1, 128], F32)
    row_s = sb("row_s", [1, T], F32)
    row_q = sb("row_q", [1, T], F32)
    row_m = sb("row_m", [1, T], F32)
    row_r = sb("row_r", [1, T], F32)
    g_sb = sb("g_sb", [128, E], F32)
    gm_sb = sb("gm_sb", [128, E], F32)
    eq_sb = sb("eq_sb", [128, 2 * E * NT], F32)
    m1_sb = sb("m1_sb", [128, 1], F32)
    m2_sb = sb("m2_sb", [128, 1], F32)
    e21_sb = sb("e21_sb", [128, 1], F32)
    zden_sb = sb("zden_sb", [128, 1], F32)
    g12_sb = sb("g12_sb", [128, 2 * NT], F32)
    wz_sb = sb("wz_sb", [128, 2 * E], F32)
    w_sb = sb("w_sb", [128, E], F32)
    cnt_sb = sb("cnt_sb", [1, 2 * E], F32)
    alw_row = sb("alw_row", [1, 2 * E], F32)
    alw_sb = sb("alw_sb", [128, 2 * E], F32)
    pm_sb = sb("pm_sb", [128, 1], F32)
    mrow_sb = sb("mrow_sb", [1, 1], F32)
    brow_sb = sb("brow_sb", [1, 1], F32)
    b2row_sb = sb("b2row_sb", [1, 1], F32)
    bias_sb = sb("bias_sb", [128, 1], F32)
    qraw = [sb(f"qraw{i}", [128, T], F32) for i in range(2)]
    q2_sb = sb("q2_sb", [128, T], BF16)
    kq_out = [sb(f"kq_out{i}", [128, T], BF16) for i in range(2)]
    vout_sb = [sb(f"vout{i}", [128, c.OGW], BF16) for i in range(2)]

    mks = lambda name: Sem(ctx, nc, name)
    dx = [mks("dx0"), mks("dx1")]
    dy = mks("dy")
    gw = [mks("gw0"), mks("gw1")]
    gz = mks("gz")
    cc = mks("cc")
    tm = mks("tm")
    vm = mks("vm")
    am = mks("am")
    gm = mks("gm")

    block = ctx.enter_context(nc.Block())

    @block.sync
    def _(e):
        e = Trk(e)
        for t, d in [(bq_sb, bq_d), (bk_sb, bk_d), (bo_sb, bo_d),
                     (gb1_sb, gb1_d), (lng_sb, lng_d), (lnb_sb, lnb_d),
                     (gq_sb, gq_d), (gk_sb, gk_d), (map_sb, map_d),
                     (idf_sb, idf_d), (gw2_sb, gw2_d), (gw1_sb, gw1_d),
                     (gb2row, gb2_d)]:
            dy.n = dy.inc(e.dma_start(t[:], d.ap()), 16)

    @block.gpsimd
    def _(e):
        e = Trk(e)
        gz.inc(e.dma_start(a_sb[:], a_d.ap()), 16)
        gz.inc(e.dma_start(bcat_sb[:], bcat_d.ap()), 16)
        gz.inc(e.dma_start(bv_sb[:], bv_d.ap()), 16)
        gm.inc(e.memset(eps5_sb[:], 1e-5))
        gm.inc(e.memset(eps6_sb[:], 1e-6))
        gm.inc(e.memset(ones_cb[:], 1.0))
        gm.inc(e.memset(ones_rb[:], 1.0))
        gm.inc(e.memset(ones_cf[:], 1.0))
        gm.inc(e.memset(ones_rf[:], 1.0))
    dy_setup, gz_setup, gm_setup = dy.n, gz.n, gm.n

    @block.scalar
    def _(e):
        e = Trk(e)
        e.wait_ge(gm.h, gm_setup)

    with nc.psum_tensor("ps_gb2", [128, E], F32) as ps_gb2:
        @block.tensor
        def _(e):
            e = Trk(e)
            e.wait_ge(dy.h, dy_setup)
            e.wait_ge(gm.h, gm_setup)
            tm.inc(e.matmul(ps_gb2[:, :], ones_rf[0:1, :], gb2row[0:1, :],
                            start=True, stop=True))
        tm_gb2 = tm.n

        @block.vector
        def _(e):
            e = Trk(e)
            e.wait_ge(tm.h, tm_gb2)
            vm.inc(e.tensor_copy(gb2b_sb[:], ps_gb2[:]))

    if STOP_AFTER == "setup":
        @block.sync
        def _(e):
            e = Trk(e)
            e.wait_ge(dy.h, dy.n)
        ctx.close()
        return nc
    vm_st = {}
    dx_st = {}
    for ic in range(NC):
        par = ic % 2
        xi, dxs = xinb[par], dx[par]

        @block.sync
        def _(e, ic=ic, xi=xi, dxs=dxs):
            e = Trk(e)
            if ic >= 2:
                e.wait_ge(vm.h, vm_st[ic - 2])
            dx_st[ic] = dxs.inc(e.dma_start(xi[:], x_d.ap()[ic * 128:(ic + 1) * 128, :]), 16)

        @block.vector
        def _(e, ic=ic, xi=xi, dxs=dxs):
            e = Trk(e)
            e.wait_ge(dxs.h, dx_st[ic])
            if ic == 0:
                e.tensor_copy(acc_s[:], xi[:])
                e.tensor_mul(acc_q[:], xi[:], xi[:])
            else:
                e.tensor_add(acc_s[:], acc_s[:], xi[:])
                e.tensor_mul(sqs[:], xi[:], xi[:])
                e.tensor_add(acc_q[:], acc_q[:], sqs[:])
            vm_st[ic] = vm.inc(e.last)

    def ln_rows(ps_sum, eps_ap, nfeat):
        @block.tensor
        def _(e):
            e = Trk(e)
            e.wait_ge(vm.h, vm.n)
            for (lo, hi) in HV:
                e.matmul(ps_sum[0:1, lo:hi], ones_cf[:], acc_s[:, lo:hi], start=True, stop=True)
                e.matmul(ps_sum[0:1, T + lo:T + hi], ones_cf[:], acc_q[:, lo:hi], start=True, stop=True)
            tm.inc(e.last)
        t0 = tm.n
        if STOP_AFTER == "rowsA":
            return

        @block.vector
        def _(e):
            e = Trk(e)
            e.wait_ge(tm.h, t0)
            e.tensor_scalar_mul(row_m[:], ps_sum[0:1, 0:T], 1.0 / nfeat)
            e.tensor_scalar_mul(row_q[:], ps_sum[0:1, T:2 * T], 1.0 / nfeat)
            e.tensor_mul(row_s[:], row_m[:], row_m[:])
            e.tensor_sub(row_q[:], row_q[:], row_s[:])
            vm.inc(e.last)
        v0 = vm.n
        if STOP_AFTER == "rowsB":
            return

        @block.scalar
        def _(e):
            e = Trk(e)
            e.wait_ge(vm.h, v0)
            e.activation(row_s[:], row_q[:], AF.Ln, bias=eps_ap[0:1, 0:1])
            e.activation(row_r[:], row_s[:], AF.Exp, scale=-0.5)
            am.inc(e.last)

    def bcast_rows(ps_meanB, ps_rstdB):
        @block.tensor
        def _(e):
            e = Trk(e)
            e.wait_ge(am.h, am.n)
            for (lo, hi) in HV:
                e.matmul(ps_meanB[:, lo:hi], ones_rf[0:1, :], row_m[0:1, lo:hi], start=True, stop=True)
                e.matmul(ps_rstdB[:, lo:hi], ones_rf[0:1, :], row_r[0:1, lo:hi], start=True, stop=True)
            tm.inc(e.last)

    def ffn_phase(src_d, dst_d, icnt):
        vm_phase0 = vm.n
        tm_phase0 = tm.n
        fstk = ExitStack()
        xf = fstk.enter_context(nc.sbuf_tensor(f"xf_f{icnt}", [128, NC * T], BF16))
        h_sb = fstk.enter_context(nc.sbuf_tensor(f"h_sb_{icnt}", [GD, T], F32))
        h2_sb = fstk.enter_context(nc.sbuf_tensor(f"h2_sb_{icnt}", [GD, T], F32))
        hsq_sb = fstk.enter_context(nc.sbuf_tensor(f"hsq_sb_{icnt}", [GD, T], F32))
        wt_sb = fstk.enter_context(nc.sbuf_tensor(f"wt_sb_{icnt}", [E, T], F32))
        wtexp_sb = fstk.enter_context(nc.sbuf_tensor(f"wtexp_sb_{icnt}", [128, T], F32))
        tmps_sb = fstk.enter_context(nc.sbuf_tensor(f"tmps_sb_{icnt}", [128, T], BF16))
        if STOP_AFTER == "stats":
            fstk.close()
            return
        with nc.psum_tensor(f"ffn_sum_{icnt}", [1, 2 * T], F32) as ps_sum:
            ln_rows(ps_sum, eps5_sb, D)
        if STOP_AFTER == "rowsdbg":
            @block.sync
            def _(e):
                e = Trk(e)
                e.wait_ge(am.h, am.n)
                dy.inc(e.dma_start(out_d.ap()[0:1, :], row_m[0:1, :]), 16)
                dy.inc(e.dma_start(out_d.ap()[1:2, :], row_r[0:1, :]), 16)
                dy.inc(e.dma_start(out_d.ap()[2:3, :], row_q[0:1, :]), 16)
                dy.inc(e.dma_start(out_d.ap()[4:132, :], acc_s[:, :]), 16)
                dy.inc(e.dma_start(out_d.ap()[132:260, :], acc_q[:, :]), 16)
                e.wait_ge(dy.h, dy.n)
        if STOP_AFTER in ("rows", "rowsA", "rowsB", "rowsdbg"):
            fstk.close()
            return

        with nc.psum_tensor(f"ffn_tmp_{icnt}", [128, T], F32) as ps_tmp:
            with (nc.psum_tensor(f"ffn_meanB_{icnt}", [128, T], F32) as ps_meanB,
                  nc.psum_tensor(f"ffn_rstdB_{icnt}", [128, T], F32) as ps_rstdB,
                  nc.psum_tensor(f"ffn_h_{icnt}", [GD, T], F32) as ps_h):
                bcast_rows(ps_meanB, ps_rstdB)
                t_bc = tm.n

                vm_ch = {}
                tm_ch = {}
                dx_ch = {}
                for ic in range(NC):
                    par = ic % 2
                    xi, dxs, tb = xinb[par], dx[par], tbuf[par]

                    @block.sync
                    def _(e, ic=ic, xi=xi, dxs=dxs):
                        e = Trk(e)
                        if ic >= 2:
                            e.wait_ge(vm.h, vm_ch[ic - 2])
                        else:
                            e.wait_ge(vm.h, vm_phase0)
                        dx_ch[ic] = dxs.inc(
                            e.dma_start(xi[:], src_d.ap()[ic * 128:(ic + 1) * 128, :]), 16)

                    @block.vector
                    def _(e, ic=ic, xi=xi, dxs=dxs, tb=tb):
                        e = Trk(e)
                        e.wait_ge(dxs.h, dx_ch[ic])
                        if ic == 0:
                            e.wait_ge(tm.h, t_bc)
                        if ic >= 2:
                            e.wait_ge(tm.h, tm_ch[ic - 2])
                        e.tensor_sub(tb[:], xi[:], ps_meanB[:])
                        e.tensor_mul(xf[:, ic * T:(ic + 1) * T], tb[:], ps_rstdB[:])
                        vm_ch[ic] = vm.inc(e.last)

                    @block.tensor
                    def _(e, ic=ic, tb=tb):
                        e = Trk(e)
                        e.wait_ge(vm.h, vm_ch[ic])
                        if ic == 0:
                            e.wait_ge(gz.h, gz_setup)
                        for (lo, hi) in HV:
                            e.matmul(ps_h[:, lo:hi], gw1_sb[:, ic * GD:(ic + 1) * GD],
                                     tb[:, lo:hi], start=(ic == 0), stop=(ic == NC - 1))
                        for (lo, hi) in HV:
                            e.matmul(ps_tmp[:, lo:hi], a_sb[:, ic * RC:(ic + 1) * RC],
                                     xf[:, ic * T + lo: ic * T + hi],
                                     start=(ic == 0), stop=(ic == NC - 1))
                        tm_ch[ic] = tm.inc(e.last)
                t_hmm = tm.n

                @block.scalar
                def _(e):
                    e = Trk(e)
                    e.wait_ge(tm.h, t_hmm)
                    am.inc(e.activation(h_sb[:], ps_h[:], AF.Copy))
                a_hcp = am.n

                @block.vector
                def _(e):
                    e = Trk(e)
                    e.wait_ge(am.h, a_hcp)
                    e.tensor_mul(h2_sb[:], h_sb[:], ps_rstdB[0:GD, :])
                    e.tensor_scalar_add(h2_sb[:], h2_sb[:], gb1_sb[:, 0:1])
                    vm.inc(e.last)
            if STOP_AFTER == "norm":
                @block.gpsimd
                def _(e):
                    e = Trk(e)
                    e.wait_ge(vm.h, vm.n)
                    gz.inc(e.dma_start(out_d.ap()[0:128, :], xf[:, 0:T]), 16)
                    gz.inc(e.dma_start(out_d.ap()[128:256, :], xf[:, (NC - 1) * T:NC * T]), 16)
                @block.sync
                def _(e):
                    e = Trk(e)
                    e.wait_ge(vm.h, vm.n)
                    dy.inc(e.dma_start(out_d.ap()[256:256 + GD, :], h2_sb[:, :]), 16)
                    e.wait_ge(dy.h, dy.n)
                    e.wait_ge(gz.h, gz.n)
                fstk.close()
                return

            with nc.psum_tensor(f"g_sum_{icnt}", [1, 2 * T], F32) as ps_gsum:
                @block.vector
                def _(e):
                    e = Trk(e)
                    e.tensor_mul(hsq_sb[:], h2_sb[:], h2_sb[:])
                    vm.inc(e.last)
                v_hsq = vm.n

                @block.tensor
                def _(e):
                    e = Trk(e)
                    e.wait_ge(vm.h, v_hsq)
                    for (lo, hi) in HV:
                        e.matmul(ps_gsum[0:1, lo:hi], ones_cf[0:GD, :], h2_sb[:, lo:hi], start=True, stop=True)
                        e.matmul(ps_gsum[0:1, T + lo:T + hi], ones_cf[0:GD, :], hsq_sb[:, lo:hi], start=True, stop=True)
                    tm.inc(e.last)
                t_gs = tm.n

                @block.vector
                def _(e):
                    e = Trk(e)
                    e.wait_ge(tm.h, t_gs)
                    e.tensor_scalar_mul(row_m[:], ps_gsum[0:1, 0:T], 1.0 / GD)
                    e.tensor_scalar_mul(row_q[:], ps_gsum[0:1, T:2 * T], 1.0 / GD)
                    e.tensor_mul(row_s[:], row_m[:], row_m[:])
                    e.tensor_sub(row_q[:], row_q[:], row_s[:])
                    vm.inc(e.last)
                v_gv = vm.n

                @block.scalar
                def _(e):
                    e = Trk(e)
                    e.wait_ge(vm.h, v_gv)
                    e.activation(row_s[:], row_q[:], AF.Ln, bias=eps5_sb[0:1, 0:1])
                    e.activation(row_r[:], row_s[:], AF.Exp, scale=-0.5)
                    am.inc(e.last)

            with (nc.psum_tensor(f"g_meanB_{icnt}", [GD, T], F32) as ps_gmB,
                  nc.psum_tensor(f"g_rstdB_{icnt}", [GD, T], F32) as ps_grB):
                @block.tensor
                def _(e):
                    e = Trk(e)
                    e.wait_ge(am.h, am.n)
                    for (lo, hi) in HV:
                        e.matmul(ps_gmB[:, lo:hi], ones_rf[0:1, 0:GD], row_m[0:1, lo:hi], start=True, stop=True)
                        e.matmul(ps_grB[:, lo:hi], ones_rf[0:1, 0:GD], row_r[0:1, lo:hi], start=True, stop=True)
                    tm.inc(e.last)
                t_gb = tm.n

                @block.vector
                def _(e):
                    e = Trk(e)
                    e.wait_ge(tm.h, t_gb)
                    e.tensor_sub(h_sb[:], h2_sb[:], ps_gmB[:])
                    e.tensor_mul(h_sb[:], h_sb[:], ps_grB[:])
                    e.tensor_scalar(h_sb[:], h_sb[:], lng_sb[:, 0:1], lnb_sb[:, 0:1], ALU.mult, ALU.add)
                    e.tensor_relu(h_sb[:], h_sb[:])
                    vm.inc(e.last)
                v_hn = vm.n

            if STOP_AFTER == "gateln":
                @block.sync
                def _(e):
                    e = Trk(e)
                    e.wait_ge(vm.h, vm.n)
                    dy.inc(e.dma_start(out_d.ap()[0:GD, :], h_sb[:, :]), 16)
                    e.wait_ge(dy.h, dy.n)
                fstk.close()
                return
            with (nc.psum_tensor(f"g_ps_{icnt}", [128, E * NT], F32) as ps_g,
                  nc.psum_tensor(f"wt_ps_{icnt}", [E, 128], F32) as ps_wt,
                  nc.psum_tensor(f"cnt_ps_{icnt}", [1, 2 * E], F32) as ps_cnt,
                  nc.psum_tensor(f"alw_ps_{icnt}", [128, 2 * E], F32) as ps_alw):
                @block.tensor
                def _(e):
                    e = Trk(e)
                    e.wait_ge(vm.h, v_hn)
                    for t in range(NT):
                        e.matmul(ps_g[:, t * E:(t + 1) * E], h_sb[:, t * 128:(t + 1) * 128],
                                 gw2_sb[:, :], start=True, stop=True)
                    tm.inc(e.last)
                t_g = tm.n

                vm_g = {}
                for t in range(NT):
                    @block.vector
                    def _(e, t=t):
                        e = Trk(e)
                        e.wait_ge(tm.h, t_g)
                        eqv = eq_sb[:, t * 2 * E:(t + 1) * 2 * E]
                        e.tensor_add(g_sb[:], ps_g[:, t * E:(t + 1) * E], gb2b_sb[:])
                        e.reduce_max(m1_sb[:], g_sb[:], axis=AX.X)
                        e.tensor_single_scalar(eqv[:, 0:E], g_sb[:], m1_sb[:, 0:1], ALU.is_equal)
                        e.scalar_tensor_tensor(gm_sb[:], eqv[:, 0:E], -1e30, g_sb[:], ALU.mult, ALU.add)
                        e.reduce_max(m2_sb[:], gm_sb[:], axis=AX.X)
                        e.tensor_single_scalar(eqv[:, E:2 * E], gm_sb[:], m2_sb[:, 0:1], ALU.is_equal)
                        e.tensor_sub(m2_sb[:], m2_sb[:], m1_sb[:])
                        vm_g[t] = vm.inc(e.last)

                    @block.scalar
                    def _(e, t=t):
                        e = Trk(e)
                        e.wait_ge(vm.h, vm_g[t])
                        am.inc(e.activation(e21_sb[:], m2_sb[:], AF.Exp))
                    a_e = am.n

                    @block.vector
                    def _(e, t=t):
                        e = Trk(e)
                        e.wait_ge(am.h, a_e)
                        e.tensor_scalar_add(zden_sb[:], e21_sb[:], 1.0)
                        e.reciprocal(g12_sb[:, 2 * t:2 * t + 1], zden_sb[:])
                        e.tensor_mul(g12_sb[:, 2 * t + 1:2 * t + 2], e21_sb[:],
                                     g12_sb[:, 2 * t:2 * t + 1])
                        vm.inc(e.last)
                    v_g12 = vm.n

                    @block.tensor
                    def _(e, t=t):
                        e = Trk(e)
                        e.wait_ge(vm.h, v_g12)
                        e.matmul(ps_cnt[0:1, :], ones_cf[:, :],
                                 eq_sb[:, t * 2 * E:(t + 1) * 2 * E],
                                 start=(t == 0), stop=(t == NT - 1))
                        tm.inc(e.last)
                t_cnt = tm.n

                if STOP_AFTER == "gates":
                    @block.sync
                    def _(e):
                        e = Trk(e)
                        e.wait_ge(tm.h, tm.n)
                        e.wait_ge(vm.h, vm.n)
                        dy.inc(e.dma_start(out_d.ap()[0:128, 0:2 * E * NT], eq_sb[:, :]), 16)
                        dy.inc(e.dma_start(out_d.ap()[128:256, 0:2 * NT], g12_sb[:, :]), 16)
                        dy.inc(e.dma_start(out_d.ap()[256:384, 0:E], g_sb[:, :]), 16)
                        dy.inc(e.dma_start(out_d.ap()[384:512, 0:1], m1_sb[:, :], allow_slow_non_contiguous=True), 16)
                        dy.inc(e.dma_start(out_d.ap()[512:640, 0:1], m2_sb[:, :], allow_slow_non_contiguous=True), 16)
                        e.wait_ge(dy.h, dy.n)
                    fstk.close()
                    return
                @block.vector
                def _(e):
                    e = Trk(e)
                    e.wait_ge(tm.h, t_cnt)
                    vm.inc(e.tensor_copy(cnt_sb[:], ps_cnt[:]))
                v_cnt = vm.n

                @block.sync
                def _(e):
                    e = Trk(e)
                    e.wait_ge(vm.h, v_cnt)
                    dy.inc(e.dma_start(cnt_in[icnt].ap(), cnt_sb[:]), 16)
                d_ci = dy.n

                @block.gpsimd
                def _(e):
                    e = Trk(e)
                    e.wait_ge(dy.h, d_ci)
                    cc.inc(e.collective_compute(
                        "AllReduce", ALU.add, replica_groups=[list(range(NCORES))],
                        ins=[cnt_in[icnt].ap().opt()], outs=[cnt_out[icnt].ap().opt()]), 1)
                c_ar = cc.n

                @block.sync
                def _(e):
                    e = Trk(e)
                    e.wait_ge(cc.h, c_ar)
                    dy.inc(e.dma_start(cnt_sb[:], cnt_out[icnt].ap()), 16)
                d_co = dy.n

                @block.vector
                def _(e):
                    e = Trk(e)
                    e.wait_ge(dy.h, d_co)
                    e.tensor_single_scalar(alw_row[:], cnt_sb[:], c.cap, ALU.is_le)
                    vm.inc(e.last)
                v_alw = vm.n

                @block.tensor
                def _(e):
                    e = Trk(e)
                    e.wait_ge(vm.h, v_alw)
                    tm.inc(e.matmul(ps_alw[:, :], ones_rf[0:1, :], alw_row[0:1, :], start=True, stop=True))
                t_alw = tm.n

                @block.vector
                def _(e):
                    e = Trk(e)
                    e.wait_ge(tm.h, t_alw)
                    vm.inc(e.tensor_copy(alw_sb[:], ps_alw[:]))
                v_alwc = vm.n

                tm_wt = {}
                for t in range(NT):
                    @block.vector
                    def _(e, t=t):
                        e = Trk(e)
                        e.wait_ge(vm.h, v_alwc)
                        if t >= 1:
                            e.wait_ge(tm.h, tm_wt[t - 1])
                        eqv = eq_sb[:, t * 2 * E:(t + 1) * 2 * E]
                        e.tensor_scalar_mul(wz_sb[:, 0:E], eqv[:, 0:E], g12_sb[:, 2 * t:2 * t + 1])
                        e.tensor_scalar_mul(wz_sb[:, E:2 * E], eqv[:, E:2 * E], g12_sb[:, 2 * t + 1:2 * t + 2])
                        e.tensor_mul(wz_sb[:], wz_sb[:], alw_sb[:])
                        e.tensor_add(w_sb[:], wz_sb[:, 0:E], wz_sb[:, E:2 * E])
                        vm.inc(e.last)
                    v_w = vm.n

                    @block.tensor
                    def _(e, t=t):
                        e = Trk(e)
                        e.wait_ge(vm.h, v_w)
                        tm_wt[t] = tm.inc(e.transpose(ps_wt[:, :], w_sb[:, :], idf_sb[:, :]))

                    @block.vector
                    def _(e, t=t):
                        e = Trk(e)
                        e.wait_ge(tm.h, tm_wt[t])
                        vm.inc(e.tensor_copy(wt_sb[:, t * 128:(t + 1) * 128], ps_wt[:, :]))
                v_wt = vm.n

            if STOP_AFTER == "w":
                @block.sync
                def _(e):
                    e = Trk(e)
                    e.wait_ge(vm.h, vm.n)
                    dy.inc(e.dma_start(out_d.ap()[0:E, :], wt_sb[:, :]), 16)
                    dy.inc(e.dma_start(out_d.ap()[8:9, 0:2 * E], cnt_sb[:, :]), 16)
                    dy.inc(e.dma_start(out_d.ap()[9:10, 0:2 * E], alw_row[:, :]), 16)
                    dy.inc(e.dma_start(out_d.ap()[10:138, 0:E], w_sb[:, :]), 16)
                    dy.inc(e.dma_start(out_d.ap()[138:266, 0:2 * NT], g12_sb[:, :]), 16)
                    dy.inc(e.dma_start(out_d.ap()[266:394, 0:2 * E * NT], eq_sb[:, :]), 16)
                    dy.inc(e.dma_start(out_d.ap()[394:522, 0:2 * E], alw_sb[:, :]), 16)
                    e.wait_ge(dy.h, dy.n)
                fstk.close()
                return
            with nc.psum_tensor(f"wtexp_ps_{icnt}", [128, T], F32) as ps_we:
                @block.tensor
                def _(e):
                    e = Trk(e)
                    e.wait_ge(vm.h, v_wt)
                    for (lo, hi) in HV:
                        e.matmul(ps_we[:, lo:hi], map_sb[:, :], wt_sb[:, lo:hi], start=True, stop=True)
                    tm.inc(e.last)
                t_we = tm.n

                @block.vector
                def _(e):
                    e = Trk(e)
                    e.wait_ge(tm.h, t_we)
                    e.tensor_copy(wtexp_sb[:], ps_we[:])
                    e.tensor_mul(tmps_sb[:], ps_tmp[:], wtexp_sb[:])
                    vm.inc(e.last)
        v_tmps = vm.n

        with (nc.psum_tensor(f"main0_{icnt}", [128, T], F32) as pm0,
              nc.psum_tensor(f"main1_{icnt}", [128, T], F32) as pm1,
              nc.sbuf_tensor(f"wslf0_{icnt}", [128, NC * 128], BF16) as wsl0,
              nc.sbuf_tensor(f"wslf1_{icnt}", [128, NC * 128], BF16) as wsl1):
            wslab = [wsl0, wsl1]
            ps_main = [pm0, pm1]
            gw_oc = {}
            tm_oc = {}
            vm_oc = {}
            dxm_oc = {}
            dyr_oc = {}
            for oc in range(NC):
                par = oc % 2
                gws = gw[par]

                @block.gpsimd
                def _(e, oc=oc, par=par, gws=gws):
                    e = Trk(e)
                    if oc >= 2:
                        e.wait_ge(tm.h, tm_oc[oc - 2])
                    else:
                        e.wait_ge(tm.h, tm_phase0)
                    gw_oc[oc] = gws.inc(
                        e.dma_start(wslab[par][:], wff_d.ap()[oc, :, :]), 16)

                @block.sync
                def _(e, oc=oc, par=par):
                    e = Trk(e)
                    if oc >= 2:
                        e.wait_ge(vm.h, vm_oc[oc - 2])
                    else:
                        e.wait_ge(vm.h, vm_ch[NC - 1])
                    dxm_oc[oc] = dx[par].inc(
                        e.dma_start(xinb[par][:], src_d.ap()[oc * 128:(oc + 1) * 128, :]), 16)

                @block.tensor
                def _(e, oc=oc, par=par, gws=gws):
                    e = Trk(e)
                    e.wait_ge(gws.h, gw_oc[oc])
                    if oc == 0:
                        e.wait_ge(vm.h, v_tmps)
                    if oc >= 2:
                        e.wait_ge(vm.h, vm_oc[oc - 2])
                    for (lo, hi) in HV:
                        for ic in range(NC):
                            e.matmul(ps_main[par][:, lo:hi],
                                     wslab[par][:, ic * 128:(ic + 1) * 128],
                                     xf[:, ic * T + lo: ic * T + hi],
                                     start=(ic == 0), stop=False)
                        e.matmul(ps_main[par][:, lo:hi],
                                 bcat_sb[:, oc * 128:(oc + 1) * 128],
                                 tmps_sb[:, lo:hi], start=False, stop=True)
                    tm_oc[oc] = tm.inc(e.last)

                @block.vector
                def _(e, oc=oc, par=par):
                    e = Trk(e)
                    e.wait_ge(tm.h, tm_oc[oc])
                    e.wait_ge(dx[par].h, dxm_oc[oc])
                    if oc >= 2:
                        e.wait_ge(dy.h, dyr_oc[oc - 2])
                    e.scalar_tensor_tensor(rout[par][:], ps_main[par][:], 0.5,
                                           xinb[par][:], ALU.mult, ALU.add)
                    if dst_d is not None:
                        if oc == 0:
                            e.tensor_copy(acc_s[:], rout[par][:])
                            e.tensor_mul(acc_q[:], rout[par][:], rout[par][:])
                        else:
                            e.tensor_add(acc_s[:], acc_s[:], rout[par][:])
                            e.tensor_mul(sqs[:], rout[par][:], rout[par][:])
                            e.tensor_add(acc_q[:], acc_q[:], sqs[:])
                    vm_oc[oc] = vm.inc(e.last)

                @block.sync
                def _(e, oc=oc, par=par):
                    e = Trk(e)
                    e.wait_ge(vm.h, vm_oc[oc])
                    tgt = dst_d if dst_d is not None else out_d
                    dyr_oc[oc] = dy.inc(e.dma_start(tgt.ap()[oc * 128:(oc + 1) * 128, :], rout[par][:]), 16)
        fstk.close()

    def qkv_attn_phase():
        vm_phase0 = vm.n
        qstk = ExitStack()
        xf = qstk.enter_context(nc.sbuf_tensor("xf_qkv", [128, NC * T], BF16))
        with nc.psum_tensor("ln2_sum", [1, 2 * T], F32) as ps_sum:
            ln_rows(ps_sum, eps5_sb, D)
        with (nc.psum_tensor("ln2_meanB", [128, T], F32) as ps_meanB,
              nc.psum_tensor("ln2_rstdB", [128, T], F32) as ps_rstdB):
            bcast_rows(ps_meanB, ps_rstdB)
            t_bc = tm.n
            vm_ch = {}
            dx_ch = {}
            for ic in range(NC):
                par = ic % 2
                xi, dxs, tb = xinb[par], dx[par], tbuf[par]

                @block.sync
                def _(e, ic=ic, xi=xi, dxs=dxs):
                    e = Trk(e)
                    if ic >= 2:
                        e.wait_ge(vm.h, vm_ch[ic - 2])
                    else:
                        e.wait_ge(vm.h, vm_phase0)
                    dx_ch[ic] = dxs.inc(
                        e.dma_start(xi[:], res1_d.ap()[ic * 128:(ic + 1) * 128, :]), 16)

                @block.vector
                def _(e, ic=ic, xi=xi, dxs=dxs, tb=tb):
                    e = Trk(e)
                    e.wait_ge(dxs.h, dx_ch[ic])
                    if ic == 0:
                        e.wait_ge(tm.h, t_bc)
                    e.tensor_sub(tb[:], xi[:], ps_meanB[:])
                    e.tensor_mul(xf[:, ic * T:(ic + 1) * T], tb[:], ps_rstdB[:])
                    vm_ch[ic] = vm.inc(e.last)
        v_xf2 = vm.n

        def kq_loop(w_dram, bias_sbuf, gain_sbuf, dst, eps_ap, tag):
            tm_kq0 = tm.n
            with (nc.psum_tensor(f"kq0_{tag}", [128, T], F32) as p0,
                  nc.psum_tensor(f"kq1_{tag}", [128, T], F32) as p1,
                  nc.psum_tensor(f"kq_ss_{tag}", [1, T], F32) as ps_ss,
                  nc.psum_tensor(f"kq_rb_{tag}", [128, T], F32) as ps_rb,
                  nc.sbuf_tensor(f"wslk0_{tag}", [128, NC * 128], BF16) as wsl0,
                  nc.sbuf_tensor(f"wslk1_{tag}", [128, NC * 128], BF16) as wsl1):
                wslab = [wsl0, wsl1]
                ps_qk = [p0, p1]
                gw_oc, tm_oc, am_oc, vm_oc = {}, {}, {}, {}
                tm_rb, vm_fin, dy_fin = {}, {}, {}
                for oc in range(H):
                    par = oc % 2
                    gws = gw[par]

                    @block.gpsimd
                    def _(e, oc=oc, par=par, gws=gws):
                        e = Trk(e)
                        if oc >= 2:
                            e.wait_ge(tm.h, tm_oc[oc - 2])
                        else:
                            e.wait_ge(tm.h, tm_kq0)
                        gw_oc[oc] = gws.inc(e.dma_start(wslab[par][:], w_dram.ap()[oc, :, :]), 16)

                    @block.tensor
                    def _(e, oc=oc, par=par, gws=gws):
                        e = Trk(e)
                        e.wait_ge(gws.h, gw_oc[oc])
                        if oc == 0:
                            e.wait_ge(vm.h, v_xf2)
                        if oc >= 2:
                            e.wait_ge(am.h, am_oc[oc - 2])
                        for (lo, hi) in HV:
                            for ic in range(NC):
                                e.matmul(ps_qk[par][:, lo:hi],
                                         wslab[par][:, ic * 128:(ic + 1) * 128],
                                         xf[:, ic * T + lo: ic * T + hi],
                                         start=(ic == 0), stop=(ic == NC - 1))
                        tm_oc[oc] = tm.inc(e.last)

                    @block.scalar
                    def _(e, oc=oc, par=par):
                        e = Trk(e)
                        e.wait_ge(tm.h, tm_oc[oc])
                        if oc >= 2:
                            e.wait_ge(vm.h, vm_fin[oc - 2])
                        am_oc[oc] = am.inc(e.activation(
                            qraw[par][:], ps_qk[par][:], AF.Identity, bias=bias_sbuf[:, oc:oc + 1]))

                    @block.vector
                    def _(e, oc=oc, par=par):
                        e = Trk(e)
                        e.wait_ge(am.h, am_oc[oc])
                        if oc >= 1:
                            e.wait_ge(vm.h, vm_fin.get(oc - 1, 0))
                        e.tensor_mul(q2_sb[:], qraw[par][:], qraw[par][:])
                        vm_oc[oc] = vm.inc(e.last)

                    @block.tensor
                    def _(e, oc=oc):
                        e = Trk(e)
                        e.wait_ge(vm.h, vm_oc[oc])
                        for (lo, hi) in HV:
                            e.matmul(ps_ss[0:1, lo:hi], ones_cb[:, :], q2_sb[:, lo:hi],
                                     start=True, stop=True)
                        tm.inc(e.last)
                    t_ss = tm.n

                    @block.vector
                    def _(e, oc=oc):
                        e = Trk(e)
                        e.wait_ge(tm.h, t_ss)
                        e.tensor_scalar_mul(row_m[:], ps_ss[0:1, 0:T], 1.0 / 128.0)
                        vm.inc(e.last)
                    v_ms = vm.n

                    @block.scalar
                    def _(e, oc=oc):
                        e = Trk(e)
                        e.wait_ge(vm.h, v_ms)
                        e.activation(row_s[:], row_m[:], AF.Ln, bias=eps_ap[0:1, 0:1])
                        e.activation(row_r[:], row_s[:], AF.Exp, scale=-0.5)
                        am.inc(e.last)
                    a_rs = am.n

                    @block.tensor
                    def _(e, oc=oc):
                        e = Trk(e)
                        e.wait_ge(am.h, a_rs)
                        for (lo, hi) in HV:
                            e.matmul(ps_rb[:, lo:hi], ones_rf[0:1, :], row_r[0:1, lo:hi],
                                     start=True, stop=True)
                        tm_rb[oc] = tm.inc(e.last)

                    @block.vector
                    def _(e, oc=oc, par=par):
                        e = Trk(e)
                        e.wait_ge(tm.h, tm_rb[oc])
                        if oc >= 2:
                            e.wait_ge(dy.h, dy_fin.get(oc - 2, 0))
                        e.tensor_mul(qraw[par][:], qraw[par][:], ps_rb[:])
                        e.tensor_scalar_mul(kq_out[par][:], qraw[par][:], gain_sbuf[:, 0:1])
                        vm_fin[oc] = vm.inc(e.last)

                    @block.sync
                    def _(e, oc=oc, par=par):
                        e = Trk(e)
                        e.wait_ge(vm.h, vm_fin[oc])
                        if dst[0] == "dram":
                            g = oc // c.HPG
                            r0 = (oc % c.HPG) * 128
                            dy_fin[oc] = dy.inc(e.dma_start(dst[1][g].ap()[r0:r0 + 128, :], kq_out[par][:]), 16)
                        else:
                            dy_fin[oc] = dy.inc(e.dma_start(dst[1].ap()[oc * 128:(oc + 1) * 128, :], kq_out[par][:]), 16)
            return dict(vm_fin=vm_fin)

        kq_loop(wk_d, bk_sb, gk_sb, ("dram", kg_own), eps6_sb, "k")
        dy_after_k = dy.n

        with ExitStack() as vctx:
            wvslab = [vctx.enter_context(nc.sbuf_tensor(f"wvslab{i}", [128, NC * c.OGW], BF16))
                      for i in range(2)]
            ps_v = vctx.enter_context(nc.psum_tensor("ps_v", [128, NT * 512], F32))
            tm_v0 = tm.n
            gw_og, tm_og, am_ot, dy_v = {}, {}, {}, {}
            for og in range(c.OG):
                par = og % 2

                @block.gpsimd
                def _(e, og=og, par=par):
                    e = Trk(e)
                    if og >= 2:
                        e.wait_ge(tm.h, tm_og[og - 2])
                    else:
                        e.wait_ge(tm.h, tm_v0)
                    gw_og[og] = gw[par].inc(e.dma_start(wvslab[par][:], wv_d.ap()[og, :, :]), 16)

                @block.tensor
                def _(e, og=og, par=par):
                    e = Trk(e)
                    e.wait_ge(gw[par].h, gw_og[og])
                    if og == 0:
                        e.wait_ge(vm.h, v_xf2)
                        e.wait_ge(gm.h, gm_setup)
                    if og >= 1:
                        e.wait_ge(am.h, am_ot[(og - 1, NT - 1)])
                    for tt in range(NT):
                        for ic in range(NC):
                            e.matmul(ps_v[:, tt * 512:tt * 512 + c.OGW],
                                     xf[:, ic * T + tt * 128: ic * T + (tt + 1) * 128],
                                     wvslab[par][:, ic * c.OGW:(ic + 1) * c.OGW],
                                     start=(ic == 0), stop=False)
                        e.matmul(ps_v[:, tt * 512:tt * 512 + c.OGW],
                                 ones_rb[0:1, :], bv_sb[0:1, og * c.OGW:(og + 1) * c.OGW],
                                 start=False, stop=True)
                        tm_og[og] = tm.inc(e.last)

                for tt in range(NT):
                    @block.scalar
                    def _(e, og=og, tt=tt):
                        e = Trk(e)
                        e.wait_ge(tm.h, tm_og[og] - (NT - 1 - tt))
                        if (og, tt) != (0, 0):
                            prev = (og, tt - 1) if tt > 0 else (og - 1, NT - 1)
                            e.wait_ge(dy.h, dy_v.get(prev, 0))
                        am_ot[(og, tt)] = am.inc(e.activation(
                            vout_sb[tt % 2][:, 0:c.OGW], ps_v[:, tt * 512:tt * 512 + c.OGW], AF.Copy))

                    @block.sync
                    def _(e, og=og, tt=tt):
                        e = Trk(e)
                        e.wait_ge(am.h, am_ot[(og, tt)])
                        g = og // (c.OG // c.NG)
                        hg0 = (og % (c.OG // c.NG)) * (c.OGW // 128)
                        nj = c.OGW // 128
                        dst4 = vg_own[g].ap().rearrange("(a b p) d -> p a b d", b=NT, p=128)
                        src = vout_sb[tt % 2][:, 0:c.OGW].rearrange("p (j d) -> p j d", d=128)
                        dy_v[(og, tt)] = dy.inc(
                            e.dma_start(dst4[:, hg0:hg0 + nj, tt, :], src), 16)

            ogpg = c.OG // c.NG
            cc_after_g = {}

            @block.gpsimd
            def _(e):
                e = Trk(e)
                e.wait_ge(dy.h, dy_after_k)
                for g in range(c.NG):
                    e.wait_ge(dy.h, dy_v[(g * ogpg + ogpg - 1, NT - 1)])
                    cc.inc(e.collective_compute(
                        "AllGather", ALU.bypass, replica_groups=[list(range(NCORES))],
                        ins=[kg_own[g].ap().opt()], outs=[kg_gth[g].ap().opt()]), 1)
                    cc.inc(e.collective_compute(
                        "AllGather", ALU.bypass, replica_groups=[list(range(NCORES))],
                        ins=[vg_own[g].ap().opt()], outs=[vg_gth[g].ap().opt()]), 1)
                    cc_after_g[g] = cc.n

        kq_loop(wq_d, bq_sb, gq_sb, ("dram1", q_d), eps6_sb, "q")
        dy_qdone = dy.n
        qstk.close()

        if True:
            with ExitStack() as actx:
                kh = [actx.enter_context(nc.sbuf_tensor(f"kh{i}", [128, NCORES * T], BF16)) for i in range(2)]
                vh = [actx.enter_context(nc.sbuf_tensor(f"vh{i}", [128, NKC * 128], BF16)) for i in range(2)]
                qh = [actx.enter_context(nc.sbuf_tensor(f"qh{i}", [128, T], BF16)) for i in range(2)]
                pt_sb = actx.enter_context(nc.sbuf_tensor("pt_sb", [128, 3 * T], BF16))
                invl_sb = actx.enter_context(nc.sbuf_tensor("invl_sb", [1, T], F32))
                invlb_sb = actx.enter_context(nc.sbuf_tensor("invlb_sb", [128, T], F32))
                lacc = actx.enter_context(nc.sbuf_tensor("lacc", [128, T], F32))
                ao_sb = [actx.enter_context(nc.sbuf_tensor(f"ao_sb{i}", [128, T], BF16)) for i in range(2)]
                ps_s = [actx.enter_context(nc.psum_tensor(f"att_s{i}", [128, T], F32)) for i in range(2)]
                ps_o = [actx.enter_context(nc.psum_tensor(f"att_o{i}", [128, T], F32)) for i in range(2)]
                pg = mks("pg")
                LWD = (3 * T) // 4

                tm_hd = {}
                vm_hd = {}
                dxk = {}
                dxv = {}
                dy_ao = {}
                for h in range(H):
                    par = h % 2
                    op = h % 2
                    g = h // c.HPG
                    hh = h % c.HPG
                    tm0 = tm.n
                    vm0 = vm.n
                    am0 = am.n
                    pg0 = pg.n

                    @block.sync
                    def _(e, h=h, par=par, g=g, hh=hh):
                        e = Trk(e)
                        e.wait_ge(cc.h, cc_after_g[g])
                        if h >= 2:
                            e.wait_ge(tm.h, tm_hd[h - 2])
                        src = kg_gth[g].ap().rearrange("(r a p) t -> p r a t", a=c.HPG, p=128)
                        dst = kh[par][:].rearrange("p (r t) -> p r t", t=T)
                        dxk[h] = dx[par].inc(e.dma_start(dst, src[:, :, hh, :]), 16)
                        e.wait_ge(dy.h, dy_qdone)
                        dxk[h] = dx[par].inc(
                            e.dma_start(qh[par][:], q_d.ap()[h * 128:(h + 1) * 128, :]), 16)
                        vsrc = vg_gth[g].ap().rearrange("(r a b p) d -> p r a b d",
                                                        a=c.HPG, b=NT, p=128)
                        vdst = vh[par][:].rearrange("p (r b d) -> p r b d", b=NT, d=128)
                        for r in range(NCORES):
                            dxv[h] = dx[par].inc(
                                e.dma_start(vdst[:, r, :, :], vsrc[:, r, hh, :, :]), 16)

                    @block.tensor
                    def _(e, h=h, par=par, op=op, tm0=tm0, vm0=vm0, am0=am0, pg0=pg0):
                        e = Trk(e)
                        e.wait_ge(dx[par].h, dxk[h])
                        e.wait_ge(am.h, am0)
                        qv = qh[par]
                        if h >= 1:
                            e.wait_ge(vm.h, vm0 - 2)
                        for (lo, hi) in HV:
                            e.matmul(ps_s[0][:, lo:hi], kh[par][:, 0:128], qv[:, lo:hi],
                                     start=True, stop=True)
                        tm.inc(e.last)
                        if h >= 1:
                            e.wait_ge(vm.h, vm0 - 1)
                        for (lo, hi) in HV:
                            e.matmul(ps_s[1][:, lo:hi], kh[par][:, 128:256], qv[:, lo:hi],
                                     start=True, stop=True)
                        tm.inc(e.last)
                        if h >= 2:
                            e.wait_ge(vm.h, vm_hd[h - 2])
                        e.wait_ge(vm.h, vm0 + 1)
                        tm.inc(e.transpose(ps_o[op][0:1, 0:128], pm_sb[:, :], idf_sb[:, :]))
                        e.wait_ge(vm.h, vm0 + 2)
                        tm.inc(e.matmul(ps_o[op][:, 0:1], ones_rf[0:1, :], brow_sb[0:1, 0:1],
                                        start=True, stop=True))
                        for cix in range(NKC):
                            e.wait_ge(am.h, am0 + cix + 1)
                            if cix == 0:
                                e.wait_ge(dx[par].h, dxv[h])
                            ptv = pt_sb[:, (cix % 3) * T:(cix % 3 + 1) * T]
                            for (lo, hi) in HV:
                                e.matmul(ps_o[op][:, lo:hi], vh[par][:, cix * 128:(cix + 1) * 128],
                                         ptv[:, lo:hi], start=(cix == 0), stop=(cix == NKC - 1))
                            if cix + 2 < NKC:
                                for (lo, hi) in HV:
                                    e.matmul(ps_s[cix % 2][:, lo:hi],
                                             kh[par][:, (cix + 2) * 128:(cix + 3) * 128],
                                             qv[:, lo:hi], start=True, stop=True)
                            tm.inc(e.last)
                        e.wait_ge(vm.h, vm0 + 3 + NKC)
                        e.wait_ge(pg.h, pg0 + NKC)
                        for (lo, hi) in HV:
                            e.matmul(ps_s[0][0:1, lo:hi], ones_cf[:, :], lacc[:, lo:hi],
                                     start=True, stop=True)
                        tm.inc(e.last)
                        e.wait_ge(vm.h, vm0 + 4 + NKC)
                        for (lo, hi) in HV:
                            e.matmul(ps_s[1][:, lo:hi], ones_rf[0:1, :], invl_sb[0:1, lo:hi],
                                     start=True, stop=True)
                        tm_hd[h] = tm.inc(e.last)
                        assert tm_hd[h] == tm0 + NKC + 6

                    @block.scalar
                    def _(e, h=h, tm0=tm0, vm0=vm0, pg0=pg0):
                        e = Trk(e)
                        for cix in range(NKC):
                            if cix < 2:
                                e.wait_ge(tm.h, tm0 + cix + 1)
                            else:
                                e.wait_ge(tm.h, tm0 + cix + 3)
                            if cix == 0:
                                e.wait_ge(vm.h, vm0 + 3)
                            if cix >= 3:
                                e.wait_ge(vm.h, vm0 + cix + 1)
                                e.wait_ge(pg.h, pg0 + cix - 2)
                            ptv = pt_sb[:, (cix % 3) * T:(cix % 3 + 1) * T]
                            am.inc(e.activation(ptv[:], ps_s[cix % 2][:], AF.Exp,
                                                bias=bias_sb[:, 0:1], scale=c.scale))

                    @block.gpsimd
                    def _(e, h=h, am0=am0):
                        e = Trk(e)
                        for cix in range(NKC):
                            e.wait_ge(am.h, am0 + cix + 1)
                            ptv = pt_sb[:, (cix % 3) * T:(cix % 3 + 1) * T]
                            if cix == 0:
                                pg.inc(e.tensor_copy(lacc[:, LWD:T], ptv[:, LWD:T]))
                            else:
                                pg.inc(e.tensor_tensor(lacc[:, LWD:T], lacc[:, LWD:T],
                                                       ptv[:, LWD:T], ALU.add))

                    @block.vector
                    def _(e, h=h, par=par, op=op, tm0=tm0, vm0=vm0, am0=am0):
                        e = Trk(e)
                        e.wait_ge(tm.h, tm0 + 1)
                        e.reduce_max(pm_sb[:], ps_s[0][:], axis=AX.X)
                        vm.inc(e.last)
                        e.wait_ge(tm.h, tm0 + 3)
                        e.reduce_max(mrow_sb[:], ps_o[op][0:1, 0:128], axis=AX.X)
                        e.tensor_scalar_mul(brow_sb[:], mrow_sb[:], 0.25)
                        e.tensor_scalar_mul(b2row_sb[:], mrow_sb[:], -0.25)
                        e.tensor_max(brow_sb[:], brow_sb[:], b2row_sb[:])
                        e.tensor_add(brow_sb[:], brow_sb[:], mrow_sb[:])
                        e.tensor_scalar_add(brow_sb[:], brow_sb[:], 10.0)
                        e.tensor_scalar_mul(brow_sb[:], brow_sb[:], -c.scale)
                        vm.inc(e.last)
                        e.wait_ge(tm.h, tm0 + 4)
                        e.tensor_copy(bias_sb[:], ps_o[op][:, 0:1])
                        vm.inc(e.last)
                        for cix in range(NKC):
                            e.wait_ge(am.h, am0 + cix + 1)
                            ptv = pt_sb[:, (cix % 3) * T:(cix % 3 + 1) * T]
                            if cix == 0:
                                vm.inc(e.tensor_copy(lacc[:, 0:LWD], ptv[:, 0:LWD]))
                            else:
                                vm.inc(e.tensor_add(lacc[:, 0:LWD], lacc[:, 0:LWD],
                                                    ptv[:, 0:LWD]))
                        e.wait_ge(tm.h, tm0 + NKC + 5)
                        e.reciprocal(invl_sb[:], ps_s[0][0:1, :])
                        vm.inc(e.last)
                        e.wait_ge(tm.h, tm0 + NKC + 6)
                        e.tensor_copy(invlb_sb[:], ps_s[1][:])
                        vm.inc(e.last)
                        if h >= 2:
                            e.wait_ge(dy.h, dy_ao[h - 2])
                        e.tensor_mul(ao_sb[par][:], ps_o[op][:], invlb_sb[:])
                        vm_hd[h] = vm.inc(e.last)
                        assert vm_hd[h] == vm0 + NKC + 6

                    @block.sync
                    def _(e, h=h, par=par):
                        e = Trk(e)
                        e.wait_ge(vm.h, vm_hd[h])
                        dy_ao[h] = dy.inc(e.dma_start(ao_d.ap()[h * 128:(h + 1) * 128, :], ao_sb[par][:]), 16)
                dy_after_ao = dy.n

        with ExitStack() as octx:
            aob = octx.enter_context(nc.sbuf_tensor("aob", [128, H * T], BF16))
            p0 = octx.enter_context(nc.psum_tensor("op0", [128, T], F32))
            p1 = octx.enter_context(nc.psum_tensor("op1", [128, T], F32))
            wslab = [octx.enter_context(nc.sbuf_tensor(f"wslo{i}", [128, NC * 128], BF16)) for i in range(2)]
            ps_op = [p0, p1]

            @block.sync
            def _(e):
                e = Trk(e)
                e.wait_ge(dy.h, dy_after_ao)
                for af in range(H):
                    dy.inc(e.dma_start(aob[:, af * T:(af + 1) * T],
                                       ao_d.ap()[af * 128:(af + 1) * 128, :]), 16)
            dy_aob = dy.n

            gw_oc, tm_oc, am_oc, vm_oc, dxo_oc, dyr_oc = {}, {}, {}, {}, {}, {}
            tm_op0 = tm.n
            vm_op0 = vm.n
            for oc in range(NC):
                par = oc % 2

                @block.gpsimd
                def _(e, oc=oc, par=par):
                    e = Trk(e)
                    if oc >= 2:
                        e.wait_ge(tm.h, tm_oc[oc - 2])
                    else:
                        e.wait_ge(tm.h, tm_op0)
                    gw_oc[oc] = gw[par].inc(e.dma_start(wslab[par][:], wo_d.ap()[oc, :, :]), 16)

                @block.sync
                def _(e, oc=oc, par=par):
                    e = Trk(e)
                    if oc >= 2:
                        e.wait_ge(vm.h, vm_oc[oc - 2])
                    else:
                        e.wait_ge(vm.h, vm_op0)
                    dxo_oc[oc] = dx[par].inc(
                        e.dma_start(xinb[par][:], res1_d.ap()[oc * 128:(oc + 1) * 128, :]), 16)

                @block.tensor
                def _(e, oc=oc, par=par):
                    e = Trk(e)
                    e.wait_ge(gw[par].h, gw_oc[oc])
                    if oc == 0:
                        e.wait_ge(dy.h, dy_aob)
                    if oc >= 2:
                        e.wait_ge(am.h, am_oc[oc - 2])
                    for (lo, hi) in HV:
                        for af in range(H):
                            e.matmul(ps_op[par][:, lo:hi],
                                     wslab[par][:, af * 128:(af + 1) * 128],
                                     aob[:, af * T + lo: af * T + hi],
                                     start=(af == 0), stop=(af == H - 1))
                    tm_oc[oc] = tm.inc(e.last)

                @block.scalar
                def _(e, oc=oc, par=par):
                    e = Trk(e)
                    e.wait_ge(tm.h, tm_oc[oc])
                    if oc >= 2:
                        e.wait_ge(vm.h, vm_oc[oc - 2])
                    am_oc[oc] = am.inc(e.activation(
                        qraw[par][:], ps_op[par][:], AF.Identity, bias=bo_sb[:, oc:oc + 1]))

                @block.vector
                def _(e, oc=oc, par=par):
                    e = Trk(e)
                    e.wait_ge(am.h, am_oc[oc])
                    e.wait_ge(dx[par].h, dxo_oc[oc])
                    if oc >= 2:
                        e.wait_ge(dy.h, dyr_oc[oc - 2])
                    e.tensor_add(rout[par][:], qraw[par][:], xinb[par][:])
                    if oc == 0:
                        e.tensor_copy(acc_s[:], rout[par][:])
                        e.tensor_mul(acc_q[:], rout[par][:], rout[par][:])
                    else:
                        e.tensor_add(acc_s[:], acc_s[:], rout[par][:])
                        e.tensor_mul(sqs[:], rout[par][:], rout[par][:])
                        e.tensor_add(acc_q[:], acc_q[:], sqs[:])
                    vm_oc[oc] = vm.inc(e.last)

                @block.sync
                def _(e, oc=oc, par=par):
                    e = Trk(e)
                    e.wait_ge(vm.h, vm_oc[oc])
                    dyr_oc[oc] = dy.inc(e.dma_start(res2_d.ap()[oc * 128:(oc + 1) * 128, :], rout[par][:]), 16)

    if phases == 1:
        ffn_phase(x_d, None, 0)
    else:
        ffn_phase(x_d, res1_d, 0)
    if phases >= 2 and STOP_AFTER is None:
        qkv_attn_phase()
    if phases == 2 and STOP_AFTER is None:
        @block.sync
        def _(e):
            e = Trk(e)
            e.wait_ge(dy.h, dy.n)
            for ic in range(NC):
                dy.inc(e.dma_start(xinb[0][:], res2_d.ap()[ic * 128:(ic + 1) * 128, :]), 16)
                e.wait_ge(dy.h, dy.n)
                dy.inc(e.dma_start(out_d.ap()[ic * 128:(ic + 1) * 128, :], xinb[0][:]), 16)
                e.wait_ge(dy.h, dy.n)
    if phases >= 3 and STOP_AFTER is None:
        ffn_phase(res2_d, None, 1)

    @block.sync
    def _(e):
        e = Trk(e)
        e.wait_ge(dy.h, dy.n)
        for s_ in (dx[0], dx[1]):
            e.wait_ge(s_.h, s_.n)

    ctx.close()
    return nc


def _block_oc(w):
    Din, Dout = w.shape
    return np.ascontiguousarray(
        w.reshape(Din // 128, 128, Dout // 128, 128).transpose(2, 1, 0, 3)
         .reshape(Dout // 128, 128, Din))


def _block_og(w, og):
    Din, Dout = w.shape
    ow = Dout // og
    return np.ascontiguousarray(
        w.reshape(Din // 128, 128, og, ow).transpose(2, 1, 0, 3)
         .reshape(og, 128, (Din // 128) * ow))


def prep_inputs(cfg, inp):
    c = cfg
    f = np.float32
    wq = np.asarray(inp["L0_attn_Wq"], f)
    wk = np.asarray(inp["L0_attn_Wk"], f)
    wv = np.asarray(inp["L0_attn_Wv"], f)
    wo = np.asarray(inp["L0_attn_Wo"], f)
    wff = np.asarray(inp["L0_ffn_weight"], f).T
    la = np.asarray(inp["L0_ffn_lora_A"], f)
    lb = np.asarray(inp["L0_ffn_lora_B"], f)
    gw1 = np.asarray(inp["L0_ffn_gw1"], f)
    gw2 = np.asarray(inp["L0_ffn_gw2"], f)
    a_cat = la.transpose(1, 0, 2).reshape(c.D, c.RC)
    b_cat = lb.reshape(c.RC, c.D)
    mp = np.zeros((c.E, 128), f)
    for e in range(c.E):
        mp[e, e * c.R:(e + 1) * c.R] = 1.0 / c.R
    d = {
        "wq_b": _block_oc(wq), "wk_b": _block_oc(wk), "wo_b": _block_oc(wo),
        "wff_b": _block_oc(wff), "wv_b": _block_og(wv, c.OG),
        "a_b": np.ascontiguousarray(
            a_cat.reshape(c.NC, 128, c.RC).transpose(1, 0, 2).reshape(128, c.NC * c.RC)),
        "b_cat": np.ascontiguousarray(b_cat.reshape(128, c.D)) if c.RC == 128 else None,
        "gw1_b": np.ascontiguousarray(
            gw1.T.reshape(c.NC, 128, c.GD).transpose(1, 0, 2).reshape(128, c.NC * c.GD)),
        "gw2_t": np.ascontiguousarray(gw2.T),
        "bq_c": np.ascontiguousarray(np.asarray(inp["L0_attn_bq"], f).reshape(c.NC, 128).T),
        "bk_c": np.ascontiguousarray(np.asarray(inp["L0_attn_bk"], f).reshape(c.NC, 128).T),
        "bo_c": np.ascontiguousarray(np.asarray(inp["L0_attn_bo"], f).reshape(c.NC, 128).T),
        "bv_r": np.asarray(inp["L0_attn_bv"], f).reshape(1, c.D).astype(ml_dtypes.bfloat16),
        "gb1_c": np.asarray(inp["L0_ffn_gb1"], f).reshape(c.GD, 1),
        "gb2_r": np.asarray(inp["L0_ffn_gb2"], f).reshape(1, c.E),
        "lng_c": np.asarray(inp["L0_ffn_ln_gamma"], f).reshape(c.GD, 1),
        "lnb_c": np.asarray(inp["L0_ffn_ln_beta"], f).reshape(c.GD, 1),
        "gq_c": np.asarray(inp["L0_attn_gq"], f).reshape(128, 1),
        "gk_c": np.asarray(inp["L0_attn_gk"], f).reshape(128, 1),
        "map_alpha": mp,
        "ident_f32": np.eye(128, dtype=f),
    }
    assert d["b_cat"] is not None, "RC must be 128"
    return d


CFG_FULL = Cfg()
_CACHE = {}


def kernel(**inputs):
    cfg = CFG_FULL
    x = np.asarray(inputs["L0_attn_hidden_states"], np.float32)[0]
    shared = prep_inputs(cfg, inputs)
    in_maps = []
    for cix in range(NCORES):
        m = dict(shared)
        m["x_fm"] = np.ascontiguousarray(x[cix * cfg.T:(cix + 1) * cfg.T, :].T)
        in_maps.append(m)
    key = "full"
    if key not in _CACHE:
        _CACHE[key] = build(cfg)
    nc = _CACHE[key]
    res = run_bass_kernel_spmd(nc, in_maps, core_ids=list(range(NCORES)))
    out = np.empty((1, NCORES * cfg.T, cfg.D), np.float32)
    for cix in range(NCORES):
        out[0, cix * cfg.T:(cix + 1) * cfg.T, :] = res.results[cix]["out_fm"].T
    return out


# version 10
# speedup vs baseline: 1.0478x; 1.0031x over previous
import math
from contextlib import ExitStack

import numpy as np
import ml_dtypes

import concourse.bass as bass
import concourse.mybir as mybir
from concourse.bass_utils import run_bass_kernel_spmd

F32 = mybir.dt.float32
BF16 = mybir.dt.bfloat16
AF = mybir.ActivationFunctionType
ALU = mybir.AluOpType
AX = mybir.AxisListType

NCORES = 8
STOP_AFTER = None


class Cfg:
    def __init__(self, D=4096, T=1024, H=32, E=8, R=16, GD=32, capfrac=3.0):
        self.D = D
        self.T = T
        self.H = H
        self.HD = 128
        self.E = E
        self.R = R
        self.RC = E * R
        self.GD = GD
        self.cap = float(int(capfrac * (NCORES * T) / E))
        self.NC = D // 128
        self.NT = T // 128
        self.NK = NCORES * T // 128
        self.NG = 4
        self.HPG = H // self.NG
        self.OG = max(8, D // 256)
        self.OGW = D // self.OG
        assert D % 128 == 0 and T % 512 == 0 and H % self.NG == 0 and H == D // 128
        assert self.OGW % 128 == 0 and (self.OGW // 128) * self.OG * 128 == D
        self.scale = 1.0 / math.sqrt(self.HD)


class Sem:
    def __init__(self, ctx, nc, name):
        self.h = ctx.enter_context(nc.semaphore(name))
        self.n = 0

    def inc(self, instr, k=1):
        instr.then_inc(self.h, k)
        self.n += k
        return self.n


_TRK_SKIP = {"nop", "engine_nop", "wait_ge", "wait_op", "drain", "sem_inc"}
_TRK_DRAIN = {"tensor_scalar", "tensor_single_scalar", "tensor_scalar_add",
              "tensor_scalar_sub", "tensor_scalar_mul", "tensor_scalar_max",
              "tensor_scalar_min", "scalar_tensor_tensor", "reciprocal"}


def _is_ap(x):
    return hasattr(x, "tensor") or hasattr(x, "ap")


class Trk:

    def __init__(self, e):
        object.__setattr__(self, "_e", e)
        object.__setattr__(self, "last", None)

    def __getattr__(self, n):
        f = getattr(self._e, n)
        if callable(f) and not n.startswith("_") and n not in _TRK_SKIP:
            is_dve = type(self._e).__name__ == "BassVectorEngine"
            def g(*a, **k):
                if is_dve and n != "dma_start" and self.last is not None:
                    self._e.drain()
                r = f(*a, **k)
                if hasattr(r, "then_inc"):
                    object.__setattr__(self, "last", r)
                return r
            return g
        return f


def build(cfg: Cfg, taps=(), detect_races=False, phases=3):
    c = cfg
    nc = bass.Bass("TRN2", target_bir_lowering=False, detect_race_conditions=detect_races)
    T, D, NC, NT, H, E, GD, RC = c.T, c.D, c.NC, c.NT, c.H, c.E, c.GD, c.RC
    HV = [(i * 512, (i + 1) * 512) for i in range(T // 512)]
    NKC = c.NK

    def xin(name, shape, dtype=F32):
        return nc.dram_tensor(name, list(shape), dtype, kind="ExternalInput")

    x_d = xin("x_fm", [D, T])
    wq_d = xin("wq_b", [NC, 128, D])
    wk_d = xin("wk_b", [NC, 128, D])
    wo_d = xin("wo_b", [NC, 128, D])
    wff_d = xin("wff_b", [NC, 128, D])
    wv_d = xin("wv_b", [c.OG, 128, NC * c.OGW])
    a_d = xin("a_b", [128, NC * RC])
    bcat_d = xin("b_cat", [128, D])
    gw1_d = xin("gw1_b", [128, NC * GD])
    gw2_d = xin("gw2_t", [GD, E])
    bq_d = xin("bq_c", [128, NC])
    bk_d = xin("bk_c", [128, NC])
    bo_d = xin("bo_c", [128, NC])
    bv_d = xin("bv_r", [1, D], BF16)
    gb1_d = xin("gb1_c", [GD, 1])
    gb2_d = xin("gb2_r", [1, E])
    lng_d = xin("lng_c", [GD, 1])
    lnb_d = xin("lnb_c", [GD, 1])
    gq_d = xin("gq_c", [128, 1])
    gk_d = xin("gk_c", [128, 1])
    map_d = xin("map_alpha", [E, 128])
    idf_d = xin("ident_f32", [128, 128])

    out_d = nc.dram_tensor("out_fm", [D, T], F32, kind="ExternalOutput")
    tap_d = {}
    for tname, tshape, tdt in taps:
        tap_d[tname] = nc.dram_tensor(tname, list(tshape), tdt, kind="ExternalOutput")

    res1_d = nc.dram_tensor("res1", [D, T], F32, kind="Internal")
    res2_d = nc.dram_tensor("res2", [D, T], F32, kind="Internal")
    ao_d = nc.dram_tensor("ao", [H * 128, T], BF16, kind="Internal")
    q_d = nc.dram_tensor("q_sp", [H * 128, T], BF16, kind="Internal")
    kg_own = [nc.dram_tensor(f"kg_own{g}", [c.HPG * 128, T], BF16, kind="Internal") for g in range(c.NG)]
    vg_own = [nc.dram_tensor(f"vg_own{g}", [c.HPG * NT * 128, 128], BF16, kind="Internal") for g in range(c.NG)]
    kg_gth = [nc.dram_tensor(f"kg_gth{g}", [NCORES * c.HPG * 128, T], BF16, kind="Internal") for g in range(c.NG)]
    vg_gth = [nc.dram_tensor(f"vg_gth{g}", [NCORES * c.HPG * NT * 128, 128], BF16, kind="Internal") for g in range(c.NG)]
    cnt_in = [nc.dram_tensor(f"cnt_in{i}", [1, 2 * E], F32, kind="Internal") for i in range(2)]
    cnt_out = [nc.dram_tensor(f"cnt_out{i}", [1, 2 * E], F32, kind="Internal") for i in range(2)]

    ctx = ExitStack()
    sb = lambda name, shape, dt: ctx.enter_context(nc.sbuf_tensor(name, list(shape), dt))

    acc_s = sb("acc_s", [128, T], F32)
    acc_q = sb("acc_q", [128, T], F32)
    tbuf = [sb(f"tchunk{i}", [128, T], F32) for i in range(2)]
    xinb = [sb(f"xin{i}", [128, T], F32) for i in range(2)]
    rout = [sb(f"rout{i}", [128, T], F32) for i in range(2)]
    sqs = sb("sqs", [128, T], F32)
    a_sb = sb("a_sb", [128, NC * RC], BF16)
    bcat_sb = sb("bcat_sb", [128, D], BF16)
    gw1_sb = sb("gw1_sb", [128, NC * GD], F32)
    gw2_sb = sb("gw2_sb", [GD, E], F32)
    bq_sb = sb("bq_sb", [128, NC], F32)
    bk_sb = sb("bk_sb", [128, NC], F32)
    bo_sb = sb("bo_sb", [128, NC], F32)
    bv_sb = sb("bv_sb", [1, D], BF16)
    gb1_sb = sb("gb1_sb", [GD, 1], F32)
    gb2row = sb("gb2row", [1, E], F32)
    gb2b_sb = sb("gb2b_sb", [128, E], F32)
    lng_sb = sb("lng_sb", [GD, 1], F32)
    lnb_sb = sb("lnb_sb", [GD, 1], F32)
    gq_sb = sb("gq_sb", [128, 1], F32)
    gk_sb = sb("gk_sb", [128, 1], F32)
    map_sb = sb("map_sb", [E, 128], F32)
    idf_sb = sb("idf_sb", [128, 128], F32)
    eps5_sb = sb("eps5_sb", [1, 1], F32)
    eps6_sb = sb("eps6_sb", [1, 1], F32)
    ones_cb = sb("ones_cb", [128, 1], BF16)
    ones_rb = sb("ones_rb", [1, 128], BF16)
    ones_cf = sb("ones_cf", [128, 1], F32)
    ones_rf = sb("ones_rf", [1, 128], F32)
    row_s = sb("row_s", [1, T], F32)
    row_q = sb("row_q", [1, T], F32)
    row_m = sb("row_m", [1, T], F32)
    row_r = sb("row_r", [1, T], F32)
    g_sb = sb("g_sb", [128, E], F32)
    gm_sb = sb("gm_sb", [128, E], F32)
    eq_sb = sb("eq_sb", [128, 2 * E * NT], F32)
    m1_sb = sb("m1_sb", [128, 1], F32)
    m2_sb = sb("m2_sb", [128, 1], F32)
    e21_sb = sb("e21_sb", [128, 1], F32)
    zden_sb = sb("zden_sb", [128, 1], F32)
    g12_sb = sb("g12_sb", [128, 2 * NT], F32)
    wz_sb = sb("wz_sb", [128, 2 * E], F32)
    w_sb = sb("w_sb", [128, E], F32)
    cnt_sb = sb("cnt_sb", [1, 2 * E], F32)
    alw_row = sb("alw_row", [1, 2 * E], F32)
    alw_sb = sb("alw_sb", [128, 2 * E], F32)
    pm_sb = sb("pm_sb", [128, 1], F32)
    mrow_sb = sb("mrow_sb", [1, 1], F32)
    brow_sb = sb("brow_sb", [1, 1], F32)
    b2row_sb = sb("b2row_sb", [1, 1], F32)
    bias_sb = sb("bias_sb", [128, 1], F32)
    qraw = [sb(f"qraw{i}", [128, T], F32) for i in range(2)]
    q2_sb = sb("q2_sb", [128, T], BF16)
    kq_out = [sb(f"kq_out{i}", [128, T], BF16) for i in range(2)]
    vout_sb = [sb(f"vout{i}", [128, c.OGW], BF16) for i in range(2)]

    mks = lambda name: Sem(ctx, nc, name)
    dx = [mks("dx0"), mks("dx1")]
    dy = mks("dy")
    gw = [mks("gw0"), mks("gw1")]
    gz = mks("gz")
    cc = mks("cc")
    tm = mks("tm")
    vm = mks("vm")
    am = mks("am")
    gm = mks("gm")

    block = ctx.enter_context(nc.Block())

    @block.sync
    def _(e):
        e = Trk(e)
        for t, d in [(bq_sb, bq_d), (bk_sb, bk_d), (bo_sb, bo_d),
                     (gb1_sb, gb1_d), (lng_sb, lng_d), (lnb_sb, lnb_d),
                     (gq_sb, gq_d), (gk_sb, gk_d), (map_sb, map_d),
                     (idf_sb, idf_d), (gw2_sb, gw2_d), (gw1_sb, gw1_d),
                     (gb2row, gb2_d)]:
            dy.n = dy.inc(e.dma_start(t[:], d.ap()), 16)

    @block.gpsimd
    def _(e):
        e = Trk(e)
        gz.inc(e.dma_start(a_sb[:], a_d.ap()), 16)
        gz.inc(e.dma_start(bcat_sb[:], bcat_d.ap()), 16)
        gz.inc(e.dma_start(bv_sb[:], bv_d.ap()), 16)
        gm.inc(e.memset(eps5_sb[:], 1e-5))
        gm.inc(e.memset(eps6_sb[:], 1e-6))
        gm.inc(e.memset(ones_cb[:], 1.0))
        gm.inc(e.memset(ones_rb[:], 1.0))
        gm.inc(e.memset(ones_cf[:], 1.0))
        gm.inc(e.memset(ones_rf[:], 1.0))
    dy_setup, gz_setup, gm_setup = dy.n, gz.n, gm.n

    @block.scalar
    def _(e):
        e = Trk(e)
        e.wait_ge(gm.h, gm_setup)

    with nc.psum_tensor("ps_gb2", [128, E], F32) as ps_gb2:
        @block.tensor
        def _(e):
            e = Trk(e)
            e.wait_ge(dy.h, dy_setup)
            e.wait_ge(gm.h, gm_setup)
            tm.inc(e.matmul(ps_gb2[:, :], ones_rf[0:1, :], gb2row[0:1, :],
                            start=True, stop=True))
        tm_gb2 = tm.n

        @block.vector
        def _(e):
            e = Trk(e)
            e.wait_ge(tm.h, tm_gb2)
            vm.inc(e.tensor_copy(gb2b_sb[:], ps_gb2[:]))

    if STOP_AFTER == "setup":
        @block.sync
        def _(e):
            e = Trk(e)
            e.wait_ge(dy.h, dy.n)
        ctx.close()
        return nc
    vm_st = {}
    dx_st = {}
    for ic in range(NC):
        par = ic % 2
        xi, dxs = xinb[par], dx[par]

        @block.sync
        def _(e, ic=ic, xi=xi, dxs=dxs):
            e = Trk(e)
            if ic >= 2:
                e.wait_ge(vm.h, vm_st[ic - 2])
            dx_st[ic] = dxs.inc(e.dma_start(xi[:], x_d.ap()[ic * 128:(ic + 1) * 128, :]), 16)

        @block.vector
        def _(e, ic=ic, xi=xi, dxs=dxs):
            e = Trk(e)
            e.wait_ge(dxs.h, dx_st[ic])
            if ic == 0:
                e.tensor_copy(acc_s[:], xi[:])
                e.tensor_mul(acc_q[:], xi[:], xi[:])
            else:
                e.tensor_add(acc_s[:], acc_s[:], xi[:])
                e.tensor_mul(sqs[:], xi[:], xi[:])
                e.tensor_add(acc_q[:], acc_q[:], sqs[:])
            vm_st[ic] = vm.inc(e.last)

    def ln_rows(ps_sum, eps_ap, nfeat):
        @block.tensor
        def _(e):
            e = Trk(e)
            e.wait_ge(vm.h, vm.n)
            for (lo, hi) in HV:
                e.matmul(ps_sum[0:1, lo:hi], ones_cf[:], acc_s[:, lo:hi], start=True, stop=True)
                e.matmul(ps_sum[0:1, T + lo:T + hi], ones_cf[:], acc_q[:, lo:hi], start=True, stop=True)
            tm.inc(e.last)
        t0 = tm.n
        if STOP_AFTER == "rowsA":
            return

        @block.vector
        def _(e):
            e = Trk(e)
            e.wait_ge(tm.h, t0)
            e.tensor_scalar_mul(row_m[:], ps_sum[0:1, 0:T], 1.0 / nfeat)
            e.tensor_scalar_mul(row_q[:], ps_sum[0:1, T:2 * T], 1.0 / nfeat)
            e.tensor_mul(row_s[:], row_m[:], row_m[:])
            e.tensor_sub(row_q[:], row_q[:], row_s[:])
            vm.inc(e.last)
        v0 = vm.n
        if STOP_AFTER == "rowsB":
            return

        @block.scalar
        def _(e):
            e = Trk(e)
            e.wait_ge(vm.h, v0)
            e.activation(row_s[:], row_q[:], AF.Ln, bias=eps_ap[0:1, 0:1])
            e.activation(row_r[:], row_s[:], AF.Exp, scale=-0.5)
            am.inc(e.last)

    def bcast_rows(ps_meanB, ps_rstdB):
        @block.tensor
        def _(e):
            e = Trk(e)
            e.wait_ge(am.h, am.n)
            for (lo, hi) in HV:
                e.matmul(ps_meanB[:, lo:hi], ones_rf[0:1, :], row_m[0:1, lo:hi], start=True, stop=True)
                e.matmul(ps_rstdB[:, lo:hi], ones_rf[0:1, :], row_r[0:1, lo:hi], start=True, stop=True)
            tm.inc(e.last)

    def ffn_phase(src_d, dst_d, icnt):
        vm_phase0 = vm.n
        tm_phase0 = tm.n
        fstk = ExitStack()
        xf = fstk.enter_context(nc.sbuf_tensor(f"xf_f{icnt}", [128, NC * T], BF16))
        h_sb = fstk.enter_context(nc.sbuf_tensor(f"h_sb_{icnt}", [GD, T], F32))
        h2_sb = fstk.enter_context(nc.sbuf_tensor(f"h2_sb_{icnt}", [GD, T], F32))
        hsq_sb = fstk.enter_context(nc.sbuf_tensor(f"hsq_sb_{icnt}", [GD, T], F32))
        wt_sb = fstk.enter_context(nc.sbuf_tensor(f"wt_sb_{icnt}", [E, T], F32))
        wtexp_sb = fstk.enter_context(nc.sbuf_tensor(f"wtexp_sb_{icnt}", [128, T], F32))
        tmps_sb = fstk.enter_context(nc.sbuf_tensor(f"tmps_sb_{icnt}", [128, T], BF16))
        if STOP_AFTER == "stats":
            fstk.close()
            return
        with nc.psum_tensor(f"ffn_sum_{icnt}", [1, 2 * T], F32) as ps_sum:
            ln_rows(ps_sum, eps5_sb, D)
        if STOP_AFTER == "rowsdbg":
            @block.sync
            def _(e):
                e = Trk(e)
                e.wait_ge(am.h, am.n)
                dy.inc(e.dma_start(out_d.ap()[0:1, :], row_m[0:1, :]), 16)
                dy.inc(e.dma_start(out_d.ap()[1:2, :], row_r[0:1, :]), 16)
                dy.inc(e.dma_start(out_d.ap()[2:3, :], row_q[0:1, :]), 16)
                dy.inc(e.dma_start(out_d.ap()[4:132, :], acc_s[:, :]), 16)
                dy.inc(e.dma_start(out_d.ap()[132:260, :], acc_q[:, :]), 16)
                e.wait_ge(dy.h, dy.n)
        if STOP_AFTER in ("rows", "rowsA", "rowsB", "rowsdbg"):
            fstk.close()
            return

        with nc.psum_tensor(f"ffn_tmp_{icnt}", [128, T], F32) as ps_tmp:
            with (nc.psum_tensor(f"ffn_meanB_{icnt}", [128, T], F32) as ps_meanB,
                  nc.psum_tensor(f"ffn_rstdB_{icnt}", [128, T], F32) as ps_rstdB,
                  nc.psum_tensor(f"ffn_h_{icnt}", [GD, T], F32) as ps_h):
                bcast_rows(ps_meanB, ps_rstdB)
                t_bc = tm.n

                vm_ch = {}
                tm_ch = {}
                dx_ch = {}
                for ic in range(NC):
                    par = ic % 2
                    xi, dxs, tb = xinb[par], dx[par], tbuf[par]

                    @block.sync
                    def _(e, ic=ic, xi=xi, dxs=dxs):
                        e = Trk(e)
                        if ic >= 2:
                            e.wait_ge(vm.h, vm_ch[ic - 2])
                        else:
                            e.wait_ge(vm.h, vm_phase0)
                        dx_ch[ic] = dxs.inc(
                            e.dma_start(xi[:], src_d.ap()[ic * 128:(ic + 1) * 128, :]), 16)

                    @block.vector
                    def _(e, ic=ic, xi=xi, dxs=dxs, tb=tb):
                        e = Trk(e)
                        e.wait_ge(dxs.h, dx_ch[ic])
                        if ic == 0:
                            e.wait_ge(tm.h, t_bc)
                        if ic >= 2:
                            e.wait_ge(tm.h, tm_ch[ic - 2])
                        e.tensor_sub(tb[:], xi[:], ps_meanB[:])
                        e.tensor_mul(xf[:, ic * T:(ic + 1) * T], tb[:], ps_rstdB[:])
                        vm_ch[ic] = vm.inc(e.last)

                    @block.tensor
                    def _(e, ic=ic, tb=tb):
                        e = Trk(e)
                        e.wait_ge(vm.h, vm_ch[ic])
                        if ic == 0:
                            e.wait_ge(gz.h, gz_setup)
                        for (lo, hi) in HV:
                            e.matmul(ps_h[:, lo:hi], gw1_sb[:, ic * GD:(ic + 1) * GD],
                                     tb[:, lo:hi], start=(ic == 0), stop=(ic == NC - 1))
                        for (lo, hi) in HV:
                            e.matmul(ps_tmp[:, lo:hi], a_sb[:, ic * RC:(ic + 1) * RC],
                                     xf[:, ic * T + lo: ic * T + hi],
                                     start=(ic == 0), stop=(ic == NC - 1))
                        tm_ch[ic] = tm.inc(e.last)
                t_hmm = tm.n

                @block.scalar
                def _(e):
                    e = Trk(e)
                    e.wait_ge(tm.h, t_hmm)
                    am.inc(e.activation(h_sb[:], ps_h[:], AF.Copy))
                a_hcp = am.n

                @block.vector
                def _(e):
                    e = Trk(e)
                    e.wait_ge(am.h, a_hcp)
                    e.tensor_mul(h2_sb[:], h_sb[:], ps_rstdB[0:GD, :])
                    e.tensor_scalar_add(h2_sb[:], h2_sb[:], gb1_sb[:, 0:1])
                    vm.inc(e.last)
            if STOP_AFTER == "norm":
                @block.gpsimd
                def _(e):
                    e = Trk(e)
                    e.wait_ge(vm.h, vm.n)
                    gz.inc(e.dma_start(out_d.ap()[0:128, :], xf[:, 0:T]), 16)
                    gz.inc(e.dma_start(out_d.ap()[128:256, :], xf[:, (NC - 1) * T:NC * T]), 16)
                @block.sync
                def _(e):
                    e = Trk(e)
                    e.wait_ge(vm.h, vm.n)
                    dy.inc(e.dma_start(out_d.ap()[256:256 + GD, :], h2_sb[:, :]), 16)
                    e.wait_ge(dy.h, dy.n)
                    e.wait_ge(gz.h, gz.n)
                fstk.close()
                return

            with nc.psum_tensor(f"g_sum_{icnt}", [1, 2 * T], F32) as ps_gsum:
                @block.vector
                def _(e):
                    e = Trk(e)
                    e.tensor_mul(hsq_sb[:], h2_sb[:], h2_sb[:])
                    vm.inc(e.last)
                v_hsq = vm.n

                @block.tensor
                def _(e):
                    e = Trk(e)
                    e.wait_ge(vm.h, v_hsq)
                    for (lo, hi) in HV:
                        e.matmul(ps_gsum[0:1, lo:hi], ones_cf[0:GD, :], h2_sb[:, lo:hi], start=True, stop=True)
                        e.matmul(ps_gsum[0:1, T + lo:T + hi], ones_cf[0:GD, :], hsq_sb[:, lo:hi], start=True, stop=True)
                    tm.inc(e.last)
                t_gs = tm.n

                @block.vector
                def _(e):
                    e = Trk(e)
                    e.wait_ge(tm.h, t_gs)
                    e.tensor_scalar_mul(row_m[:], ps_gsum[0:1, 0:T], 1.0 / GD)
                    e.tensor_scalar_mul(row_q[:], ps_gsum[0:1, T:2 * T], 1.0 / GD)
                    e.tensor_mul(row_s[:], row_m[:], row_m[:])
                    e.tensor_sub(row_q[:], row_q[:], row_s[:])
                    vm.inc(e.last)
                v_gv = vm.n

                @block.scalar
                def _(e):
                    e = Trk(e)
                    e.wait_ge(vm.h, v_gv)
                    e.activation(row_s[:], row_q[:], AF.Ln, bias=eps5_sb[0:1, 0:1])
                    e.activation(row_r[:], row_s[:], AF.Exp, scale=-0.5)
                    am.inc(e.last)

            with (nc.psum_tensor(f"g_meanB_{icnt}", [GD, T], F32) as ps_gmB,
                  nc.psum_tensor(f"g_rstdB_{icnt}", [GD, T], F32) as ps_grB):
                @block.tensor
                def _(e):
                    e = Trk(e)
                    e.wait_ge(am.h, am.n)
                    for (lo, hi) in HV:
                        e.matmul(ps_gmB[:, lo:hi], ones_rf[0:1, 0:GD], row_m[0:1, lo:hi], start=True, stop=True)
                        e.matmul(ps_grB[:, lo:hi], ones_rf[0:1, 0:GD], row_r[0:1, lo:hi], start=True, stop=True)
                    tm.inc(e.last)
                t_gb = tm.n

                @block.vector
                def _(e):
                    e = Trk(e)
                    e.wait_ge(tm.h, t_gb)
                    e.tensor_sub(h_sb[:], h2_sb[:], ps_gmB[:])
                    e.tensor_mul(h_sb[:], h_sb[:], ps_grB[:])
                    e.tensor_scalar(h_sb[:], h_sb[:], lng_sb[:, 0:1], lnb_sb[:, 0:1], ALU.mult, ALU.add)
                    e.tensor_relu(h_sb[:], h_sb[:])
                    vm.inc(e.last)
                v_hn = vm.n

            if STOP_AFTER == "gateln":
                @block.sync
                def _(e):
                    e = Trk(e)
                    e.wait_ge(vm.h, vm.n)
                    dy.inc(e.dma_start(out_d.ap()[0:GD, :], h_sb[:, :]), 16)
                    e.wait_ge(dy.h, dy.n)
                fstk.close()
                return
            with (nc.psum_tensor(f"g_ps_{icnt}", [128, E * NT], F32) as ps_g,
                  nc.psum_tensor(f"wt_ps_{icnt}", [E, 128], F32) as ps_wt,
                  nc.psum_tensor(f"cnt_ps_{icnt}", [1, 2 * E], F32) as ps_cnt,
                  nc.psum_tensor(f"alw_ps_{icnt}", [128, 2 * E], F32) as ps_alw):
                @block.tensor
                def _(e):
                    e = Trk(e)
                    e.wait_ge(vm.h, v_hn)
                    for t in range(NT):
                        e.matmul(ps_g[:, t * E:(t + 1) * E], h_sb[:, t * 128:(t + 1) * 128],
                                 gw2_sb[:, :], start=True, stop=True)
                    tm.inc(e.last)
                t_g = tm.n

                vm_g = {}
                for t in range(NT):
                    @block.vector
                    def _(e, t=t):
                        e = Trk(e)
                        e.wait_ge(tm.h, t_g)
                        eqv = eq_sb[:, t * 2 * E:(t + 1) * 2 * E]
                        e.tensor_add(g_sb[:], ps_g[:, t * E:(t + 1) * E], gb2b_sb[:])
                        e.reduce_max(m1_sb[:], g_sb[:], axis=AX.X)
                        e.tensor_single_scalar(eqv[:, 0:E], g_sb[:], m1_sb[:, 0:1], ALU.is_equal)
                        e.scalar_tensor_tensor(gm_sb[:], eqv[:, 0:E], -1e30, g_sb[:], ALU.mult, ALU.add)
                        e.reduce_max(m2_sb[:], gm_sb[:], axis=AX.X)
                        e.tensor_single_scalar(eqv[:, E:2 * E], gm_sb[:], m2_sb[:, 0:1], ALU.is_equal)
                        e.tensor_sub(m2_sb[:], m2_sb[:], m1_sb[:])
                        vm_g[t] = vm.inc(e.last)

                    @block.scalar
                    def _(e, t=t):
                        e = Trk(e)
                        e.wait_ge(vm.h, vm_g[t])
                        am.inc(e.activation(e21_sb[:], m2_sb[:], AF.Exp))
                    a_e = am.n

                    @block.vector
                    def _(e, t=t):
                        e = Trk(e)
                        e.wait_ge(am.h, a_e)
                        e.tensor_scalar_add(zden_sb[:], e21_sb[:], 1.0)
                        e.reciprocal(g12_sb[:, 2 * t:2 * t + 1], zden_sb[:])
                        e.tensor_mul(g12_sb[:, 2 * t + 1:2 * t + 2], e21_sb[:],
                                     g12_sb[:, 2 * t:2 * t + 1])
                        vm.inc(e.last)
                    v_g12 = vm.n

                    @block.tensor
                    def _(e, t=t):
                        e = Trk(e)
                        e.wait_ge(vm.h, v_g12)
                        e.matmul(ps_cnt[0:1, :], ones_cf[:, :],
                                 eq_sb[:, t * 2 * E:(t + 1) * 2 * E],
                                 start=(t == 0), stop=(t == NT - 1))
                        tm.inc(e.last)
                t_cnt = tm.n

                if STOP_AFTER == "gates":
                    @block.sync
                    def _(e):
                        e = Trk(e)
                        e.wait_ge(tm.h, tm.n)
                        e.wait_ge(vm.h, vm.n)
                        dy.inc(e.dma_start(out_d.ap()[0:128, 0:2 * E * NT], eq_sb[:, :]), 16)
                        dy.inc(e.dma_start(out_d.ap()[128:256, 0:2 * NT], g12_sb[:, :]), 16)
                        dy.inc(e.dma_start(out_d.ap()[256:384, 0:E], g_sb[:, :]), 16)
                        dy.inc(e.dma_start(out_d.ap()[384:512, 0:1], m1_sb[:, :], allow_slow_non_contiguous=True), 16)
                        dy.inc(e.dma_start(out_d.ap()[512:640, 0:1], m2_sb[:, :], allow_slow_non_contiguous=True), 16)
                        e.wait_ge(dy.h, dy.n)
                    fstk.close()
                    return
                @block.vector
                def _(e):
                    e = Trk(e)
                    e.wait_ge(tm.h, t_cnt)
                    vm.inc(e.tensor_copy(cnt_sb[:], ps_cnt[:]))
                v_cnt = vm.n

                @block.sync
                def _(e):
                    e = Trk(e)
                    e.wait_ge(vm.h, v_cnt)
                    dy.inc(e.dma_start(cnt_in[icnt].ap(), cnt_sb[:]), 16)
                d_ci = dy.n

                @block.gpsimd
                def _(e):
                    e = Trk(e)
                    e.wait_ge(dy.h, d_ci)
                    cc.inc(e.collective_compute(
                        "AllReduce", ALU.add, replica_groups=[list(range(NCORES))],
                        ins=[cnt_in[icnt].ap().opt()], outs=[cnt_out[icnt].ap().opt()]), 1)
                c_ar = cc.n

                @block.sync
                def _(e):
                    e = Trk(e)
                    e.wait_ge(cc.h, c_ar)
                    dy.inc(e.dma_start(cnt_sb[:], cnt_out[icnt].ap()), 16)
                d_co = dy.n

                @block.vector
                def _(e):
                    e = Trk(e)
                    e.wait_ge(dy.h, d_co)
                    e.tensor_single_scalar(alw_row[:], cnt_sb[:], c.cap, ALU.is_le)
                    vm.inc(e.last)
                v_alw = vm.n

                @block.tensor
                def _(e):
                    e = Trk(e)
                    e.wait_ge(vm.h, v_alw)
                    tm.inc(e.matmul(ps_alw[:, :], ones_rf[0:1, :], alw_row[0:1, :], start=True, stop=True))
                t_alw = tm.n

                @block.vector
                def _(e):
                    e = Trk(e)
                    e.wait_ge(tm.h, t_alw)
                    vm.inc(e.tensor_copy(alw_sb[:], ps_alw[:]))
                v_alwc = vm.n

                tm_wt = {}
                for t in range(NT):
                    @block.vector
                    def _(e, t=t):
                        e = Trk(e)
                        e.wait_ge(vm.h, v_alwc)
                        if t >= 1:
                            e.wait_ge(tm.h, tm_wt[t - 1])
                        eqv = eq_sb[:, t * 2 * E:(t + 1) * 2 * E]
                        e.tensor_scalar_mul(wz_sb[:, 0:E], eqv[:, 0:E], g12_sb[:, 2 * t:2 * t + 1])
                        e.tensor_scalar_mul(wz_sb[:, E:2 * E], eqv[:, E:2 * E], g12_sb[:, 2 * t + 1:2 * t + 2])
                        e.tensor_mul(wz_sb[:], wz_sb[:], alw_sb[:])
                        e.tensor_add(w_sb[:], wz_sb[:, 0:E], wz_sb[:, E:2 * E])
                        vm.inc(e.last)
                    v_w = vm.n

                    @block.tensor
                    def _(e, t=t):
                        e = Trk(e)
                        e.wait_ge(vm.h, v_w)
                        tm_wt[t] = tm.inc(e.transpose(ps_wt[:, :], w_sb[:, :], idf_sb[:, :]))

                    @block.vector
                    def _(e, t=t):
                        e = Trk(e)
                        e.wait_ge(tm.h, tm_wt[t])
                        vm.inc(e.tensor_copy(wt_sb[:, t * 128:(t + 1) * 128], ps_wt[:, :]))
                v_wt = vm.n

            if STOP_AFTER == "w":
                @block.sync
                def _(e):
                    e = Trk(e)
                    e.wait_ge(vm.h, vm.n)
                    dy.inc(e.dma_start(out_d.ap()[0:E, :], wt_sb[:, :]), 16)
                    dy.inc(e.dma_start(out_d.ap()[8:9, 0:2 * E], cnt_sb[:, :]), 16)
                    dy.inc(e.dma_start(out_d.ap()[9:10, 0:2 * E], alw_row[:, :]), 16)
                    dy.inc(e.dma_start(out_d.ap()[10:138, 0:E], w_sb[:, :]), 16)
                    dy.inc(e.dma_start(out_d.ap()[138:266, 0:2 * NT], g12_sb[:, :]), 16)
                    dy.inc(e.dma_start(out_d.ap()[266:394, 0:2 * E * NT], eq_sb[:, :]), 16)
                    dy.inc(e.dma_start(out_d.ap()[394:522, 0:2 * E], alw_sb[:, :]), 16)
                    e.wait_ge(dy.h, dy.n)
                fstk.close()
                return
            with nc.psum_tensor(f"wtexp_ps_{icnt}", [128, T], F32) as ps_we:
                @block.tensor
                def _(e):
                    e = Trk(e)
                    e.wait_ge(vm.h, v_wt)
                    for (lo, hi) in HV:
                        e.matmul(ps_we[:, lo:hi], map_sb[:, :], wt_sb[:, lo:hi], start=True, stop=True)
                    tm.inc(e.last)
                t_we = tm.n

                @block.vector
                def _(e):
                    e = Trk(e)
                    e.wait_ge(tm.h, t_we)
                    e.tensor_copy(wtexp_sb[:], ps_we[:])
                    e.tensor_mul(tmps_sb[:], ps_tmp[:], wtexp_sb[:])
                    vm.inc(e.last)
        v_tmps = vm.n

        with (nc.psum_tensor(f"main0_{icnt}", [128, T], F32) as pm0,
              nc.psum_tensor(f"main1_{icnt}", [128, T], F32) as pm1,
              nc.sbuf_tensor(f"wslf0_{icnt}", [128, NC * 128], BF16) as wsl0,
              nc.sbuf_tensor(f"wslf1_{icnt}", [128, NC * 128], BF16) as wsl1):
            wslab = [wsl0, wsl1]
            ps_main = [pm0, pm1]
            gw_oc = {}
            tm_oc = {}
            vm_oc = {}
            dxm_oc = {}
            dyr_oc = {}
            for oc in range(NC):
                par = oc % 2
                gws = gw[par]

                @block.gpsimd
                def _(e, oc=oc, par=par, gws=gws):
                    e = Trk(e)
                    if oc >= 2:
                        e.wait_ge(tm.h, tm_oc[oc - 2])
                    else:
                        e.wait_ge(tm.h, tm_phase0)
                    gw_oc[oc] = gws.inc(
                        e.dma_start(wslab[par][:], wff_d.ap()[oc, :, :]), 16)

                @block.sync
                def _(e, oc=oc, par=par):
                    e = Trk(e)
                    if oc >= 2:
                        e.wait_ge(vm.h, vm_oc[oc - 2])
                    else:
                        e.wait_ge(vm.h, vm_ch[NC - 1])
                    dxm_oc[oc] = dx[par].inc(
                        e.dma_start(xinb[par][:], src_d.ap()[oc * 128:(oc + 1) * 128, :]), 16)

                @block.tensor
                def _(e, oc=oc, par=par, gws=gws):
                    e = Trk(e)
                    e.wait_ge(gws.h, gw_oc[oc])
                    if oc == 0:
                        e.wait_ge(vm.h, v_tmps)
                    if oc >= 2:
                        e.wait_ge(vm.h, vm_oc[oc - 2])
                    for (lo, hi) in HV:
                        for ic in range(NC):
                            e.matmul(ps_main[par][:, lo:hi],
                                     wslab[par][:, ic * 128:(ic + 1) * 128],
                                     xf[:, ic * T + lo: ic * T + hi],
                                     start=(ic == 0), stop=False)
                        e.matmul(ps_main[par][:, lo:hi],
                                 bcat_sb[:, oc * 128:(oc + 1) * 128],
                                 tmps_sb[:, lo:hi], start=False, stop=True)
                    tm_oc[oc] = tm.inc(e.last)

                @block.vector
                def _(e, oc=oc, par=par):
                    e = Trk(e)
                    e.wait_ge(tm.h, tm_oc[oc])
                    e.wait_ge(dx[par].h, dxm_oc[oc])
                    if oc >= 2:
                        e.wait_ge(dy.h, dyr_oc[oc - 2])
                    e.scalar_tensor_tensor(rout[par][:], ps_main[par][:], 0.5,
                                           xinb[par][:], ALU.mult, ALU.add)
                    if dst_d is not None:
                        if oc == 0:
                            e.tensor_copy(acc_s[:], rout[par][:])
                            e.tensor_mul(acc_q[:], rout[par][:], rout[par][:])
                        else:
                            e.tensor_add(acc_s[:], acc_s[:], rout[par][:])
                            e.tensor_mul(sqs[:], rout[par][:], rout[par][:])
                            e.tensor_add(acc_q[:], acc_q[:], sqs[:])
                    vm_oc[oc] = vm.inc(e.last)

                @block.sync
                def _(e, oc=oc, par=par):
                    e = Trk(e)
                    e.wait_ge(vm.h, vm_oc[oc])
                    tgt = dst_d if dst_d is not None else out_d
                    dyr_oc[oc] = dy.inc(e.dma_start(tgt.ap()[oc * 128:(oc + 1) * 128, :], rout[par][:]), 16)
        fstk.close()

    def qkv_attn_phase():
        vm_phase0 = vm.n
        qstk = ExitStack()
        xf = qstk.enter_context(nc.sbuf_tensor("xf_qkv", [128, NC * T], BF16))
        with nc.psum_tensor("ln2_sum", [1, 2 * T], F32) as ps_sum:
            ln_rows(ps_sum, eps5_sb, D)
        with (nc.psum_tensor("ln2_meanB", [128, T], F32) as ps_meanB,
              nc.psum_tensor("ln2_rstdB", [128, T], F32) as ps_rstdB):
            bcast_rows(ps_meanB, ps_rstdB)
            t_bc = tm.n
            vm_ch = {}
            dx_ch = {}
            for ic in range(NC):
                par = ic % 2
                xi, dxs, tb = xinb[par], dx[par], tbuf[par]

                @block.sync
                def _(e, ic=ic, xi=xi, dxs=dxs):
                    e = Trk(e)
                    if ic >= 2:
                        e.wait_ge(vm.h, vm_ch[ic - 2])
                    else:
                        e.wait_ge(vm.h, vm_phase0)
                    dx_ch[ic] = dxs.inc(
                        e.dma_start(xi[:], res1_d.ap()[ic * 128:(ic + 1) * 128, :]), 16)

                @block.vector
                def _(e, ic=ic, xi=xi, dxs=dxs, tb=tb):
                    e = Trk(e)
                    e.wait_ge(dxs.h, dx_ch[ic])
                    if ic == 0:
                        e.wait_ge(tm.h, t_bc)
                    e.tensor_sub(tb[:], xi[:], ps_meanB[:])
                    e.tensor_mul(xf[:, ic * T:(ic + 1) * T], tb[:], ps_rstdB[:])
                    vm_ch[ic] = vm.inc(e.last)
        v_xf2 = vm.n

        def kq_loop(w_dram, bias_sbuf, gain_sbuf, dst, eps_ap, tag):
            tm_kq0 = tm.n
            with (nc.psum_tensor(f"kq0_{tag}", [128, T], F32) as p0,
                  nc.psum_tensor(f"kq1_{tag}", [128, T], F32) as p1,
                  nc.psum_tensor(f"kq_ss_{tag}", [1, T], F32) as ps_ss,
                  nc.psum_tensor(f"kq_rb_{tag}", [128, T], F32) as ps_rb,
                  nc.sbuf_tensor(f"wslk0_{tag}", [128, NC * 128], BF16) as wsl0,
                  nc.sbuf_tensor(f"wslk1_{tag}", [128, NC * 128], BF16) as wsl1):
                wslab = [wsl0, wsl1]
                ps_qk = [p0, p1]
                gw_oc, tm_oc, am_oc, vm_oc = {}, {}, {}, {}
                tm_rb, vm_fin, dy_fin = {}, {}, {}
                for oc in range(H):
                    par = oc % 2
                    gws = gw[par]

                    @block.gpsimd
                    def _(e, oc=oc, par=par, gws=gws):
                        e = Trk(e)
                        if oc >= 2:
                            e.wait_ge(tm.h, tm_oc[oc - 2])
                        else:
                            e.wait_ge(tm.h, tm_kq0)
                        gw_oc[oc] = gws.inc(e.dma_start(wslab[par][:], w_dram.ap()[oc, :, :]), 16)

                    @block.tensor
                    def _(e, oc=oc, par=par, gws=gws):
                        e = Trk(e)
                        e.wait_ge(gws.h, gw_oc[oc])
                        if oc == 0:
                            e.wait_ge(vm.h, v_xf2)
                        if oc >= 2:
                            e.wait_ge(am.h, am_oc[oc - 2])
                        for (lo, hi) in HV:
                            for ic in range(NC):
                                e.matmul(ps_qk[par][:, lo:hi],
                                         wslab[par][:, ic * 128:(ic + 1) * 128],
                                         xf[:, ic * T + lo: ic * T + hi],
                                         start=(ic == 0), stop=(ic == NC - 1))
                        tm_oc[oc] = tm.inc(e.last)

                    @block.scalar
                    def _(e, oc=oc, par=par):
                        e = Trk(e)
                        e.wait_ge(tm.h, tm_oc[oc])
                        if oc >= 2:
                            e.wait_ge(vm.h, vm_fin[oc - 2])
                        am_oc[oc] = am.inc(e.activation(
                            qraw[par][:], ps_qk[par][:], AF.Identity, bias=bias_sbuf[:, oc:oc + 1]))

                    @block.vector
                    def _(e, oc=oc, par=par):
                        e = Trk(e)
                        e.wait_ge(am.h, am_oc[oc])
                        if oc >= 1:
                            e.wait_ge(vm.h, vm_fin.get(oc - 1, 0))
                        e.tensor_mul(q2_sb[:], qraw[par][:], qraw[par][:])
                        vm_oc[oc] = vm.inc(e.last)

                    @block.tensor
                    def _(e, oc=oc):
                        e = Trk(e)
                        e.wait_ge(vm.h, vm_oc[oc])
                        for (lo, hi) in HV:
                            e.matmul(ps_ss[0:1, lo:hi], ones_cb[:, :], q2_sb[:, lo:hi],
                                     start=True, stop=True)
                        tm.inc(e.last)
                    t_ss = tm.n

                    @block.vector
                    def _(e, oc=oc):
                        e = Trk(e)
                        e.wait_ge(tm.h, t_ss)
                        e.tensor_scalar_mul(row_m[:], ps_ss[0:1, 0:T], 1.0 / 128.0)
                        vm.inc(e.last)
                    v_ms = vm.n

                    @block.scalar
                    def _(e, oc=oc):
                        e = Trk(e)
                        e.wait_ge(vm.h, v_ms)
                        e.activation(row_s[:], row_m[:], AF.Ln, bias=eps_ap[0:1, 0:1])
                        e.activation(row_r[:], row_s[:], AF.Exp, scale=-0.5)
                        am.inc(e.last)
                    a_rs = am.n

                    @block.tensor
                    def _(e, oc=oc):
                        e = Trk(e)
                        e.wait_ge(am.h, a_rs)
                        for (lo, hi) in HV:
                            e.matmul(ps_rb[:, lo:hi], ones_rf[0:1, :], row_r[0:1, lo:hi],
                                     start=True, stop=True)
                        tm_rb[oc] = tm.inc(e.last)

                    @block.vector
                    def _(e, oc=oc, par=par):
                        e = Trk(e)
                        e.wait_ge(tm.h, tm_rb[oc])
                        if oc >= 2:
                            e.wait_ge(dy.h, dy_fin.get(oc - 2, 0))
                        e.tensor_mul(qraw[par][:], qraw[par][:], ps_rb[:])
                        e.tensor_scalar_mul(kq_out[par][:], qraw[par][:], gain_sbuf[:, 0:1])
                        vm_fin[oc] = vm.inc(e.last)

                    @block.sync
                    def _(e, oc=oc, par=par):
                        e = Trk(e)
                        e.wait_ge(vm.h, vm_fin[oc])
                        if dst[0] == "dram":
                            g = oc // c.HPG
                            r0 = (oc % c.HPG) * 128
                            dy_fin[oc] = dy.inc(e.dma_start(dst[1][g].ap()[r0:r0 + 128, :], kq_out[par][:]), 16)
                        else:
                            dy_fin[oc] = dy.inc(e.dma_start(dst[1].ap()[oc * 128:(oc + 1) * 128, :], kq_out[par][:]), 16)
            return dict(vm_fin=vm_fin)

        kq_loop(wk_d, bk_sb, gk_sb, ("dram", kg_own), eps6_sb, "k")
        dy_after_k = dy.n

        with ExitStack() as vctx:
            wvslab = [vctx.enter_context(nc.sbuf_tensor(f"wvslab{i}", [128, NC * c.OGW], BF16))
                      for i in range(2)]
            ps_v = vctx.enter_context(nc.psum_tensor("ps_v", [128, NT * 512], F32))
            tm_v0 = tm.n
            gw_og, tm_og, am_ot, dy_v = {}, {}, {}, {}
            for og in range(c.OG):
                par = og % 2

                @block.gpsimd
                def _(e, og=og, par=par):
                    e = Trk(e)
                    if og >= 2:
                        e.wait_ge(tm.h, tm_og[og - 2])
                    else:
                        e.wait_ge(tm.h, tm_v0)
                    gw_og[og] = gw[par].inc(e.dma_start(wvslab[par][:], wv_d.ap()[og, :, :]), 16)

                @block.tensor
                def _(e, og=og, par=par):
                    e = Trk(e)
                    e.wait_ge(gw[par].h, gw_og[og])
                    if og == 0:
                        e.wait_ge(vm.h, v_xf2)
                        e.wait_ge(gm.h, gm_setup)
                    if og >= 1:
                        e.wait_ge(am.h, am_ot[(og - 1, NT - 1)])
                    for tt in range(NT):
                        for ic in range(NC):
                            e.matmul(ps_v[:, tt * 512:tt * 512 + c.OGW],
                                     xf[:, ic * T + tt * 128: ic * T + (tt + 1) * 128],
                                     wvslab[par][:, ic * c.OGW:(ic + 1) * c.OGW],
                                     start=(ic == 0), stop=False)
                        e.matmul(ps_v[:, tt * 512:tt * 512 + c.OGW],
                                 ones_rb[0:1, :], bv_sb[0:1, og * c.OGW:(og + 1) * c.OGW],
                                 start=False, stop=True)
                        tm_og[og] = tm.inc(e.last)

                for tt in range(NT):
                    @block.scalar
                    def _(e, og=og, tt=tt):
                        e = Trk(e)
                        e.wait_ge(tm.h, tm_og[og] - (NT - 1 - tt))
                        if (og, tt) != (0, 0):
                            prev = (og, tt - 1) if tt > 0 else (og - 1, NT - 1)
                            e.wait_ge(dy.h, dy_v.get(prev, 0))
                        am_ot[(og, tt)] = am.inc(e.activation(
                            vout_sb[tt % 2][:, 0:c.OGW], ps_v[:, tt * 512:tt * 512 + c.OGW], AF.Copy))

                    @block.sync
                    def _(e, og=og, tt=tt):
                        e = Trk(e)
                        e.wait_ge(am.h, am_ot[(og, tt)])
                        g = og // (c.OG // c.NG)
                        hg0 = (og % (c.OG // c.NG)) * (c.OGW // 128)
                        nj = c.OGW // 128
                        dst4 = vg_own[g].ap().rearrange("(a b p) d -> p a b d", b=NT, p=128)
                        src = vout_sb[tt % 2][:, 0:c.OGW].rearrange("p (j d) -> p j d", d=128)
                        dy_v[(og, tt)] = dy.inc(
                            e.dma_start(dst4[:, hg0:hg0 + nj, tt, :], src), 16)

            ogpg = c.OG // c.NG
            cc_after_g = {}

            @block.gpsimd
            def _(e):
                e = Trk(e)
                e.wait_ge(dy.h, dy_after_k)
                for g in range(c.NG):
                    e.wait_ge(dy.h, dy_v[(g * ogpg + ogpg - 1, NT - 1)])
                    cc.inc(e.collective_compute(
                        "AllGather", ALU.bypass, replica_groups=[list(range(NCORES))],
                        ins=[kg_own[g].ap().opt()], outs=[kg_gth[g].ap().opt()]), 1)
                    cc.inc(e.collective_compute(
                        "AllGather", ALU.bypass, replica_groups=[list(range(NCORES))],
                        ins=[vg_own[g].ap().opt()], outs=[vg_gth[g].ap().opt()]), 1)
                    cc_after_g[g] = cc.n

        kq_loop(wq_d, bq_sb, gq_sb, ("dram1", q_d), eps6_sb, "q")
        dy_qdone = dy.n
        qstk.close()

        if True:
            with ExitStack() as actx:
                kh = [actx.enter_context(nc.sbuf_tensor(f"kh{i}", [128, NCORES * T], BF16)) for i in range(2)]
                vh = [actx.enter_context(nc.sbuf_tensor(f"vh{i}", [128, NKC * 128], BF16)) for i in range(2)]
                qh = [actx.enter_context(nc.sbuf_tensor(f"qh{i}", [128, T], BF16)) for i in range(2)]
                pt_sb = actx.enter_context(nc.sbuf_tensor("pt_sb", [128, 3 * T], BF16))
                invl_sb = actx.enter_context(nc.sbuf_tensor("invl_sb", [1, T], F32))
                invlb_sb = actx.enter_context(nc.sbuf_tensor("invlb_sb", [128, T], F32))
                lacc = actx.enter_context(nc.sbuf_tensor("lacc", [128, T], F32))
                ao_sb = [actx.enter_context(nc.sbuf_tensor(f"ao_sb{i}", [128, T], BF16)) for i in range(2)]
                ps_s = [actx.enter_context(nc.psum_tensor(f"att_s{i}", [128, T], F32)) for i in range(3)]
                ps_o = actx.enter_context(nc.psum_tensor("att_o", [128, T], F32))
                pg = mks("pg")
                LWD = (3 * T) // 4

                tm_hd = {}
                vm_hd = {}
                dxk = {}
                dxv = {}
                dy_ao = {}
                for h in range(H):
                    par = h % 2
                    g = h // c.HPG
                    hh = h % c.HPG
                    tm0 = tm.n
                    vm0 = vm.n
                    am0 = am.n
                    pg0 = pg.n

                    @block.sync
                    def _(e, h=h, par=par, g=g, hh=hh):
                        e = Trk(e)
                        e.wait_ge(cc.h, cc_after_g[g])
                        if h >= 2:
                            e.wait_ge(tm.h, tm_hd[h - 2])
                        src = kg_gth[g].ap().rearrange("(r a p) t -> p r a t", a=c.HPG, p=128)
                        dst = kh[par][:].rearrange("p (r t) -> p r t", t=T)
                        dxk[h] = dx[par].inc(e.dma_start(dst, src[:, :, hh, :]), 16)
                        e.wait_ge(dy.h, dy_qdone)
                        dxk[h] = dx[par].inc(
                            e.dma_start(qh[par][:], q_d.ap()[h * 128:(h + 1) * 128, :]), 16)
                        vsrc = vg_gth[g].ap().rearrange("(r a b p) d -> p r a b d",
                                                        a=c.HPG, b=NT, p=128)
                        vdst = vh[par][:].rearrange("p (r b d) -> p r b d", b=NT, d=128)
                        for r in range(NCORES):
                            dxv[h] = dx[par].inc(
                                e.dma_start(vdst[:, r, :, :], vsrc[:, r, hh, :, :]), 16)

                    @block.tensor
                    def _(e, h=h, par=par, tm0=tm0, vm0=vm0, am0=am0, pg0=pg0):
                        e = Trk(e)
                        e.wait_ge(dx[par].h, dxk[h])
                        e.wait_ge(am.h, am0)
                        qv = qh[par]
                        for (lo, hi) in HV:
                            e.matmul(ps_s[0][:, lo:hi], kh[par][:, 0:128], qv[:, lo:hi],
                                     start=True, stop=True)
                        tm.inc(e.last)
                        for (lo, hi) in HV:
                            e.matmul(ps_s[1][:, lo:hi], kh[par][:, 128:256], qv[:, lo:hi],
                                     start=True, stop=True)
                        tm.inc(e.last)
                        if h >= 1:
                            e.wait_ge(vm.h, vm_hd[h - 1])
                        e.wait_ge(vm.h, vm0 + 1)
                        tm.inc(e.transpose(ps_o[0:1, 0:128], pm_sb[:, :], idf_sb[:, :]))
                        e.wait_ge(vm.h, vm0 + 2)
                        tm.inc(e.matmul(ps_o[:, 0:1], ones_rf[0:1, :], brow_sb[0:1, 0:1],
                                        start=True, stop=True))
                        for ci in range(NKC + 1):
                            e.wait_ge(am.h, am0 + ci)
                            if ci == 1:
                                e.wait_ge(dx[par].h, dxv[h])
                            if ci >= 1:
                                cm = ci - 1
                                ptv = pt_sb[:, (cm % 3) * T:(cm % 3 + 1) * T]
                                for (lo, hi) in HV:
                                    e.matmul(ps_o[:, lo:hi], vh[par][:, cm * 128:(cm + 1) * 128],
                                             ptv[:, lo:hi], start=(cm == 0), stop=(cm == NKC - 1))
                            if ci + 2 <= NKC - 1:
                                if ci == 0 and h >= 1:
                                    e.wait_ge(vm.h, vm_hd[h - 1] - 1)
                                for (lo, hi) in HV:
                                    e.matmul(ps_s[(ci + 2) % 3][:, lo:hi],
                                             kh[par][:, (ci + 2) * 128:(ci + 3) * 128],
                                             qv[:, lo:hi], start=True, stop=True)
                            tm.inc(e.last)
                        e.wait_ge(vm.h, vm0 + 3 + NKC)
                        e.wait_ge(pg.h, pg0 + NKC)
                        for (lo, hi) in HV:
                            e.matmul(ps_s[2][0:1, lo:hi], ones_cf[:, :], lacc[:, lo:hi],
                                     start=True, stop=True)
                        tm.inc(e.last)
                        e.wait_ge(vm.h, vm0 + 4 + NKC)
                        for (lo, hi) in HV:
                            e.matmul(ps_s[2][:, lo:hi], ones_rf[0:1, :], invl_sb[0:1, lo:hi],
                                     start=True, stop=True)
                        tm_hd[h] = tm.inc(e.last)
                        assert tm_hd[h] == tm0 + NKC + 7

                    @block.scalar
                    def _(e, h=h, tm0=tm0, vm0=vm0, pg0=pg0):
                        e = Trk(e)
                        for cix in range(NKC):
                            if cix < 2:
                                e.wait_ge(tm.h, tm0 + cix + 1)
                            else:
                                e.wait_ge(tm.h, tm0 + cix + 3)
                            if cix == 0:
                                e.wait_ge(vm.h, vm0 + 3)
                            if cix >= 3:
                                e.wait_ge(vm.h, vm0 + cix + 1)
                                e.wait_ge(pg.h, pg0 + cix - 2)
                            ptv = pt_sb[:, (cix % 3) * T:(cix % 3 + 1) * T]
                            am.inc(e.activation(ptv[:], ps_s[cix % 3][:], AF.Exp,
                                                bias=bias_sb[:, 0:1], scale=c.scale))

                    @block.gpsimd
                    def _(e, h=h, am0=am0):
                        e = Trk(e)
                        for cix in range(NKC):
                            e.wait_ge(am.h, am0 + cix + 1)
                            ptv = pt_sb[:, (cix % 3) * T:(cix % 3 + 1) * T]
                            if cix == 0:
                                pg.inc(e.tensor_copy(lacc[:, LWD:T], ptv[:, LWD:T]))
                            else:
                                pg.inc(e.tensor_tensor(lacc[:, LWD:T], lacc[:, LWD:T],
                                                       ptv[:, LWD:T], ALU.add))

                    @block.vector
                    def _(e, h=h, par=par, tm0=tm0, vm0=vm0, am0=am0):
                        e = Trk(e)
                        e.wait_ge(tm.h, tm0 + 1)
                        e.reduce_max(pm_sb[:], ps_s[0][:], axis=AX.X)
                        vm.inc(e.last)
                        e.wait_ge(tm.h, tm0 + 3)
                        e.reduce_max(mrow_sb[:], ps_o[0:1, 0:128], axis=AX.X)
                        e.tensor_scalar_mul(brow_sb[:], mrow_sb[:], 0.25)
                        e.tensor_scalar_mul(b2row_sb[:], mrow_sb[:], -0.25)
                        e.tensor_max(brow_sb[:], brow_sb[:], b2row_sb[:])
                        e.tensor_add(brow_sb[:], brow_sb[:], mrow_sb[:])
                        e.tensor_scalar_add(brow_sb[:], brow_sb[:], 10.0)
                        e.tensor_scalar_mul(brow_sb[:], brow_sb[:], -c.scale)
                        vm.inc(e.last)
                        e.wait_ge(tm.h, tm0 + 4)
                        e.tensor_copy(bias_sb[:], ps_o[:, 0:1])
                        vm.inc(e.last)
                        for cix in range(NKC):
                            e.wait_ge(am.h, am0 + cix + 1)
                            ptv = pt_sb[:, (cix % 3) * T:(cix % 3 + 1) * T]
                            if cix == 0:
                                vm.inc(e.tensor_copy(lacc[:, 0:LWD], ptv[:, 0:LWD]))
                            else:
                                vm.inc(e.tensor_add(lacc[:, 0:LWD], lacc[:, 0:LWD],
                                                    ptv[:, 0:LWD]))
                        e.wait_ge(tm.h, tm0 + NKC + 6)
                        e.reciprocal(invl_sb[:], ps_s[2][0:1, :])
                        vm.inc(e.last)
                        e.wait_ge(tm.h, tm0 + NKC + 7)
                        e.tensor_copy(invlb_sb[:], ps_s[2][:])
                        vm.inc(e.last)
                        if h >= 2:
                            e.wait_ge(dy.h, dy_ao[h - 2])
                        e.tensor_mul(ao_sb[par][:], ps_o[:], invlb_sb[:])
                        vm_hd[h] = vm.inc(e.last)
                        assert vm_hd[h] == vm0 + NKC + 6

                    @block.sync
                    def _(e, h=h, par=par):
                        e = Trk(e)
                        e.wait_ge(vm.h, vm_hd[h])
                        dy_ao[h] = dy.inc(e.dma_start(ao_d.ap()[h * 128:(h + 1) * 128, :], ao_sb[par][:]), 16)
                dy_after_ao = dy.n

        with ExitStack() as octx:
            aob = octx.enter_context(nc.sbuf_tensor("aob", [128, H * T], BF16))
            p0 = octx.enter_context(nc.psum_tensor("op0", [128, T], F32))
            p1 = octx.enter_context(nc.psum_tensor("op1", [128, T], F32))
            wslab = [octx.enter_context(nc.sbuf_tensor(f"wslo{i}", [128, NC * 128], BF16)) for i in range(2)]
            ps_op = [p0, p1]

            @block.sync
            def _(e):
                e = Trk(e)
                e.wait_ge(dy.h, dy_after_ao)
                for af in range(H):
                    dy.inc(e.dma_start(aob[:, af * T:(af + 1) * T],
                                       ao_d.ap()[af * 128:(af + 1) * 128, :]), 16)
            dy_aob = dy.n

            gw_oc, tm_oc, am_oc, vm_oc, dxo_oc, dyr_oc = {}, {}, {}, {}, {}, {}
            tm_op0 = tm.n
            vm_op0 = vm.n
            for oc in range(NC):
                par = oc % 2

                @block.gpsimd
                def _(e, oc=oc, par=par):
                    e = Trk(e)
                    if oc >= 2:
                        e.wait_ge(tm.h, tm_oc[oc - 2])
                    else:
                        e.wait_ge(tm.h, tm_op0)
                    gw_oc[oc] = gw[par].inc(e.dma_start(wslab[par][:], wo_d.ap()[oc, :, :]), 16)

                @block.sync
                def _(e, oc=oc, par=par):
                    e = Trk(e)
                    if oc >= 2:
                        e.wait_ge(vm.h, vm_oc[oc - 2])
                    else:
                        e.wait_ge(vm.h, vm_op0)
                    dxo_oc[oc] = dx[par].inc(
                        e.dma_start(xinb[par][:], res1_d.ap()[oc * 128:(oc + 1) * 128, :]), 16)

                @block.tensor
                def _(e, oc=oc, par=par):
                    e = Trk(e)
                    e.wait_ge(gw[par].h, gw_oc[oc])
                    if oc == 0:
                        e.wait_ge(dy.h, dy_aob)
                    if oc >= 2:
                        e.wait_ge(am.h, am_oc[oc - 2])
                    for (lo, hi) in HV:
                        for af in range(H):
                            e.matmul(ps_op[par][:, lo:hi],
                                     wslab[par][:, af * 128:(af + 1) * 128],
                                     aob[:, af * T + lo: af * T + hi],
                                     start=(af == 0), stop=(af == H - 1))
                    tm_oc[oc] = tm.inc(e.last)

                @block.scalar
                def _(e, oc=oc, par=par):
                    e = Trk(e)
                    e.wait_ge(tm.h, tm_oc[oc])
                    if oc >= 2:
                        e.wait_ge(vm.h, vm_oc[oc - 2])
                    am_oc[oc] = am.inc(e.activation(
                        qraw[par][:], ps_op[par][:], AF.Identity, bias=bo_sb[:, oc:oc + 1]))

                @block.vector
                def _(e, oc=oc, par=par):
                    e = Trk(e)
                    e.wait_ge(am.h, am_oc[oc])
                    e.wait_ge(dx[par].h, dxo_oc[oc])
                    if oc >= 2:
                        e.wait_ge(dy.h, dyr_oc[oc - 2])
                    e.tensor_add(rout[par][:], qraw[par][:], xinb[par][:])
                    if oc == 0:
                        e.tensor_copy(acc_s[:], rout[par][:])
                        e.tensor_mul(acc_q[:], rout[par][:], rout[par][:])
                    else:
                        e.tensor_add(acc_s[:], acc_s[:], rout[par][:])
                        e.tensor_mul(sqs[:], rout[par][:], rout[par][:])
                        e.tensor_add(acc_q[:], acc_q[:], sqs[:])
                    vm_oc[oc] = vm.inc(e.last)

                @block.sync
                def _(e, oc=oc, par=par):
                    e = Trk(e)
                    e.wait_ge(vm.h, vm_oc[oc])
                    dyr_oc[oc] = dy.inc(e.dma_start(res2_d.ap()[oc * 128:(oc + 1) * 128, :], rout[par][:]), 16)

    if phases == 1:
        ffn_phase(x_d, None, 0)
    else:
        ffn_phase(x_d, res1_d, 0)
    if phases >= 2 and STOP_AFTER is None:
        qkv_attn_phase()
    if phases == 2 and STOP_AFTER is None:
        @block.sync
        def _(e):
            e = Trk(e)
            e.wait_ge(dy.h, dy.n)
            for ic in range(NC):
                dy.inc(e.dma_start(xinb[0][:], res2_d.ap()[ic * 128:(ic + 1) * 128, :]), 16)
                e.wait_ge(dy.h, dy.n)
                dy.inc(e.dma_start(out_d.ap()[ic * 128:(ic + 1) * 128, :], xinb[0][:]), 16)
                e.wait_ge(dy.h, dy.n)
    if phases >= 3 and STOP_AFTER is None:
        ffn_phase(res2_d, None, 1)

    @block.sync
    def _(e):
        e = Trk(e)
        e.wait_ge(dy.h, dy.n)
        for s_ in (dx[0], dx[1]):
            e.wait_ge(s_.h, s_.n)

    ctx.close()
    return nc


def _block_oc(w):
    Din, Dout = w.shape
    return np.ascontiguousarray(
        w.reshape(Din // 128, 128, Dout // 128, 128).transpose(2, 1, 0, 3)
         .reshape(Dout // 128, 128, Din))


def _block_og(w, og):
    Din, Dout = w.shape
    ow = Dout // og
    return np.ascontiguousarray(
        w.reshape(Din // 128, 128, og, ow).transpose(2, 1, 0, 3)
         .reshape(og, 128, (Din // 128) * ow))


def prep_inputs(cfg, inp):
    c = cfg
    f = np.float32
    wq = np.asarray(inp["L0_attn_Wq"], f)
    wk = np.asarray(inp["L0_attn_Wk"], f)
    wv = np.asarray(inp["L0_attn_Wv"], f)
    wo = np.asarray(inp["L0_attn_Wo"], f)
    wff = np.asarray(inp["L0_ffn_weight"], f).T
    la = np.asarray(inp["L0_ffn_lora_A"], f)
    lb = np.asarray(inp["L0_ffn_lora_B"], f)
    gw1 = np.asarray(inp["L0_ffn_gw1"], f)
    gw2 = np.asarray(inp["L0_ffn_gw2"], f)
    a_cat = la.transpose(1, 0, 2).reshape(c.D, c.RC)
    b_cat = lb.reshape(c.RC, c.D)
    mp = np.zeros((c.E, 128), f)
    for e in range(c.E):
        mp[e, e * c.R:(e + 1) * c.R] = 1.0 / c.R
    d = {
        "wq_b": _block_oc(wq), "wk_b": _block_oc(wk), "wo_b": _block_oc(wo),
        "wff_b": _block_oc(wff), "wv_b": _block_og(wv, c.OG),
        "a_b": np.ascontiguousarray(
            a_cat.reshape(c.NC, 128, c.RC).transpose(1, 0, 2).reshape(128, c.NC * c.RC)),
        "b_cat": np.ascontiguousarray(b_cat.reshape(128, c.D)) if c.RC == 128 else None,
        "gw1_b": np.ascontiguousarray(
            gw1.T.reshape(c.NC, 128, c.GD).transpose(1, 0, 2).reshape(128, c.NC * c.GD)),
        "gw2_t": np.ascontiguousarray(gw2.T),
        "bq_c": np.ascontiguousarray(np.asarray(inp["L0_attn_bq"], f).reshape(c.NC, 128).T),
        "bk_c": np.ascontiguousarray(np.asarray(inp["L0_attn_bk"], f).reshape(c.NC, 128).T),
        "bo_c": np.ascontiguousarray(np.asarray(inp["L0_attn_bo"], f).reshape(c.NC, 128).T),
        "bv_r": np.asarray(inp["L0_attn_bv"], f).reshape(1, c.D).astype(ml_dtypes.bfloat16),
        "gb1_c": np.asarray(inp["L0_ffn_gb1"], f).reshape(c.GD, 1),
        "gb2_r": np.asarray(inp["L0_ffn_gb2"], f).reshape(1, c.E),
        "lng_c": np.asarray(inp["L0_ffn_ln_gamma"], f).reshape(c.GD, 1),
        "lnb_c": np.asarray(inp["L0_ffn_ln_beta"], f).reshape(c.GD, 1),
        "gq_c": np.asarray(inp["L0_attn_gq"], f).reshape(128, 1),
        "gk_c": np.asarray(inp["L0_attn_gk"], f).reshape(128, 1),
        "map_alpha": mp,
        "ident_f32": np.eye(128, dtype=f),
    }
    assert d["b_cat"] is not None, "RC must be 128"
    return d


CFG_FULL = Cfg()
_CACHE = {}


def kernel(**inputs):
    cfg = CFG_FULL
    x = np.asarray(inputs["L0_attn_hidden_states"], np.float32)[0]
    shared = prep_inputs(cfg, inputs)
    in_maps = []
    for cix in range(NCORES):
        m = dict(shared)
        m["x_fm"] = np.ascontiguousarray(x[cix * cfg.T:(cix + 1) * cfg.T, :].T)
        in_maps.append(m)
    key = "full"
    if key not in _CACHE:
        _CACHE[key] = build(cfg)
    nc = _CACHE[key]
    res = run_bass_kernel_spmd(nc, in_maps, core_ids=list(range(NCORES)))
    out = np.empty((1, NCORES * cfg.T, cfg.D), np.float32)
    for cix in range(NCORES):
        out[0, cix * cfg.T:(cix + 1) * cfg.T, :] = res.results[cix]["out_fm"].T
    return out


# version 11
# speedup vs baseline: 1.1336x; 1.0818x over previous
import math
from contextlib import ExitStack

import numpy as np
import ml_dtypes

import concourse.bass as bass
import concourse.mybir as mybir
from concourse.bass_utils import run_bass_kernel_spmd

F32 = mybir.dt.float32
BF16 = mybir.dt.bfloat16
AF = mybir.ActivationFunctionType
ALU = mybir.AluOpType
AX = mybir.AxisListType

NCORES = 8
STOP_AFTER = None


class Cfg:
    def __init__(self, D=4096, T=1024, H=32, E=8, R=16, GD=32, capfrac=3.0):
        self.D = D
        self.T = T
        self.H = H
        self.HD = 128
        self.E = E
        self.R = R
        self.RC = E * R
        self.GD = GD
        self.cap = float(int(capfrac * (NCORES * T) / E))
        self.NC = D // 128
        self.NT = T // 128
        self.NK = NCORES * T // 128
        self.NG = 4
        self.HPG = H // self.NG
        self.OG = max(8, D // 256)
        self.OGW = D // self.OG
        assert D % 128 == 0 and T % 512 == 0 and H % self.NG == 0 and H == D // 128
        assert self.OGW % 128 == 0 and (self.OGW // 128) * self.OG * 128 == D
        self.scale = 1.0 / math.sqrt(self.HD)


class Sem:
    def __init__(self, ctx, nc, name):
        self.h = ctx.enter_context(nc.semaphore(name))
        self.n = 0

    def inc(self, instr, k=1):
        instr.then_inc(self.h, k)
        self.n += k
        return self.n


_TRK_SKIP = {"nop", "engine_nop", "wait_ge", "wait_op", "drain", "sem_inc"}
_TRK_DRAIN = {"tensor_scalar", "tensor_single_scalar", "tensor_scalar_add",
              "tensor_scalar_sub", "tensor_scalar_mul", "tensor_scalar_max",
              "tensor_scalar_min", "scalar_tensor_tensor", "reciprocal"}


def _is_ap(x):
    return hasattr(x, "tensor") or hasattr(x, "ap")


class Trk:

    def __init__(self, e):
        object.__setattr__(self, "_e", e)
        object.__setattr__(self, "last", None)

    def __getattr__(self, n):
        f = getattr(self._e, n)
        if callable(f) and not n.startswith("_") and n not in _TRK_SKIP:
            is_dve = type(self._e).__name__ == "BassVectorEngine"
            def g(*a, **k):
                if is_dve and n != "dma_start" and self.last is not None:
                    self._e.drain()
                r = f(*a, **k)
                if hasattr(r, "then_inc"):
                    object.__setattr__(self, "last", r)
                return r
            return g
        return f


def build(cfg: Cfg, taps=(), detect_races=False, phases=3):
    c = cfg
    nc = bass.Bass("TRN2", target_bir_lowering=False, detect_race_conditions=detect_races)
    T, D, NC, NT, H, E, GD, RC = c.T, c.D, c.NC, c.NT, c.H, c.E, c.GD, c.RC
    HV = [(i * 512, (i + 1) * 512) for i in range(T // 512)]
    NKC = c.NK

    def xin(name, shape, dtype=F32):
        return nc.dram_tensor(name, list(shape), dtype, kind="ExternalInput")

    x_d = xin("x_fm", [D, T])
    wq_d = xin("wq_b", [NC, 128, D])
    wk_d = xin("wk_b", [NC, 128, D])
    wo_d = xin("wo_b", [NC, 128, D])
    wff_d = xin("wff_b", [NC, 128, D])
    wv_d = xin("wv_b", [c.OG, 128, NC * c.OGW])
    a_d = xin("a_b", [128, NC * RC])
    bcat_d = xin("b_cat", [128, D])
    gw1_d = xin("gw1_b", [128, NC * GD])
    gw2_d = xin("gw2_t", [GD, E])
    bq_d = xin("bq_c", [128, NC])
    bk_d = xin("bk_c", [128, NC])
    bo_d = xin("bo_c", [128, NC])
    bv_d = xin("bv_r", [1, D], BF16)
    gb1_d = xin("gb1_c", [GD, 1])
    gb2_d = xin("gb2_r", [1, E])
    lng_d = xin("lng_c", [GD, 1])
    lnb_d = xin("lnb_c", [GD, 1])
    gq_d = xin("gq_c", [128, 1])
    gk_d = xin("gk_c", [128, 1])
    map_d = xin("map_alpha", [E, 128])
    idf_d = xin("ident_f32", [128, 128])

    out_d = nc.dram_tensor("out_fm", [D, T], F32, kind="ExternalOutput")
    tap_d = {}
    for tname, tshape, tdt in taps:
        tap_d[tname] = nc.dram_tensor(tname, list(tshape), tdt, kind="ExternalOutput")

    res1_d = nc.dram_tensor("res1", [D, T], F32, kind="Internal")
    res2_d = nc.dram_tensor("res2", [D, T], F32, kind="Internal")
    ao_d = nc.dram_tensor("ao", [H * 128, T], BF16, kind="Internal")
    q_d = nc.dram_tensor("q_sp", [H * 128, T], BF16, kind="Internal")
    kg_own = [nc.dram_tensor(f"kg_own{g}", [c.HPG * 128, T], BF16, kind="Internal") for g in range(c.NG)]
    vg_own = [nc.dram_tensor(f"vg_own{g}", [c.HPG * NT * 128, 128], BF16, kind="Internal") for g in range(c.NG)]
    kg_gth = [nc.dram_tensor(f"kg_gth{g}", [NCORES * c.HPG * 128, T], BF16, kind="Internal") for g in range(c.NG)]
    vg_gth = [nc.dram_tensor(f"vg_gth{g}", [NCORES * c.HPG * NT * 128, 128], BF16, kind="Internal") for g in range(c.NG)]
    cnt_in = [nc.dram_tensor(f"cnt_in{i}", [1, 2 * E], F32, kind="Internal") for i in range(2)]
    cnt_out = [nc.dram_tensor(f"cnt_out{i}", [1, 2 * E], F32, kind="Internal") for i in range(2)]

    ctx = ExitStack()
    sb = lambda name, shape, dt: ctx.enter_context(nc.sbuf_tensor(name, list(shape), dt))

    acc_s = sb("acc_s", [128, T], F32)
    acc_q = sb("acc_q", [128, T], F32)
    tbuf = [sb(f"tchunk{i}", [128, T], F32) for i in range(2)]
    xinb = [sb(f"xin{i}", [128, T], F32) for i in range(2)]
    rout = [sb(f"rout{i}", [128, T], F32) for i in range(2)]
    sqs = sb("sqs", [128, T], F32)
    a_sb = sb("a_sb", [128, NC * RC], BF16)
    bcat_sb = sb("bcat_sb", [128, D], BF16)
    gw1_sb = sb("gw1_sb", [128, NC * GD], F32)
    gw2_sb = sb("gw2_sb", [GD, E], F32)
    bq_sb = sb("bq_sb", [128, NC], F32)
    bk_sb = sb("bk_sb", [128, NC], F32)
    bo_sb = sb("bo_sb", [128, NC], F32)
    bv_sb = sb("bv_sb", [1, D], BF16)
    gb1_sb = sb("gb1_sb", [GD, 1], F32)
    gb2row = sb("gb2row", [1, E], F32)
    gb2b_sb = sb("gb2b_sb", [128, E], F32)
    lng_sb = sb("lng_sb", [GD, 1], F32)
    lnb_sb = sb("lnb_sb", [GD, 1], F32)
    gq_sb = sb("gq_sb", [128, 1], F32)
    gk_sb = sb("gk_sb", [128, 1], F32)
    map_sb = sb("map_sb", [E, 128], F32)
    idf_sb = sb("idf_sb", [128, 128], F32)
    eps5_sb = sb("eps5_sb", [1, 1], F32)
    eps6_sb = sb("eps6_sb", [1, 1], F32)
    ones_cb = sb("ones_cb", [128, 1], BF16)
    ones_rb = sb("ones_rb", [1, 128], BF16)
    ones_cf = sb("ones_cf", [128, 1], F32)
    ones_rf = sb("ones_rf", [1, 128], F32)
    row_s = sb("row_s", [1, T], F32)
    row_q = sb("row_q", [1, T], F32)
    row_m = sb("row_m", [1, T], F32)
    row_r = sb("row_r", [1, T], F32)
    g_sb = sb("g_sb", [128, E], F32)
    gm_sb = sb("gm_sb", [128, E], F32)
    eq_sb = sb("eq_sb", [128, 2 * E * NT], F32)
    m1_sb = sb("m1_sb", [128, 1], F32)
    m2_sb = sb("m2_sb", [128, 1], F32)
    e21_sb = sb("e21_sb", [128, 1], F32)
    zden_sb = sb("zden_sb", [128, 1], F32)
    g12_sb = sb("g12_sb", [128, 2 * NT], F32)
    wz_sb = sb("wz_sb", [128, 2 * E], F32)
    w_sb = sb("w_sb", [128, E], F32)
    cnt_sb = sb("cnt_sb", [1, 2 * E], F32)
    alw_row = sb("alw_row", [1, 2 * E], F32)
    alw_sb = sb("alw_sb", [128, 2 * E], F32)
    pm_sb = sb("pm_sb", [128, 1], F32)
    mrow_sb = sb("mrow_sb", [1, 1], F32)
    brow_sb = sb("brow_sb", [1, 1], F32)
    b2row_sb = sb("b2row_sb", [1, 1], F32)
    bias_sb = sb("bias_sb", [128, 1], F32)
    qraw = [sb(f"qraw{i}", [128, T], F32) for i in range(2)]
    q2_sb = sb("q2_sb", [128, T], BF16)
    kq_out = [sb(f"kq_out{i}", [128, T], BF16) for i in range(2)]
    vout_sb = [sb(f"vout{i}", [128, c.OGW], BF16) for i in range(2)]

    mks = lambda name: Sem(ctx, nc, name)
    dx = [mks("dx0"), mks("dx1")]
    dy = mks("dy")
    gw = [mks("gw0"), mks("gw1")]
    gz = mks("gz")
    cc = mks("cc")
    tm = mks("tm")
    vm = mks("vm")
    am = mks("am")
    gm = mks("gm")

    block = ctx.enter_context(nc.Block())

    @block.sync
    def _(e):
        e = Trk(e)
        for t, d in [(bq_sb, bq_d), (bk_sb, bk_d), (bo_sb, bo_d),
                     (gb1_sb, gb1_d), (lng_sb, lng_d), (lnb_sb, lnb_d),
                     (gq_sb, gq_d), (gk_sb, gk_d), (map_sb, map_d),
                     (idf_sb, idf_d), (gw2_sb, gw2_d), (gw1_sb, gw1_d),
                     (gb2row, gb2_d)]:
            dy.n = dy.inc(e.dma_start(t[:], d.ap()), 16)

    @block.gpsimd
    def _(e):
        e = Trk(e)
        gz.inc(e.dma_start(a_sb[:], a_d.ap()), 16)
        gz.inc(e.dma_start(bcat_sb[:], bcat_d.ap()), 16)
        gz.inc(e.dma_start(bv_sb[:], bv_d.ap()), 16)
        gm.inc(e.memset(eps5_sb[:], 1e-5))
        gm.inc(e.memset(eps6_sb[:], 1e-6))
        gm.inc(e.memset(ones_cb[:], 1.0))
        gm.inc(e.memset(ones_rb[:], 1.0))
        gm.inc(e.memset(ones_cf[:], 1.0))
        gm.inc(e.memset(ones_rf[:], 1.0))
    dy_setup, gz_setup, gm_setup = dy.n, gz.n, gm.n

    @block.scalar
    def _(e):
        e = Trk(e)
        e.wait_ge(gm.h, gm_setup)

    with nc.psum_tensor("ps_gb2", [128, E], F32) as ps_gb2:
        @block.tensor
        def _(e):
            e = Trk(e)
            e.wait_ge(dy.h, dy_setup)
            e.wait_ge(gm.h, gm_setup)
            tm.inc(e.matmul(ps_gb2[:, :], ones_rf[0:1, :], gb2row[0:1, :],
                            start=True, stop=True))
        tm_gb2 = tm.n

        @block.vector
        def _(e):
            e = Trk(e)
            e.wait_ge(tm.h, tm_gb2)
            vm.inc(e.tensor_copy(gb2b_sb[:], ps_gb2[:]))

    if STOP_AFTER == "setup":
        @block.sync
        def _(e):
            e = Trk(e)
            e.wait_ge(dy.h, dy.n)
        ctx.close()
        return nc
    vm_st = {}
    dx_st = {}
    for ic in range(NC):
        par = ic % 2
        xi, dxs = xinb[par], dx[par]

        @block.sync
        def _(e, ic=ic, xi=xi, dxs=dxs):
            e = Trk(e)
            if ic >= 2:
                e.wait_ge(vm.h, vm_st[ic - 2])
            dx_st[ic] = dxs.inc(e.dma_start(xi[:], x_d.ap()[ic * 128:(ic + 1) * 128, :]), 16)

        @block.vector
        def _(e, ic=ic, xi=xi, dxs=dxs):
            e = Trk(e)
            e.wait_ge(dxs.h, dx_st[ic])
            if ic == 0:
                e.tensor_copy(acc_s[:], xi[:])
                e.tensor_mul(acc_q[:], xi[:], xi[:])
            else:
                e.tensor_add(acc_s[:], acc_s[:], xi[:])
                e.tensor_mul(sqs[:], xi[:], xi[:])
                e.tensor_add(acc_q[:], acc_q[:], sqs[:])
            vm_st[ic] = vm.inc(e.last)

    def ln_rows(ps_sum, eps_ap, nfeat):
        @block.tensor
        def _(e):
            e = Trk(e)
            e.wait_ge(vm.h, vm.n)
            for (lo, hi) in HV:
                e.matmul(ps_sum[0:1, lo:hi], ones_cf[:], acc_s[:, lo:hi], start=True, stop=True)
                e.matmul(ps_sum[0:1, T + lo:T + hi], ones_cf[:], acc_q[:, lo:hi], start=True, stop=True)
            tm.inc(e.last)
        t0 = tm.n
        if STOP_AFTER == "rowsA":
            return

        @block.vector
        def _(e):
            e = Trk(e)
            e.wait_ge(tm.h, t0)
            e.tensor_scalar_mul(row_m[:], ps_sum[0:1, 0:T], 1.0 / nfeat)
            e.tensor_scalar_mul(row_q[:], ps_sum[0:1, T:2 * T], 1.0 / nfeat)
            e.tensor_mul(row_s[:], row_m[:], row_m[:])
            e.tensor_sub(row_q[:], row_q[:], row_s[:])
            vm.inc(e.last)
        v0 = vm.n
        if STOP_AFTER == "rowsB":
            return

        @block.scalar
        def _(e):
            e = Trk(e)
            e.wait_ge(vm.h, v0)
            e.activation(row_s[:], row_q[:], AF.Ln, bias=eps_ap[0:1, 0:1])
            e.activation(row_r[:], row_s[:], AF.Exp, scale=-0.5)
            am.inc(e.last)

    def bcast_rows(ps_meanB, ps_rstdB):
        @block.tensor
        def _(e):
            e = Trk(e)
            e.wait_ge(am.h, am.n)
            for (lo, hi) in HV:
                e.matmul(ps_meanB[:, lo:hi], ones_rf[0:1, :], row_m[0:1, lo:hi], start=True, stop=True)
                e.matmul(ps_rstdB[:, lo:hi], ones_rf[0:1, :], row_r[0:1, lo:hi], start=True, stop=True)
            tm.inc(e.last)

    def ffn_phase(src_d, dst_d, icnt):
        vm_phase0 = vm.n
        tm_phase0 = tm.n
        fstk = ExitStack()
        xf = fstk.enter_context(nc.sbuf_tensor(f"xf_f{icnt}", [128, NC * T], BF16))
        h_sb = fstk.enter_context(nc.sbuf_tensor(f"h_sb_{icnt}", [GD, T], F32))
        h2_sb = fstk.enter_context(nc.sbuf_tensor(f"h2_sb_{icnt}", [GD, T], F32))
        hsq_sb = fstk.enter_context(nc.sbuf_tensor(f"hsq_sb_{icnt}", [GD, T], F32))
        wt_sb = fstk.enter_context(nc.sbuf_tensor(f"wt_sb_{icnt}", [E, T], F32))
        wtexp_sb = fstk.enter_context(nc.sbuf_tensor(f"wtexp_sb_{icnt}", [128, T], F32))
        tmps_sb = fstk.enter_context(nc.sbuf_tensor(f"tmps_sb_{icnt}", [128, T], BF16))
        if STOP_AFTER == "stats":
            fstk.close()
            return
        with nc.psum_tensor(f"ffn_sum_{icnt}", [1, 2 * T], F32) as ps_sum:
            ln_rows(ps_sum, eps5_sb, D)
        if STOP_AFTER == "rowsdbg":
            @block.sync
            def _(e):
                e = Trk(e)
                e.wait_ge(am.h, am.n)
                dy.inc(e.dma_start(out_d.ap()[0:1, :], row_m[0:1, :]), 16)
                dy.inc(e.dma_start(out_d.ap()[1:2, :], row_r[0:1, :]), 16)
                dy.inc(e.dma_start(out_d.ap()[2:3, :], row_q[0:1, :]), 16)
                dy.inc(e.dma_start(out_d.ap()[4:132, :], acc_s[:, :]), 16)
                dy.inc(e.dma_start(out_d.ap()[132:260, :], acc_q[:, :]), 16)
                e.wait_ge(dy.h, dy.n)
        if STOP_AFTER in ("rows", "rowsA", "rowsB", "rowsdbg"):
            fstk.close()
            return

        with nc.psum_tensor(f"ffn_tmp_{icnt}", [128, T], F32) as ps_tmp:
            with (nc.psum_tensor(f"ffn_meanB_{icnt}", [128, T], F32) as ps_meanB,
                  nc.psum_tensor(f"ffn_rstdB_{icnt}", [128, T], F32) as ps_rstdB,
                  nc.psum_tensor(f"ffn_h_{icnt}", [GD, T], F32) as ps_h):
                bcast_rows(ps_meanB, ps_rstdB)
                t_bc = tm.n

                vm_ch = {}
                tm_ch = {}
                dx_ch = {}
                for ic in range(NC):
                    par = ic % 2
                    xi, dxs, tb = xinb[par], dx[par], tbuf[par]

                    @block.sync
                    def _(e, ic=ic, xi=xi, dxs=dxs):
                        e = Trk(e)
                        if ic >= 2:
                            e.wait_ge(vm.h, vm_ch[ic - 2])
                        else:
                            e.wait_ge(vm.h, vm_phase0)
                        dx_ch[ic] = dxs.inc(
                            e.dma_start(xi[:], src_d.ap()[ic * 128:(ic + 1) * 128, :]), 16)

                    @block.vector
                    def _(e, ic=ic, xi=xi, dxs=dxs, tb=tb):
                        e = Trk(e)
                        e.wait_ge(dxs.h, dx_ch[ic])
                        if ic == 0:
                            e.wait_ge(tm.h, t_bc)
                        if ic >= 2:
                            e.wait_ge(tm.h, tm_ch[ic - 2])
                        e.tensor_sub(tb[:], xi[:], ps_meanB[:])
                        e.tensor_mul(xf[:, ic * T:(ic + 1) * T], tb[:], ps_rstdB[:])
                        vm_ch[ic] = vm.inc(e.last)

                    @block.tensor
                    def _(e, ic=ic, tb=tb):
                        e = Trk(e)
                        e.wait_ge(vm.h, vm_ch[ic])
                        if ic == 0:
                            e.wait_ge(gz.h, gz_setup)
                        for (lo, hi) in HV:
                            e.matmul(ps_h[:, lo:hi], gw1_sb[:, ic * GD:(ic + 1) * GD],
                                     tb[:, lo:hi], start=(ic == 0), stop=(ic == NC - 1))
                        for (lo, hi) in HV:
                            e.matmul(ps_tmp[:, lo:hi], a_sb[:, ic * RC:(ic + 1) * RC],
                                     xf[:, ic * T + lo: ic * T + hi],
                                     start=(ic == 0), stop=(ic == NC - 1))
                        tm_ch[ic] = tm.inc(e.last)
                t_hmm = tm.n

                @block.scalar
                def _(e):
                    e = Trk(e)
                    e.wait_ge(tm.h, t_hmm)
                    am.inc(e.activation(h_sb[:], ps_h[:], AF.Copy))
                a_hcp = am.n

                @block.vector
                def _(e):
                    e = Trk(e)
                    e.wait_ge(am.h, a_hcp)
                    e.tensor_mul(h2_sb[:], h_sb[:], ps_rstdB[0:GD, :])
                    e.tensor_scalar_add(h2_sb[:], h2_sb[:], gb1_sb[:, 0:1])
                    vm.inc(e.last)
            if STOP_AFTER == "norm":
                @block.gpsimd
                def _(e):
                    e = Trk(e)
                    e.wait_ge(vm.h, vm.n)
                    gz.inc(e.dma_start(out_d.ap()[0:128, :], xf[:, 0:T]), 16)
                    gz.inc(e.dma_start(out_d.ap()[128:256, :], xf[:, (NC - 1) * T:NC * T]), 16)
                @block.sync
                def _(e):
                    e = Trk(e)
                    e.wait_ge(vm.h, vm.n)
                    dy.inc(e.dma_start(out_d.ap()[256:256 + GD, :], h2_sb[:, :]), 16)
                    e.wait_ge(dy.h, dy.n)
                    e.wait_ge(gz.h, gz.n)
                fstk.close()
                return

            with nc.psum_tensor(f"g_sum_{icnt}", [1, 2 * T], F32) as ps_gsum:
                @block.vector
                def _(e):
                    e = Trk(e)
                    e.tensor_mul(hsq_sb[:], h2_sb[:], h2_sb[:])
                    vm.inc(e.last)
                v_hsq = vm.n

                @block.tensor
                def _(e):
                    e = Trk(e)
                    e.wait_ge(vm.h, v_hsq)
                    for (lo, hi) in HV:
                        e.matmul(ps_gsum[0:1, lo:hi], ones_cf[0:GD, :], h2_sb[:, lo:hi], start=True, stop=True)
                        e.matmul(ps_gsum[0:1, T + lo:T + hi], ones_cf[0:GD, :], hsq_sb[:, lo:hi], start=True, stop=True)
                    tm.inc(e.last)
                t_gs = tm.n

                @block.vector
                def _(e):
                    e = Trk(e)
                    e.wait_ge(tm.h, t_gs)
                    e.tensor_scalar_mul(row_m[:], ps_gsum[0:1, 0:T], 1.0 / GD)
                    e.tensor_scalar_mul(row_q[:], ps_gsum[0:1, T:2 * T], 1.0 / GD)
                    e.tensor_mul(row_s[:], row_m[:], row_m[:])
                    e.tensor_sub(row_q[:], row_q[:], row_s[:])
                    vm.inc(e.last)
                v_gv = vm.n

                @block.scalar
                def _(e):
                    e = Trk(e)
                    e.wait_ge(vm.h, v_gv)
                    e.activation(row_s[:], row_q[:], AF.Ln, bias=eps5_sb[0:1, 0:1])
                    e.activation(row_r[:], row_s[:], AF.Exp, scale=-0.5)
                    am.inc(e.last)

            with (nc.psum_tensor(f"g_meanB_{icnt}", [GD, T], F32) as ps_gmB,
                  nc.psum_tensor(f"g_rstdB_{icnt}", [GD, T], F32) as ps_grB):
                @block.tensor
                def _(e):
                    e = Trk(e)
                    e.wait_ge(am.h, am.n)
                    for (lo, hi) in HV:
                        e.matmul(ps_gmB[:, lo:hi], ones_rf[0:1, 0:GD], row_m[0:1, lo:hi], start=True, stop=True)
                        e.matmul(ps_grB[:, lo:hi], ones_rf[0:1, 0:GD], row_r[0:1, lo:hi], start=True, stop=True)
                    tm.inc(e.last)
                t_gb = tm.n

                @block.vector
                def _(e):
                    e = Trk(e)
                    e.wait_ge(tm.h, t_gb)
                    e.tensor_sub(h_sb[:], h2_sb[:], ps_gmB[:])
                    e.tensor_mul(h_sb[:], h_sb[:], ps_grB[:])
                    e.tensor_scalar(h_sb[:], h_sb[:], lng_sb[:, 0:1], lnb_sb[:, 0:1], ALU.mult, ALU.add)
                    e.tensor_relu(h_sb[:], h_sb[:])
                    vm.inc(e.last)
                v_hn = vm.n

            if STOP_AFTER == "gateln":
                @block.sync
                def _(e):
                    e = Trk(e)
                    e.wait_ge(vm.h, vm.n)
                    dy.inc(e.dma_start(out_d.ap()[0:GD, :], h_sb[:, :]), 16)
                    e.wait_ge(dy.h, dy.n)
                fstk.close()
                return
            with (nc.psum_tensor(f"g_ps_{icnt}", [128, E * NT], F32) as ps_g,
                  nc.psum_tensor(f"wt_ps_{icnt}", [E, 128], F32) as ps_wt,
                  nc.psum_tensor(f"cnt_ps_{icnt}", [1, 2 * E], F32) as ps_cnt,
                  nc.psum_tensor(f"alw_ps_{icnt}", [128, 2 * E], F32) as ps_alw):
                @block.tensor
                def _(e):
                    e = Trk(e)
                    e.wait_ge(vm.h, v_hn)
                    for t in range(NT):
                        e.matmul(ps_g[:, t * E:(t + 1) * E], h_sb[:, t * 128:(t + 1) * 128],
                                 gw2_sb[:, :], start=True, stop=True)
                    tm.inc(e.last)
                t_g = tm.n

                vm_g = {}
                for t in range(NT):
                    @block.vector
                    def _(e, t=t):
                        e = Trk(e)
                        e.wait_ge(tm.h, t_g)
                        eqv = eq_sb[:, t * 2 * E:(t + 1) * 2 * E]
                        e.tensor_add(g_sb[:], ps_g[:, t * E:(t + 1) * E], gb2b_sb[:])
                        e.reduce_max(m1_sb[:], g_sb[:], axis=AX.X)
                        e.tensor_single_scalar(eqv[:, 0:E], g_sb[:], m1_sb[:, 0:1], ALU.is_equal)
                        e.scalar_tensor_tensor(gm_sb[:], eqv[:, 0:E], -1e30, g_sb[:], ALU.mult, ALU.add)
                        e.reduce_max(m2_sb[:], gm_sb[:], axis=AX.X)
                        e.tensor_single_scalar(eqv[:, E:2 * E], gm_sb[:], m2_sb[:, 0:1], ALU.is_equal)
                        e.tensor_sub(m2_sb[:], m2_sb[:], m1_sb[:])
                        vm_g[t] = vm.inc(e.last)

                    @block.scalar
                    def _(e, t=t):
                        e = Trk(e)
                        e.wait_ge(vm.h, vm_g[t])
                        am.inc(e.activation(e21_sb[:], m2_sb[:], AF.Exp))
                    a_e = am.n

                    @block.vector
                    def _(e, t=t):
                        e = Trk(e)
                        e.wait_ge(am.h, a_e)
                        e.tensor_scalar_add(zden_sb[:], e21_sb[:], 1.0)
                        e.reciprocal(g12_sb[:, 2 * t:2 * t + 1], zden_sb[:])
                        e.tensor_mul(g12_sb[:, 2 * t + 1:2 * t + 2], e21_sb[:],
                                     g12_sb[:, 2 * t:2 * t + 1])
                        vm.inc(e.last)
                    v_g12 = vm.n

                    @block.tensor
                    def _(e, t=t):
                        e = Trk(e)
                        e.wait_ge(vm.h, v_g12)
                        e.matmul(ps_cnt[0:1, :], ones_cf[:, :],
                                 eq_sb[:, t * 2 * E:(t + 1) * 2 * E],
                                 start=(t == 0), stop=(t == NT - 1))
                        tm.inc(e.last)
                t_cnt = tm.n

                if STOP_AFTER == "gates":
                    @block.sync
                    def _(e):
                        e = Trk(e)
                        e.wait_ge(tm.h, tm.n)
                        e.wait_ge(vm.h, vm.n)
                        dy.inc(e.dma_start(out_d.ap()[0:128, 0:2 * E * NT], eq_sb[:, :]), 16)
                        dy.inc(e.dma_start(out_d.ap()[128:256, 0:2 * NT], g12_sb[:, :]), 16)
                        dy.inc(e.dma_start(out_d.ap()[256:384, 0:E], g_sb[:, :]), 16)
                        dy.inc(e.dma_start(out_d.ap()[384:512, 0:1], m1_sb[:, :], allow_slow_non_contiguous=True), 16)
                        dy.inc(e.dma_start(out_d.ap()[512:640, 0:1], m2_sb[:, :], allow_slow_non_contiguous=True), 16)
                        e.wait_ge(dy.h, dy.n)
                    fstk.close()
                    return
                @block.vector
                def _(e):
                    e = Trk(e)
                    e.wait_ge(tm.h, t_cnt)
                    vm.inc(e.tensor_copy(cnt_sb[:], ps_cnt[:]))
                v_cnt = vm.n

                @block.sync
                def _(e):
                    e = Trk(e)
                    e.wait_ge(vm.h, v_cnt)
                    dy.inc(e.dma_start(cnt_in[icnt].ap(), cnt_sb[:]), 16)
                d_ci = dy.n

                @block.gpsimd
                def _(e):
                    e = Trk(e)
                    e.wait_ge(dy.h, d_ci)
                    cc.inc(e.collective_compute(
                        "AllReduce", ALU.add, replica_groups=[list(range(NCORES))],
                        ins=[cnt_in[icnt].ap().opt()], outs=[cnt_out[icnt].ap().opt()]), 1)
                c_ar = cc.n

                @block.sync
                def _(e):
                    e = Trk(e)
                    e.wait_ge(cc.h, c_ar)
                    dy.inc(e.dma_start(cnt_sb[:], cnt_out[icnt].ap()), 16)
                d_co = dy.n

                @block.vector
                def _(e):
                    e = Trk(e)
                    e.wait_ge(dy.h, d_co)
                    e.tensor_single_scalar(alw_row[:], cnt_sb[:], c.cap, ALU.is_le)
                    vm.inc(e.last)
                v_alw = vm.n

                @block.tensor
                def _(e):
                    e = Trk(e)
                    e.wait_ge(vm.h, v_alw)
                    tm.inc(e.matmul(ps_alw[:, :], ones_rf[0:1, :], alw_row[0:1, :], start=True, stop=True))
                t_alw = tm.n

                @block.vector
                def _(e):
                    e = Trk(e)
                    e.wait_ge(tm.h, t_alw)
                    vm.inc(e.tensor_copy(alw_sb[:], ps_alw[:]))
                v_alwc = vm.n

                tm_wt = {}
                for t in range(NT):
                    @block.vector
                    def _(e, t=t):
                        e = Trk(e)
                        e.wait_ge(vm.h, v_alwc)
                        if t >= 1:
                            e.wait_ge(tm.h, tm_wt[t - 1])
                        eqv = eq_sb[:, t * 2 * E:(t + 1) * 2 * E]
                        e.tensor_scalar_mul(wz_sb[:, 0:E], eqv[:, 0:E], g12_sb[:, 2 * t:2 * t + 1])
                        e.tensor_scalar_mul(wz_sb[:, E:2 * E], eqv[:, E:2 * E], g12_sb[:, 2 * t + 1:2 * t + 2])
                        e.tensor_mul(wz_sb[:], wz_sb[:], alw_sb[:])
                        e.tensor_add(w_sb[:], wz_sb[:, 0:E], wz_sb[:, E:2 * E])
                        vm.inc(e.last)
                    v_w = vm.n

                    @block.tensor
                    def _(e, t=t):
                        e = Trk(e)
                        e.wait_ge(vm.h, v_w)
                        tm_wt[t] = tm.inc(e.transpose(ps_wt[:, :], w_sb[:, :], idf_sb[:, :]))

                    @block.vector
                    def _(e, t=t):
                        e = Trk(e)
                        e.wait_ge(tm.h, tm_wt[t])
                        vm.inc(e.tensor_copy(wt_sb[:, t * 128:(t + 1) * 128], ps_wt[:, :]))
                v_wt = vm.n

            if STOP_AFTER == "w":
                @block.sync
                def _(e):
                    e = Trk(e)
                    e.wait_ge(vm.h, vm.n)
                    dy.inc(e.dma_start(out_d.ap()[0:E, :], wt_sb[:, :]), 16)
                    dy.inc(e.dma_start(out_d.ap()[8:9, 0:2 * E], cnt_sb[:, :]), 16)
                    dy.inc(e.dma_start(out_d.ap()[9:10, 0:2 * E], alw_row[:, :]), 16)
                    dy.inc(e.dma_start(out_d.ap()[10:138, 0:E], w_sb[:, :]), 16)
                    dy.inc(e.dma_start(out_d.ap()[138:266, 0:2 * NT], g12_sb[:, :]), 16)
                    dy.inc(e.dma_start(out_d.ap()[266:394, 0:2 * E * NT], eq_sb[:, :]), 16)
                    dy.inc(e.dma_start(out_d.ap()[394:522, 0:2 * E], alw_sb[:, :]), 16)
                    e.wait_ge(dy.h, dy.n)
                fstk.close()
                return
            with nc.psum_tensor(f"wtexp_ps_{icnt}", [128, T], F32) as ps_we:
                @block.tensor
                def _(e):
                    e = Trk(e)
                    e.wait_ge(vm.h, v_wt)
                    for (lo, hi) in HV:
                        e.matmul(ps_we[:, lo:hi], map_sb[:, :], wt_sb[:, lo:hi], start=True, stop=True)
                    tm.inc(e.last)
                t_we = tm.n

                @block.vector
                def _(e):
                    e = Trk(e)
                    e.wait_ge(tm.h, t_we)
                    e.tensor_copy(wtexp_sb[:], ps_we[:])
                    e.tensor_mul(tmps_sb[:], ps_tmp[:], wtexp_sb[:])
                    vm.inc(e.last)
        v_tmps = vm.n

        with (nc.psum_tensor(f"main0_{icnt}", [128, T], F32) as pm0,
              nc.psum_tensor(f"main1_{icnt}", [128, T], F32) as pm1,
              nc.sbuf_tensor(f"wslf0_{icnt}", [128, NC * 128], BF16) as wsl0,
              nc.sbuf_tensor(f"wslf1_{icnt}", [128, NC * 128], BF16) as wsl1):
            wslab = [wsl0, wsl1]
            ps_main = [pm0, pm1]
            gw_oc = {}
            tm_oc = {}
            vm_oc = {}
            dxm_oc = {}
            dyr_oc = {}
            for oc in range(NC):
                par = oc % 2
                gws = gw[par]

                @block.gpsimd
                def _(e, oc=oc, par=par, gws=gws):
                    e = Trk(e)
                    if oc >= 2:
                        e.wait_ge(tm.h, tm_oc[oc - 2])
                    else:
                        e.wait_ge(tm.h, tm_phase0)
                    gw_oc[oc] = gws.inc(
                        e.dma_start(wslab[par][:], wff_d.ap()[oc, :, :]), 16)

                @block.sync
                def _(e, oc=oc, par=par):
                    e = Trk(e)
                    if oc >= 2:
                        e.wait_ge(vm.h, vm_oc[oc - 2])
                    else:
                        e.wait_ge(vm.h, vm_ch[NC - 1])
                    dxm_oc[oc] = dx[par].inc(
                        e.dma_start(xinb[par][:], src_d.ap()[oc * 128:(oc + 1) * 128, :]), 16)

                @block.tensor
                def _(e, oc=oc, par=par, gws=gws):
                    e = Trk(e)
                    e.wait_ge(gws.h, gw_oc[oc])
                    if oc == 0:
                        e.wait_ge(vm.h, v_tmps)
                    if oc >= 2:
                        e.wait_ge(vm.h, vm_oc[oc - 2])
                    for (lo, hi) in HV:
                        for ic in range(NC):
                            e.matmul(ps_main[par][:, lo:hi],
                                     wslab[par][:, ic * 128:(ic + 1) * 128],
                                     xf[:, ic * T + lo: ic * T + hi],
                                     start=(ic == 0), stop=False)
                        e.matmul(ps_main[par][:, lo:hi],
                                 bcat_sb[:, oc * 128:(oc + 1) * 128],
                                 tmps_sb[:, lo:hi], start=False, stop=True)
                    tm_oc[oc] = tm.inc(e.last)

                @block.vector
                def _(e, oc=oc, par=par):
                    e = Trk(e)
                    e.wait_ge(tm.h, tm_oc[oc])
                    e.wait_ge(dx[par].h, dxm_oc[oc])
                    if oc >= 2:
                        e.wait_ge(dy.h, dyr_oc[oc - 2])
                    e.scalar_tensor_tensor(rout[par][:], ps_main[par][:], 0.5,
                                           xinb[par][:], ALU.mult, ALU.add)
                    if dst_d is not None:
                        if oc == 0:
                            e.tensor_copy(acc_s[:], rout[par][:])
                            e.tensor_mul(acc_q[:], rout[par][:], rout[par][:])
                        else:
                            e.tensor_add(acc_s[:], acc_s[:], rout[par][:])
                            e.tensor_mul(sqs[:], rout[par][:], rout[par][:])
                            e.tensor_add(acc_q[:], acc_q[:], sqs[:])
                    vm_oc[oc] = vm.inc(e.last)

                @block.sync
                def _(e, oc=oc, par=par):
                    e = Trk(e)
                    e.wait_ge(vm.h, vm_oc[oc])
                    tgt = dst_d if dst_d is not None else out_d
                    dyr_oc[oc] = dy.inc(e.dma_start(tgt.ap()[oc * 128:(oc + 1) * 128, :], rout[par][:]), 16)
        fstk.close()

    def qkv_attn_phase():
        vm_phase0 = vm.n
        qstk = ExitStack()
        xf = qstk.enter_context(nc.sbuf_tensor("xf_qkv", [128, NC * T], BF16))
        with nc.psum_tensor("ln2_sum", [1, 2 * T], F32) as ps_sum:
            ln_rows(ps_sum, eps5_sb, D)
        with (nc.psum_tensor("ln2_meanB", [128, T], F32) as ps_meanB,
              nc.psum_tensor("ln2_rstdB", [128, T], F32) as ps_rstdB):
            bcast_rows(ps_meanB, ps_rstdB)
            t_bc = tm.n
            vm_ch = {}
            dx_ch = {}
            for ic in range(NC):
                par = ic % 2
                xi, dxs, tb = xinb[par], dx[par], tbuf[par]

                @block.sync
                def _(e, ic=ic, xi=xi, dxs=dxs):
                    e = Trk(e)
                    if ic >= 2:
                        e.wait_ge(vm.h, vm_ch[ic - 2])
                    else:
                        e.wait_ge(vm.h, vm_phase0)
                    dx_ch[ic] = dxs.inc(
                        e.dma_start(xi[:], res1_d.ap()[ic * 128:(ic + 1) * 128, :]), 16)

                @block.vector
                def _(e, ic=ic, xi=xi, dxs=dxs, tb=tb):
                    e = Trk(e)
                    e.wait_ge(dxs.h, dx_ch[ic])
                    if ic == 0:
                        e.wait_ge(tm.h, t_bc)
                    e.tensor_sub(tb[:], xi[:], ps_meanB[:])
                    e.tensor_mul(xf[:, ic * T:(ic + 1) * T], tb[:], ps_rstdB[:])
                    vm_ch[ic] = vm.inc(e.last)
        v_xf2 = vm.n

        def kq_loop(w_dram, bias_sbuf, gain_sbuf, dst, eps_ap, tag):
            tm_kq0 = tm.n
            with (nc.psum_tensor(f"kq0_{tag}", [128, T], F32) as p0,
                  nc.psum_tensor(f"kq1_{tag}", [128, T], F32) as p1,
                  nc.psum_tensor(f"kq_ss_{tag}", [1, T], F32) as ps_ss,
                  nc.psum_tensor(f"kq_rb_{tag}", [128, T], F32) as ps_rb,
                  nc.sbuf_tensor(f"wslk0_{tag}", [128, NC * 128], BF16) as wsl0,
                  nc.sbuf_tensor(f"wslk1_{tag}", [128, NC * 128], BF16) as wsl1):
                wslab = [wsl0, wsl1]
                ps_qk = [p0, p1]
                gw_oc, tm_oc, am_oc, vm_oc = {}, {}, {}, {}
                tm_rb, vm_fin, dy_fin = {}, {}, {}
                for oc in range(H):
                    par = oc % 2
                    gws = gw[par]

                    @block.gpsimd
                    def _(e, oc=oc, par=par, gws=gws):
                        e = Trk(e)
                        if oc >= 2:
                            e.wait_ge(tm.h, tm_oc[oc - 2])
                        else:
                            e.wait_ge(tm.h, tm_kq0)
                        gw_oc[oc] = gws.inc(e.dma_start(wslab[par][:], w_dram.ap()[oc, :, :]), 16)

                    @block.tensor
                    def _(e, oc=oc, par=par, gws=gws):
                        e = Trk(e)
                        e.wait_ge(gws.h, gw_oc[oc])
                        if oc == 0:
                            e.wait_ge(vm.h, v_xf2)
                        if oc >= 2:
                            e.wait_ge(am.h, am_oc[oc - 2])
                        for (lo, hi) in HV:
                            for ic in range(NC):
                                e.matmul(ps_qk[par][:, lo:hi],
                                         wslab[par][:, ic * 128:(ic + 1) * 128],
                                         xf[:, ic * T + lo: ic * T + hi],
                                         start=(ic == 0), stop=(ic == NC - 1))
                        tm_oc[oc] = tm.inc(e.last)

                    @block.scalar
                    def _(e, oc=oc, par=par):
                        e = Trk(e)
                        e.wait_ge(tm.h, tm_oc[oc])
                        if oc >= 2:
                            e.wait_ge(vm.h, vm_fin[oc - 2])
                        am_oc[oc] = am.inc(e.activation(
                            qraw[par][:], ps_qk[par][:], AF.Identity, bias=bias_sbuf[:, oc:oc + 1]))

                    @block.vector
                    def _(e, oc=oc, par=par):
                        e = Trk(e)
                        e.wait_ge(am.h, am_oc[oc])
                        if oc >= 1:
                            e.wait_ge(vm.h, vm_fin.get(oc - 1, 0))
                        e.tensor_mul(q2_sb[:], qraw[par][:], qraw[par][:])
                        vm_oc[oc] = vm.inc(e.last)

                    @block.tensor
                    def _(e, oc=oc):
                        e = Trk(e)
                        e.wait_ge(vm.h, vm_oc[oc])
                        for (lo, hi) in HV:
                            e.matmul(ps_ss[0:1, lo:hi], ones_cb[:, :], q2_sb[:, lo:hi],
                                     start=True, stop=True)
                        tm.inc(e.last)
                    t_ss = tm.n

                    @block.vector
                    def _(e, oc=oc):
                        e = Trk(e)
                        e.wait_ge(tm.h, t_ss)
                        e.tensor_scalar_mul(row_m[:], ps_ss[0:1, 0:T], 1.0 / 128.0)
                        vm.inc(e.last)
                    v_ms = vm.n

                    @block.scalar
                    def _(e, oc=oc):
                        e = Trk(e)
                        e.wait_ge(vm.h, v_ms)
                        e.activation(row_s[:], row_m[:], AF.Ln, bias=eps_ap[0:1, 0:1])
                        e.activation(row_r[:], row_s[:], AF.Exp, scale=-0.5)
                        am.inc(e.last)
                    a_rs = am.n

                    @block.tensor
                    def _(e, oc=oc):
                        e = Trk(e)
                        e.wait_ge(am.h, a_rs)
                        for (lo, hi) in HV:
                            e.matmul(ps_rb[:, lo:hi], ones_rf[0:1, :], row_r[0:1, lo:hi],
                                     start=True, stop=True)
                        tm_rb[oc] = tm.inc(e.last)

                    @block.vector
                    def _(e, oc=oc, par=par):
                        e = Trk(e)
                        e.wait_ge(tm.h, tm_rb[oc])
                        if oc >= 2:
                            e.wait_ge(dy.h, dy_fin.get(oc - 2, 0))
                        e.tensor_mul(qraw[par][:], qraw[par][:], ps_rb[:])
                        e.tensor_scalar_mul(kq_out[par][:], qraw[par][:], gain_sbuf[:, 0:1])
                        vm_fin[oc] = vm.inc(e.last)

                    @block.sync
                    def _(e, oc=oc, par=par):
                        e = Trk(e)
                        e.wait_ge(vm.h, vm_fin[oc])
                        if dst[0] == "dram":
                            g = oc // c.HPG
                            r0 = (oc % c.HPG) * 128
                            dy_fin[oc] = dy.inc(e.dma_start(dst[1][g].ap()[r0:r0 + 128, :], kq_out[par][:]), 16)
                        else:
                            dy_fin[oc] = dy.inc(e.dma_start(dst[1].ap()[oc * 128:(oc + 1) * 128, :], kq_out[par][:]), 16)
            return dict(vm_fin=vm_fin)

        kq_loop(wk_d, bk_sb, gk_sb, ("dram", kg_own), eps6_sb, "k")
        dy_after_k = dy.n

        with ExitStack() as vctx:
            wvslab = [vctx.enter_context(nc.sbuf_tensor(f"wvslab{i}", [128, NC * c.OGW], BF16))
                      for i in range(2)]
            ps_v = vctx.enter_context(nc.psum_tensor("ps_v", [128, NT * 512], F32))
            tm_v0 = tm.n
            gw_og, tm_og, am_ot, dy_v = {}, {}, {}, {}
            for og in range(c.OG):
                par = og % 2

                @block.gpsimd
                def _(e, og=og, par=par):
                    e = Trk(e)
                    if og >= 2:
                        e.wait_ge(tm.h, tm_og[og - 2])
                    else:
                        e.wait_ge(tm.h, tm_v0)
                    gw_og[og] = gw[par].inc(e.dma_start(wvslab[par][:], wv_d.ap()[og, :, :]), 16)

                @block.tensor
                def _(e, og=og, par=par):
                    e = Trk(e)
                    e.wait_ge(gw[par].h, gw_og[og])
                    if og == 0:
                        e.wait_ge(vm.h, v_xf2)
                        e.wait_ge(gm.h, gm_setup)
                    if og >= 1:
                        e.wait_ge(am.h, am_ot[(og - 1, NT - 1)])
                    for tt in range(NT):
                        for ic in range(NC):
                            e.matmul(ps_v[:, tt * 512:tt * 512 + c.OGW],
                                     xf[:, ic * T + tt * 128: ic * T + (tt + 1) * 128],
                                     wvslab[par][:, ic * c.OGW:(ic + 1) * c.OGW],
                                     start=(ic == 0), stop=False)
                        e.matmul(ps_v[:, tt * 512:tt * 512 + c.OGW],
                                 ones_rb[0:1, :], bv_sb[0:1, og * c.OGW:(og + 1) * c.OGW],
                                 start=False, stop=True)
                        tm_og[og] = tm.inc(e.last)

                for tt in range(NT):
                    @block.scalar
                    def _(e, og=og, tt=tt):
                        e = Trk(e)
                        e.wait_ge(tm.h, tm_og[og] - (NT - 1 - tt))
                        if (og, tt) != (0, 0):
                            prev = (og, tt - 1) if tt > 0 else (og - 1, NT - 1)
                            e.wait_ge(dy.h, dy_v.get(prev, 0))
                        am_ot[(og, tt)] = am.inc(e.activation(
                            vout_sb[tt % 2][:, 0:c.OGW], ps_v[:, tt * 512:tt * 512 + c.OGW], AF.Copy))

                    @block.sync
                    def _(e, og=og, tt=tt):
                        e = Trk(e)
                        e.wait_ge(am.h, am_ot[(og, tt)])
                        g = og // (c.OG // c.NG)
                        hg0 = (og % (c.OG // c.NG)) * (c.OGW // 128)
                        nj = c.OGW // 128
                        dst4 = vg_own[g].ap().rearrange("(a b p) d -> p a b d", b=NT, p=128)
                        src = vout_sb[tt % 2][:, 0:c.OGW].rearrange("p (j d) -> p j d", d=128)
                        dy_v[(og, tt)] = dy.inc(
                            e.dma_start(dst4[:, hg0:hg0 + nj, tt, :], src), 16)

            ogpg = c.OG // c.NG
            cc_after_g = {}

            @block.gpsimd
            def _(e):
                e = Trk(e)
                e.wait_ge(dy.h, dy_after_k)
                for g in range(c.NG):
                    e.wait_ge(dy.h, dy_v[(g * ogpg + ogpg - 1, NT - 1)])
                    cc.inc(e.collective_compute(
                        "AllGather", ALU.bypass, replica_groups=[list(range(NCORES))],
                        ins=[kg_own[g].ap().opt()], outs=[kg_gth[g].ap().opt()]), 1)
                    cc.inc(e.collective_compute(
                        "AllGather", ALU.bypass, replica_groups=[list(range(NCORES))],
                        ins=[vg_own[g].ap().opt()], outs=[vg_gth[g].ap().opt()]), 1)
                    cc_after_g[g] = cc.n

        kq_loop(wq_d, bq_sb, gq_sb, ("dram1", q_d), eps6_sb, "q")
        dy_qdone = dy.n
        qstk.close()

        if True:
            with ExitStack() as actx:
                kh = [actx.enter_context(nc.sbuf_tensor(f"kh{i}", [128, NCORES * T], BF16)) for i in range(2)]
                vh = [actx.enter_context(nc.sbuf_tensor(f"vh{i}", [128, NKC * 128], BF16)) for i in range(2)]
                qh = [actx.enter_context(nc.sbuf_tensor(f"qh{i}", [128, T], BF16)) for i in range(2)]
                pt_sb = actx.enter_context(nc.sbuf_tensor("pt_sb", [128, 3 * T], BF16))
                invl_sb = actx.enter_context(nc.sbuf_tensor("invl_sb", [1, T], F32))
                invlb_sb = actx.enter_context(nc.sbuf_tensor("invlb_sb", [128, T], F32))
                lacc = actx.enter_context(nc.sbuf_tensor("lacc", [128, T], BF16))
                ao_sb = [actx.enter_context(nc.sbuf_tensor(f"ao_sb{i}", [128, T], BF16)) for i in range(2)]
                ps_s = [actx.enter_context(nc.psum_tensor(f"att_s{i}", [128, T], F32)) for i in range(3)]
                ps_o = actx.enter_context(nc.psum_tensor("att_o", [128, T], F32))

                tm_hd = {}
                vm_hd = {}
                dxk = {}
                dxv = {}
                dy_ao = {}
                for h in range(H):
                    par = h % 2
                    g = h // c.HPG
                    hh = h % c.HPG
                    tm0 = tm.n
                    vm0 = vm.n
                    am0 = am.n

                    @block.sync
                    def _(e, h=h, par=par, g=g, hh=hh):
                        e = Trk(e)
                        e.wait_ge(cc.h, cc_after_g[g])
                        if h >= 2:
                            e.wait_ge(tm.h, tm_hd[h - 2])
                        src = kg_gth[g].ap().rearrange("(r a p) t -> p r a t", a=c.HPG, p=128)
                        dst = kh[par][:].rearrange("p (r t) -> p r t", t=T)
                        dxk[h] = dx[par].inc(e.dma_start(dst, src[:, :, hh, :]), 16)
                        e.wait_ge(dy.h, dy_qdone)
                        dxk[h] = dx[par].inc(
                            e.dma_start(qh[par][:], q_d.ap()[h * 128:(h + 1) * 128, :]), 16)
                        vsrc = vg_gth[g].ap().rearrange("(r a b p) d -> p r a b d",
                                                        a=c.HPG, b=NT, p=128)
                        vdst = vh[par][:].rearrange("p (r b d) -> p r b d", b=NT, d=128)
                        for r in range(NCORES):
                            dxv[h] = dx[par].inc(
                                e.dma_start(vdst[:, r, :, :], vsrc[:, r, hh, :, :]), 16)

                    @block.tensor
                    def _(e, h=h, par=par, tm0=tm0, vm0=vm0, am0=am0):
                        e = Trk(e)
                        e.wait_ge(dx[par].h, dxk[h])
                        e.wait_ge(am.h, am0)
                        qv = qh[par]
                        for (lo, hi) in HV:
                            e.matmul(ps_s[0][:, lo:hi], kh[par][:, 0:128], qv[:, lo:hi],
                                     start=True, stop=True)
                        tm.inc(e.last)
                        for (lo, hi) in HV:
                            e.matmul(ps_s[1][:, lo:hi], kh[par][:, 128:256], qv[:, lo:hi],
                                     start=True, stop=True)
                        tm.inc(e.last)
                        if h >= 1:
                            e.wait_ge(vm.h, vm_hd[h - 1])
                        e.wait_ge(vm.h, vm0 + 1)
                        tm.inc(e.transpose(ps_o[0:1, 0:128], pm_sb[:, :], idf_sb[:, :]))
                        e.wait_ge(vm.h, vm0 + 2)
                        tm.inc(e.matmul(ps_o[:, 0:1], ones_rf[0:1, :], brow_sb[0:1, 0:1],
                                        start=True, stop=True))
                        for ci in range(NKC + 1):
                            e.wait_ge(am.h, am0 + ci)
                            if ci == 1:
                                e.wait_ge(dx[par].h, dxv[h])
                            if ci >= 1:
                                cm = ci - 1
                                ptv = pt_sb[:, (cm % 3) * T:(cm % 3 + 1) * T]
                                for (lo, hi) in HV:
                                    e.matmul(ps_o[:, lo:hi], vh[par][:, cm * 128:(cm + 1) * 128],
                                             ptv[:, lo:hi], start=(cm == 0), stop=(cm == NKC - 1))
                            if ci + 2 <= NKC - 1:
                                if ci == 0 and h >= 1:
                                    e.wait_ge(vm.h, vm_hd[h - 1] - 1)
                                for (lo, hi) in HV:
                                    e.matmul(ps_s[(ci + 2) % 3][:, lo:hi],
                                             kh[par][:, (ci + 2) * 128:(ci + 3) * 128],
                                             qv[:, lo:hi], start=True, stop=True)
                            tm.inc(e.last)
                        e.wait_ge(vm.h, vm0 + 3 + NKC)
                        for (lo, hi) in HV:
                            e.matmul(ps_s[2][0:1, lo:hi], ones_cb[:, :], lacc[:, lo:hi],
                                     start=True, stop=True)
                        tm.inc(e.last)
                        e.wait_ge(vm.h, vm0 + 4 + NKC)
                        for (lo, hi) in HV:
                            e.matmul(ps_s[2][:, lo:hi], ones_rf[0:1, :], invl_sb[0:1, lo:hi],
                                     start=True, stop=True)
                        tm_hd[h] = tm.inc(e.last)
                        assert tm_hd[h] == tm0 + NKC + 7

                    @block.scalar
                    def _(e, h=h, tm0=tm0, vm0=vm0):
                        e = Trk(e)
                        for cix in range(NKC):
                            if cix < 2:
                                e.wait_ge(tm.h, tm0 + cix + 1)
                            else:
                                e.wait_ge(tm.h, tm0 + cix + 3)
                            if cix == 0:
                                e.wait_ge(vm.h, vm0 + 3)
                            if cix >= 3:
                                e.wait_ge(vm.h, vm0 + cix + 1)
                            ptv = pt_sb[:, (cix % 3) * T:(cix % 3 + 1) * T]
                            am.inc(e.activation(ptv[:], ps_s[cix % 3][:], AF.Exp,
                                                bias=bias_sb[:, 0:1], scale=c.scale))

                    @block.vector
                    def _(e, h=h, par=par, tm0=tm0, vm0=vm0, am0=am0):
                        e = Trk(e)
                        e.wait_ge(tm.h, tm0 + 1)
                        e.reduce_max(pm_sb[:], ps_s[0][:], axis=AX.X)
                        vm.inc(e.last)
                        e.wait_ge(tm.h, tm0 + 3)
                        e.reduce_max(mrow_sb[:], ps_o[0:1, 0:128], axis=AX.X)
                        e.tensor_scalar_mul(brow_sb[:], mrow_sb[:], 0.25)
                        e.tensor_scalar_mul(b2row_sb[:], mrow_sb[:], -0.25)
                        e.tensor_max(brow_sb[:], brow_sb[:], b2row_sb[:])
                        e.tensor_add(brow_sb[:], brow_sb[:], mrow_sb[:])
                        e.tensor_scalar_add(brow_sb[:], brow_sb[:], 10.0)
                        e.tensor_scalar_mul(brow_sb[:], brow_sb[:], -c.scale)
                        vm.inc(e.last)
                        e.wait_ge(tm.h, tm0 + 4)
                        e.tensor_copy(bias_sb[:], ps_o[:, 0:1])
                        vm.inc(e.last)
                        for cix in range(NKC):
                            e.wait_ge(am.h, am0 + cix + 1)
                            ptv = pt_sb[:, (cix % 3) * T:(cix % 3 + 1) * T]
                            if cix == 0:
                                vm.inc(e.tensor_copy(lacc[:], ptv[:]))
                            else:
                                vm.inc(e.tensor_add(lacc[:], lacc[:], ptv[:]))
                        e.wait_ge(tm.h, tm0 + NKC + 6)
                        e.reciprocal(invl_sb[:], ps_s[2][0:1, :])
                        vm.inc(e.last)
                        e.wait_ge(tm.h, tm0 + NKC + 7)
                        e.tensor_copy(invlb_sb[:], ps_s[2][:])
                        vm.inc(e.last)
                        if h >= 2:
                            e.wait_ge(dy.h, dy_ao[h - 2])
                        e.tensor_mul(ao_sb[par][:], ps_o[:], invlb_sb[:])
                        vm_hd[h] = vm.inc(e.last)
                        assert vm_hd[h] == vm0 + NKC + 6

                    @block.sync
                    def _(e, h=h, par=par):
                        e = Trk(e)
                        e.wait_ge(vm.h, vm_hd[h])
                        dy_ao[h] = dy.inc(e.dma_start(ao_d.ap()[h * 128:(h + 1) * 128, :], ao_sb[par][:]), 16)
                dy_after_ao = dy.n

        with ExitStack() as octx:
            aob = octx.enter_context(nc.sbuf_tensor("aob", [128, H * T], BF16))
            p0 = octx.enter_context(nc.psum_tensor("op0", [128, T], F32))
            p1 = octx.enter_context(nc.psum_tensor("op1", [128, T], F32))
            wslab = [octx.enter_context(nc.sbuf_tensor(f"wslo{i}", [128, NC * 128], BF16)) for i in range(2)]
            ps_op = [p0, p1]

            @block.sync
            def _(e):
                e = Trk(e)
                e.wait_ge(dy.h, dy_after_ao)
                for af in range(H):
                    dy.inc(e.dma_start(aob[:, af * T:(af + 1) * T],
                                       ao_d.ap()[af * 128:(af + 1) * 128, :]), 16)
            dy_aob = dy.n

            gw_oc, tm_oc, am_oc, vm_oc, dxo_oc, dyr_oc = {}, {}, {}, {}, {}, {}
            tm_op0 = tm.n
            vm_op0 = vm.n
            for oc in range(NC):
                par = oc % 2

                @block.gpsimd
                def _(e, oc=oc, par=par):
                    e = Trk(e)
                    if oc >= 2:
                        e.wait_ge(tm.h, tm_oc[oc - 2])
                    else:
                        e.wait_ge(tm.h, tm_op0)
                    gw_oc[oc] = gw[par].inc(e.dma_start(wslab[par][:], wo_d.ap()[oc, :, :]), 16)

                @block.sync
                def _(e, oc=oc, par=par):
                    e = Trk(e)
                    if oc >= 2:
                        e.wait_ge(vm.h, vm_oc[oc - 2])
                    else:
                        e.wait_ge(vm.h, vm_op0)
                    dxo_oc[oc] = dx[par].inc(
                        e.dma_start(xinb[par][:], res1_d.ap()[oc * 128:(oc + 1) * 128, :]), 16)

                @block.tensor
                def _(e, oc=oc, par=par):
                    e = Trk(e)
                    e.wait_ge(gw[par].h, gw_oc[oc])
                    if oc == 0:
                        e.wait_ge(dy.h, dy_aob)
                    if oc >= 2:
                        e.wait_ge(am.h, am_oc[oc - 2])
                    for (lo, hi) in HV:
                        for af in range(H):
                            e.matmul(ps_op[par][:, lo:hi],
                                     wslab[par][:, af * 128:(af + 1) * 128],
                                     aob[:, af * T + lo: af * T + hi],
                                     start=(af == 0), stop=(af == H - 1))
                    tm_oc[oc] = tm.inc(e.last)

                @block.scalar
                def _(e, oc=oc, par=par):
                    e = Trk(e)
                    e.wait_ge(tm.h, tm_oc[oc])
                    if oc >= 2:
                        e.wait_ge(vm.h, vm_oc[oc - 2])
                    am_oc[oc] = am.inc(e.activation(
                        qraw[par][:], ps_op[par][:], AF.Identity, bias=bo_sb[:, oc:oc + 1]))

                @block.vector
                def _(e, oc=oc, par=par):
                    e = Trk(e)
                    e.wait_ge(am.h, am_oc[oc])
                    e.wait_ge(dx[par].h, dxo_oc[oc])
                    if oc >= 2:
                        e.wait_ge(dy.h, dyr_oc[oc - 2])
                    e.tensor_add(rout[par][:], qraw[par][:], xinb[par][:])
                    if oc == 0:
                        e.tensor_copy(acc_s[:], rout[par][:])
                        e.tensor_mul(acc_q[:], rout[par][:], rout[par][:])
                    else:
                        e.tensor_add(acc_s[:], acc_s[:], rout[par][:])
                        e.tensor_mul(sqs[:], rout[par][:], rout[par][:])
                        e.tensor_add(acc_q[:], acc_q[:], sqs[:])
                    vm_oc[oc] = vm.inc(e.last)

                @block.sync
                def _(e, oc=oc, par=par):
                    e = Trk(e)
                    e.wait_ge(vm.h, vm_oc[oc])
                    dyr_oc[oc] = dy.inc(e.dma_start(res2_d.ap()[oc * 128:(oc + 1) * 128, :], rout[par][:]), 16)

    if phases == 1:
        ffn_phase(x_d, None, 0)
    else:
        ffn_phase(x_d, res1_d, 0)
    if phases >= 2 and STOP_AFTER is None:
        qkv_attn_phase()
    if phases == 2 and STOP_AFTER is None:
        @block.sync
        def _(e):
            e = Trk(e)
            e.wait_ge(dy.h, dy.n)
            for ic in range(NC):
                dy.inc(e.dma_start(xinb[0][:], res2_d.ap()[ic * 128:(ic + 1) * 128, :]), 16)
                e.wait_ge(dy.h, dy.n)
                dy.inc(e.dma_start(out_d.ap()[ic * 128:(ic + 1) * 128, :], xinb[0][:]), 16)
                e.wait_ge(dy.h, dy.n)
    if phases >= 3 and STOP_AFTER is None:
        ffn_phase(res2_d, None, 1)

    @block.sync
    def _(e):
        e = Trk(e)
        e.wait_ge(dy.h, dy.n)
        for s_ in (dx[0], dx[1]):
            e.wait_ge(s_.h, s_.n)

    ctx.close()
    return nc


def _block_oc(w):
    Din, Dout = w.shape
    return np.ascontiguousarray(
        w.reshape(Din // 128, 128, Dout // 128, 128).transpose(2, 1, 0, 3)
         .reshape(Dout // 128, 128, Din))


def _block_og(w, og):
    Din, Dout = w.shape
    ow = Dout // og
    return np.ascontiguousarray(
        w.reshape(Din // 128, 128, og, ow).transpose(2, 1, 0, 3)
         .reshape(og, 128, (Din // 128) * ow))


def prep_inputs(cfg, inp):
    c = cfg
    f = np.float32
    wq = np.asarray(inp["L0_attn_Wq"], f)
    wk = np.asarray(inp["L0_attn_Wk"], f)
    wv = np.asarray(inp["L0_attn_Wv"], f)
    wo = np.asarray(inp["L0_attn_Wo"], f)
    wff = np.asarray(inp["L0_ffn_weight"], f).T
    la = np.asarray(inp["L0_ffn_lora_A"], f)
    lb = np.asarray(inp["L0_ffn_lora_B"], f)
    gw1 = np.asarray(inp["L0_ffn_gw1"], f)
    gw2 = np.asarray(inp["L0_ffn_gw2"], f)
    a_cat = la.transpose(1, 0, 2).reshape(c.D, c.RC)
    b_cat = lb.reshape(c.RC, c.D)
    mp = np.zeros((c.E, 128), f)
    for e in range(c.E):
        mp[e, e * c.R:(e + 1) * c.R] = 1.0 / c.R
    d = {
        "wq_b": _block_oc(wq), "wk_b": _block_oc(wk), "wo_b": _block_oc(wo),
        "wff_b": _block_oc(wff), "wv_b": _block_og(wv, c.OG),
        "a_b": np.ascontiguousarray(
            a_cat.reshape(c.NC, 128, c.RC).transpose(1, 0, 2).reshape(128, c.NC * c.RC)),
        "b_cat": np.ascontiguousarray(b_cat.reshape(128, c.D)) if c.RC == 128 else None,
        "gw1_b": np.ascontiguousarray(
            gw1.T.reshape(c.NC, 128, c.GD).transpose(1, 0, 2).reshape(128, c.NC * c.GD)),
        "gw2_t": np.ascontiguousarray(gw2.T),
        "bq_c": np.ascontiguousarray(np.asarray(inp["L0_attn_bq"], f).reshape(c.NC, 128).T),
        "bk_c": np.ascontiguousarray(np.asarray(inp["L0_attn_bk"], f).reshape(c.NC, 128).T),
        "bo_c": np.ascontiguousarray(np.asarray(inp["L0_attn_bo"], f).reshape(c.NC, 128).T),
        "bv_r": np.asarray(inp["L0_attn_bv"], f).reshape(1, c.D).astype(ml_dtypes.bfloat16),
        "gb1_c": np.asarray(inp["L0_ffn_gb1"], f).reshape(c.GD, 1),
        "gb2_r": np.asarray(inp["L0_ffn_gb2"], f).reshape(1, c.E),
        "lng_c": np.asarray(inp["L0_ffn_ln_gamma"], f).reshape(c.GD, 1),
        "lnb_c": np.asarray(inp["L0_ffn_ln_beta"], f).reshape(c.GD, 1),
        "gq_c": np.asarray(inp["L0_attn_gq"], f).reshape(128, 1),
        "gk_c": np.asarray(inp["L0_attn_gk"], f).reshape(128, 1),
        "map_alpha": mp,
        "ident_f32": np.eye(128, dtype=f),
    }
    assert d["b_cat"] is not None, "RC must be 128"
    return d


CFG_FULL = Cfg()
_CACHE = {}


def kernel(**inputs):
    cfg = CFG_FULL
    x = np.asarray(inputs["L0_attn_hidden_states"], np.float32)[0]
    shared = prep_inputs(cfg, inputs)
    in_maps = []
    for cix in range(NCORES):
        m = dict(shared)
        m["x_fm"] = np.ascontiguousarray(x[cix * cfg.T:(cix + 1) * cfg.T, :].T)
        in_maps.append(m)
    key = "full"
    if key not in _CACHE:
        _CACHE[key] = build(cfg)
    nc = _CACHE[key]
    res = run_bass_kernel_spmd(nc, in_maps, core_ids=list(range(NCORES)))
    out = np.empty((1, NCORES * cfg.T, cfg.D), np.float32)
    for cix in range(NCORES):
        out[0, cix * cfg.T:(cix + 1) * cfg.T, :] = res.results[cix]["out_fm"].T
    return out
